# version 02
# baseline (speedup 1.0000x reference)
import numpy as np
from contextlib import ExitStack

import concourse.bass as bass
import concourse.mybir as mybir
from concourse.bass_utils import run_bass_kernel_spmd

F32 = mybir.dt.float32
BF16 = mybir.dt.bfloat16
AF = mybir.ActivationFunctionType
ALU = mybir.AluOpType

B, T, C, H, D = 4, 4096, 2048, 16, 128
HG = 8
CG = HG * D
NT, TS = 8, 512
NCH = C // 128
OWN = 2048
NOT_ = 4
SCALE = float(1.0 / np.sqrt(D))
EPS = 1e-5
NEG = -1e30

ENGS = ("tensor", "scalar", "vector", "gpsimd", "sync")
PAIRS = [[0, 1], [2, 3], [4, 5], [6, 7]]


def _attn_plan():
    nb = T // 64
    bp = np.arange(nb)
    qb, kb = bp[:, None], bp[None, :]
    dist = qb - kb
    bm = (((dist >= 0) & (dist <= 16)) | (kb < 2) | (kb % 4 == 0)) & (kb <= qb)
    plan = []
    for c in range(8):
        rows = bm[c * 8:(c + 1) * 8]
        groups = []
        for j in range(nb // 2):
            a = np.where(rows[:, 2 * j])[0]
            b = np.where(rows[:, 2 * j + 1])[0]
            if len(a) == 0 and len(b) == 0:
                continue
            if len(a) and len(b):
                r0 = int(min(a[0], b[0])); r1 = int(max(a[-1], b[-1]))
                segs = []
                cur = None
                for r in range(r0, r1 + 1):
                    key = (r in a, r in b)
                    if cur is None or cur[2] != key:
                        if cur is not None:
                            segs.append(tuple(cur))
                        cur = [r, r, key]
                    else:
                        cur[1] = r
                segs.append(tuple(cur))
                segs = [(s0, s1, 0 if k == (True, True) else (1 if k == (True, False) else 2))
                        for (s0, s1, k) in segs]
                groups.append(dict(kind="pair", j=j, r0=r0, r1=r1, segs=segs))
            else:
                side = 0 if len(a) else 1
                rr = a if len(a) else b
                groups.append(dict(kind="single", j=j, side=side,
                                   r0=int(rr[0]), r1=int(rr[-1])))
        plan.append(groups)
    return plan


ATTN_PLAN = _attn_plan()


class Sem:
    __slots__ = ("h", "n", "name")

    def __init__(self, handle, name):
        self.h = handle
        self.n = 0
        self.name = name


def build_nc(debug=False):
    nc = bass.Bass("TRN2", target_bir_lowering=False, debug=False)

    def din(name, shape, dt):
        return nc.dram_tensor(name, shape, dt, kind="ExternalInput").ap()

    def dout(name, shape, dt):
        return nc.dram_tensor(name, shape, dt, kind="ExternalOutput").ap()

    def dint(name, shape, dt):
        if debug:
            return dout(name, shape, dt)
        return nc.dram_tensor(name, shape, dt).ap()

    xT = din("xT", [C, T], F32)
    xoT = din("xoT", [C, OWN], F32)
    wqT = din("wqT", [C, CG], F32)
    wkT = din("wkT", [C, CG], F32)
    wvT = din("wvT", [C, CG], F32)
    woT = din("woT", [C, C], F32)
    WT = din("WT", [C, C], F32)
    brow = din("brow", [1, C], F32)
    AstkT = din("AstkT", [C, 128], F32)
    BstkT = din("BstkT", [128, C], F32)
    gwT = din("gwT", [C, 8], F32)
    gbrow = din("gbrow", [1, 8], F32)
    cosT = din("cosT", [128, T], F32)
    sinT = din("sinT", [128, T], F32)
    c_ones = din("c_ones", [128, 1], F32)
    c_masks = din("c_masks", [128, 3], F32)
    c_E8 = din("c_E8", [8, 128], F32)
    c_id = din("c_id", [128, 128], F32)
    c_ones512 = din("c_ones512", [1, 512], F32)
    c_eps = din("c_eps", [1, 1], F32)

    outT = dout("outT", [C, OWN], F32)

    qbuf = dint("qbuf", [CG, T], BF16)
    kbuf = dint("kbuf", [CG, T], BF16)
    vbuf = dint("vbuf", [128, 32 * CG], BF16)
    ctxin = nc.dram_tensor("ctxin", [8 * CG, TS], BF16).ap()
    ctxout = nc.dram_tensor("ctxout", [16 * CG, TS], BF16).ap()
    ctxout_dbg = dout("ctxout", [16 * CG, TS], BF16) if False else None
    if debug:
        ctxout_dbg = dout("ctxout_dbg", [16 * CG, TS], BF16)
        d_xh2 = dout("d_xh2", [128, NCH * TS], BF16)
        d_w8 = dout("d_w8", [8, TS], F32)
        d_xab = dout("d_xab", [128, TS], BF16)
        d_y4 = dout("d_y4", [128, NCH * TS], F32)
        d_rows = dout("d_rows", [1, 4 * TS], F32)
        d_gate = dout("d_gate", [128, 64], F32)
    x2buf = dint("x2buf", [128, NCH * OWN], F32)
    agw_i = nc.dram_tensor("agw_i", [8, 512], BF16).ap()
    agw_o = nc.dram_tensor("agw_o", [16, 512], BF16).ap()

    ops = {e: [] for e in ENGS}

    def on(eng, fn):
        ops[eng].append(fn)

    ctx_stack = ExitStack()
    E = ctx_stack.enter_context

    sems = {}
    for nm in ("ste", "ssc", "sve", "scc", "sbar",
               "sx0", "sx1", "sldg", "slds", "sst", "sa0", "sa1", "sb0", "sb1"):
        sems[nm] = Sem(E(nc.semaphore(nm)), nm)
    ste, ssc, sve = sems["ste"], sems["ssc"], sems["sve"]
    scc, sbar = sems["scc"], sems["sbar"]
    sx = [sems["sx0"], sems["sx1"]]
    sldg, slds, sst = sems["sldg"], sems["slds"], sems["sst"]
    sa = [sems["sa0"], sems["sa1"]]
    sb = [sems["sb0"], sems["sb1"]]

    def W(eng, sem, val):
        assert val <= sem.n, f"wait {sem.name}>={val} but only {sem.n} planned"
        on(eng, lambda e, s=sem.h, v=val: e.wait_ge(s, v))

    nbar = [0]

    def barrier():
        nbar[0] += 1
        tgt = 5 * nbar[0]
        for eng in ENGS:
            on(eng, lambda e: e.nop().then_inc(sbar.h, 1))
        sbar.n += 5
        for eng in ENGS:
            W(eng, sbar, tgt)

    ONEB = E(nc.sbuf_tensor("ONEB", [128, 1], BF16))
    ONEF = E(nc.sbuf_tensor("ONEF", [128, 1], F32))
    ONERF = E(nc.sbuf_tensor("ONERF", [1, 128], F32))
    ONE1F = E(nc.sbuf_tensor("ONE1F", [1, 1], F32))
    OR512B = E(nc.sbuf_tensor("OR512B", [1, 512], BF16))
    MASKC = E(nc.sbuf_tensor("MASKC", [128, 3], F32))
    E8F = E(nc.sbuf_tensor("E8F", [8, 128], F32))
    ID128 = E(nc.sbuf_tensor("ID128", [128, 128], F32))
    GBB = E(nc.sbuf_tensor("GBB", [128, 8], F32))
    GBR = E(nc.sbuf_tensor("GBR", [1, 8], F32))
    RSTD = E(nc.sbuf_tensor("RSTD", [1, T], F32))
    RSTDC = E(nc.sbuf_tensor("RSTDC", [128, 32], F32))
    ROWA = E(nc.sbuf_tensor("ROWA", [1, 512], F32))
    ROWB = E(nc.sbuf_tensor("ROWB", [1, 512], F32))
    ROWC = E(nc.sbuf_tensor("ROWC", [1, 512], F32))
    EPSC = E(nc.sbuf_tensor("EPSC", [1, 1], F32))

    def ld0(sy):
        sy.dma_start(out=ONEF[:, :], in_=c_ones).then_inc(slds.h, 16)
        sy.dma_start(out=ONERF[:, :], in_=c_ones.rearrange("p o -> o p")).then_inc(slds.h, 16)
        sy.dma_start(out=ONE1F[:, :], in_=c_ones[0:1, :]).then_inc(slds.h, 16)
        sy.dma_start(out=MASKC[:, :], in_=c_masks).then_inc(slds.h, 16)
        sy.dma_start(out=E8F[:, :], in_=c_E8).then_inc(slds.h, 16)
        sy.dma_start(out=ID128[:, :], in_=c_id).then_inc(slds.h, 16)
        sy.dma_start(out=GBR[:, :], in_=gbrow).then_inc(slds.h, 16)
        sy.dma_start(out=EPSC[:, :], in_=c_eps).then_inc(slds.h, 16)
    on("sync", ld0)
    slds.n += 8 * 16

    def ld0g(g):
        g.dma_start(out=ONEB[:, :], in_=c_ones).then_inc(sldg.h, 16)
        g.dma_start(out=OR512B[:, :], in_=c_ones512).then_inc(sldg.h, 16)
        g.collective_compute("AllGather", ALU.bypass, replica_groups=PAIRS,
                             ins=[agw_i[:, :]], outs=[agw_o[:, :]]).then_inc(scc.h, 1)
    on("gpsimd", ld0g)
    sldg.n += 2 * 16
    scc.n += 1

    S0p = ExitStack()
    PB0 = S0p.enter_context(nc.psum_tensor("PB0", [128, 8], F32))
    W("tensor", slds, slds.n)
    on("tensor", lambda e: e.matmul(PB0[:, :], ONERF[:, :], GBR[:, :],
                                    start=True, stop=True).then_inc(ste.h, 1))
    ste.n += 1
    W("vector", ste, ste.n)
    on("vector", lambda e: e.tensor_copy(out=GBB[:, :], in_=PB0[:, :]).then_inc(sve.h, 1))
    sve.n += 1
    S0p.close()

    with ExitStack() as S1:
        e1 = S1.enter_context
        WQ = e1(nc.sbuf_tensor("WQ", [128, NCH * CG], BF16))
        WK = e1(nc.sbuf_tensor("WK", [128, NCH * CG], BF16))
        WV = e1(nc.sbuf_tensor("WV", [128, NCH * CG], BF16))
        XBF = [e1(nc.sbuf_tensor(f"XBF{i}", [128, NCH * TS], BF16)) for i in range(2)]
        XSQ = e1(nc.sbuf_tensor("XSQ", [128, NCH * TS], BF16))
        QST = e1(nc.sbuf_tensor("QST", [128, HG * TS], BF16))
        KST = e1(nc.sbuf_tensor("KST", [128, HG * TS], BF16))
        VST = e1(nc.sbuf_tensor("VST", [128, 4 * CG], BF16))
        NEGM = [e1(nc.sbuf_tensor(f"NEGM{i}", [1, TS], BF16)) for i in range(2)]
        SQR = e1(nc.sbuf_tensor("SQR", [1, CG], BF16))
        SKR = e1(nc.sbuf_tensor("SKR", [1, CG], BF16))
        SVR = e1(nc.sbuf_tensor("SVR", [1, CG], BF16))
        Qp = [e1(nc.psum_tensor(f"QpA{i}", [128, 512], F32)) for i in range(2)]
        Kp = [e1(nc.psum_tensor(f"KpA{i}", [128, 512], F32)) for i in range(2)]
        Vp = e1(nc.psum_tensor("VpA", [128, 512], F32))
        Sump = e1(nc.psum_tensor("SumpA", [1, 512], F32))
        Sqp = e1(nc.psum_tensor("SqpA", [1, 512], F32))
        Colp = e1(nc.psum_tensor("ColpA", [128, 4], F32))

        def wload(g):
            g.dma_start(out=WQ[:, :].rearrange("d (k o) -> d k o", k=NCH),
                        in_=wqT.rearrange("(k d) o -> d k o", d=128)).then_inc(sldg.h, 16)
            g.dma_start(out=WK[:, :].rearrange("d (k o) -> d k o", k=NCH),
                        in_=wkT.rearrange("(k d) o -> d k o", d=128)).then_inc(sldg.h, 16)
            g.dma_start(out=WV[:, :].rearrange("d (k o) -> d k o", k=NCH),
                        in_=wvT.rearrange("(k d) o -> d k o", d=128)).then_inc(sldg.h, 16)
        on("gpsimd", wload)
        sldg.n += 3 * 16

        W("tensor", sldg, sldg.n)
        for (Wt, Srow) in ((WQ, SQR), (WK, SKR), (WV, SVR)):
            for sl in range(2):
                def smm(e, Wt=Wt, sl=sl):
                    for dc in range(NCH):
                        ins = e.matmul(Sump[:, :], ONEB[:, :],
                                       Wt[:, dc * CG + sl * 512: dc * CG + (sl + 1) * 512],
                                       start=(dc == 0), stop=(dc == NCH - 1))
                    ins.then_inc(ste.h, 1)
                on("tensor", smm)
                ste.n += 1
                W("vector", ste, ste.n)
                on("vector", lambda e, Srow=Srow, sl=sl: e.tensor_copy(
                    out=Srow[:, sl * 512:(sl + 1) * 512], in_=Sump[:, :]).then_inc(sve.h, 1))
                sve.n += 1
                W("tensor", sve, sve.n)

        xload = [None] * NT
        rows_sc = [None] * NT
        vrows_ve = [None] * NT
        stats_te = [None] * NT
        qkv_te_done = [None] * NT
        store_id = [None] * NT
        evacQ, evacK, evacV = {}, {}, {}

        def issue_xload(j):
            def f(g, j=j):
                g.dma_start(out=XBF[j % 2][:, :].rearrange("d (k n) -> d k n", k=NCH),
                            in_=xT.rearrange("(k d) n -> d k n", d=128)[:, :, j * TS:(j + 1) * TS]
                            ).then_inc(sx[j % 2].h, 16)
            on("gpsimd", f)
            sx[j % 2].n += 16
            xload[j] = sx[j % 2].n

        issue_xload(0)
        issue_xload(1)

        def emit_stats(j):
            W("scalar", sx[j % 2], xload[j])
            if j > 0:
                W("scalar", ste, stats_te[j - 1])
            on("scalar", lambda e, j=j: e.activation(
                out=XSQ[:, :], in_=XBF[j % 2][:, :], func=AF.Square).then_inc(ssc.h, 1))
            ssc.n += 1
            sq_done = ssc.n
            W("tensor", sx[j % 2], xload[j])
            if j > 0:
                W("tensor", sve, vrows_ve[j - 1])
            def summ(e, j=j):
                for dc in range(NCH):
                    ins = e.matmul(Sump[:, :], ONEB[:, :], XBF[j % 2][:, dc * TS:(dc + 1) * TS],
                                   start=(dc == 0), stop=(dc == NCH - 1))
                ins.then_inc(ste.h, 1)
            on("tensor", summ)
            ste.n += 1
            W("tensor", ssc, sq_done)
            def sqm(e):
                for dc in range(NCH):
                    ins = e.matmul(Sqp[:, :], ONEB[:, :], XSQ[:, dc * TS:(dc + 1) * TS],
                                   start=(dc == 0), stop=(dc == NCH - 1))
                ins.then_inc(ste.h, 1)
            on("tensor", sqm)
            ste.n += 1
            stats_te[j] = ste.n
            W("vector", ste, stats_te[j])
            if j > 0:
                W("vector", ssc, rows_sc[j - 1])
            def vrows(e, j=j):
                e.tensor_scalar(out=ROWC[:, :], in0=Sump[:, :],
                                scalar1=1.0 / C, scalar2=None, op0=ALU.mult)
                e.tensor_scalar(out=ROWA[:, :], in0=Sqp[:, :],
                                scalar1=1.0 / C, scalar2=None, op0=ALU.mult)
                e.tensor_tensor(out=ROWB[:, :], in0=ROWC[:, :], in1=ROWC[:, :], op=ALU.mult)
                e.tensor_tensor(out=ROWA[:, :], in0=ROWA[:, :], in1=ROWB[:, :], op=ALU.subtract)
                e.tensor_scalar(out=NEGM[j % 2][:, :], in0=ROWC[:, :],
                                scalar1=-1.0, scalar2=None, op0=ALU.mult).then_inc(sve.h, 1)
            on("vector", vrows)
            sve.n += 1
            vrows_ve[j] = sve.n
            W("scalar", sve, sve.n)
            def srows(e, j=j):
                e.activation(out=ROWB[:, :], in_=ROWA[:, :], func=AF.Ln, bias=EPSC[:, 0:1])
                e.activation(out=RSTD[:, j * TS:(j + 1) * TS], in_=ROWB[:, :],
                             func=AF.Exp, scale=-0.5).then_inc(ssc.h, 1)
            on("scalar", srows)
            ssc.n += 1
            rows_sc[j] = ssc.n

        def emit_qkv(j):
            W("tensor", ssc, rows_sc[j])
            def colmm(e, j=j):
                for tc in range(4):
                    ins = e.matmul(Colp[:, tc:tc + 1],
                                   RSTD[:, j * TS + tc * 128: j * TS + (tc + 1) * 128],
                                   ONE1F[:, :], start=True, stop=True)
                ins.then_inc(ste.h, 1)
            on("tensor", colmm)
            ste.n += 1
            W("vector", ste, ste.n)
            on("vector", lambda e, j=j: e.tensor_copy(
                out=RSTDC[:, j * 4:(j + 1) * 4], in_=Colp[:, :]).then_inc(sve.h, 1))
            sve.n += 1
            col_ve = sve.n
            if j > 0:
                W("scalar", sst, store_id[j - 1])
            for (Wt, Pp, DST, Srow, ed) in ((WQ, Qp, QST, SQR, evacQ), (WK, Kp, KST, SKR, evacK)):
                for oc in range(HG):
                    if (j, oc - 2) in ed:
                        W("tensor", ssc, ed[(j, oc - 2)])
                    def qmm(e, j=j, oc=oc, Wt=Wt, Pp=Pp, Srow=Srow):
                        bank = Pp[oc % 2]
                        for dc in range(NCH):
                            e.matmul(bank[:, :],
                                     Wt[:, dc * CG + oc * 128: dc * CG + (oc + 1) * 128],
                                     XBF[j % 2][:, dc * TS:(dc + 1) * TS],
                                     start=(dc == 0), stop=False)
                        e.matmul(bank[:, :], Srow[:, oc * 128:(oc + 1) * 128],
                                 NEGM[j % 2][:, :],
                                 start=False, stop=True).then_inc(ste.h, 1)
                    on("tensor", qmm)
                    ste.n += 1
                    W("scalar", ste, ste.n)
                    on("scalar", lambda e, oc=oc, Pp=Pp, DST=DST: e.copy(
                        out=DST[:, oc * TS:(oc + 1) * TS],
                        in_=Pp[oc % 2][:, :]).then_inc(ssc.h, 1))
                    ssc.n += 1
                    ed[(j, oc)] = ssc.n
            for tc in range(4):
                for os_ in range(2):
                    if (j, tc, os_) != (j, 0, 0) and (j, (tc, os_)) not in ():
                        pass
                    key_prev = (tc, os_ - 1) if os_ > 0 else (tc - 1, 1)
                    if key_prev[0] >= 0 and (j,) + key_prev in evacV:
                        W("tensor", ssc, evacV[(j,) + key_prev])
                    def vmm(e, j=j, tc=tc, os_=os_):
                        for dc in range(NCH):
                            e.matmul(Vp[:, :],
                                     XBF[j % 2][:, dc * TS + tc * 128: dc * TS + (tc + 1) * 128],
                                     WV[:, dc * CG + os_ * 512: dc * CG + (os_ + 1) * 512],
                                     start=(dc == 0), stop=False)
                        e.matmul(Vp[:, :],
                                 NEGM[j % 2][:, tc * 128:(tc + 1) * 128],
                                 SVR[:, os_ * 512:(os_ + 1) * 512],
                                 start=False, stop=True).then_inc(ste.h, 1)
                    on("tensor", vmm)
                    ste.n += 1
                    W("scalar", ste, ste.n)
                    W("scalar", sve, col_ve)
                    on("scalar", lambda e, j=j, tc=tc, os_=os_: e.activation(
                        out=VST[:, tc * CG + os_ * 512: tc * CG + (os_ + 1) * 512],
                        in_=Vp[:, :], func=AF.Copy,
                        scale=RSTDC[:, j * 4 + tc: j * 4 + tc + 1]).then_inc(ssc.h, 1))
                    ssc.n += 1
                    evacV[(j, tc, os_)] = ssc.n
            qkv_te_done[j] = ste.n
            W("sync", ssc, ssc.n)
            def store(sy, j=j):
                sy.dma_start(out=qbuf.rearrange("(h d) n -> d h n", d=128)[:, :, j * TS:(j + 1) * TS],
                             in_=QST[:, :].rearrange("d (h n) -> d h n", h=HG)).then_inc(sst.h, 16)
                sy.dma_start(out=kbuf.rearrange("(h d) n -> d h n", d=128)[:, :, j * TS:(j + 1) * TS],
                             in_=KST[:, :].rearrange("d (h n) -> d h n", h=HG)).then_inc(sst.h, 16)
                sy.dma_start(out=vbuf[:, j * 4 * CG:(j + 1) * 4 * CG],
                             in_=VST[:, :]).then_inc(sst.h, 16)
            on("sync", store)
            sst.n += 3 * 16
            store_id[j] = sst.n
            if j + 2 < NT:
                W("gpsimd", ste, qkv_te_done[j])
                issue_xload(j + 2)

        for j in range(NT):
            emit_stats(j)
            if j > 0:
                emit_qkv(j - 1)
        emit_qkv(NT - 1)
        W("sync", sst, sst.n)
    barrier()

    with ExitStack() as S2:
        e2 = S2.enter_context
        KH = e2(nc.sbuf_tensor("KH", [128, HG * T], BF16))
        VS = e2(nc.sbuf_tensor("VS", [128, 32 * CG], BF16))
        CB = e2(nc.sbuf_tensor("CB", [128, 512], F32))
        SB = e2(nc.sbuf_tensor("SB", [128, 512], F32))
        CS = e2(nc.sbuf_tensor("CS", [128, 512], F32))
        SS = e2(nc.sbuf_tensor("SS", [128, 512], F32))
        T1 = e2(nc.sbuf_tensor("T1", [128, 512], F32))
        T2 = e2(nc.sbuf_tensor("T2", [128, 512], F32))
        Rbp = e2(nc.psum_tensor("RbpB", [128, 512], F32))
        S2k = ExitStack()
        KT = S2k.enter_context(nc.sbuf_tensor("KT", [128, HG * TS], BF16))
        KW = S2k.enter_context(nc.sbuf_tensor("KW", [128, HG * TS], BF16))

        on("sync", lambda sy: sy.dma_start(out=VS[:, :], in_=vbuf[:, :]).then_inc(slds.h, 16))
        slds.n += 16
        vs_ld = slds.n

        for j in range(NT):
            cl = j * TS
            def ldk(sy, cl=cl):
                sy.dma_start(out=CS[:, :], in_=cosT[:, cl:cl + TS]).then_inc(slds.h, 16)
                sy.dma_start(out=SS[:, :], in_=sinT[:, cl:cl + TS]).then_inc(slds.h, 16)
                sy.dma_start(out=KT[:, :].rearrange("d (h n) -> d h n", h=HG),
                             in_=kbuf.rearrange("(h d) n -> d h n", d=128)[:, :, cl:cl + TS]
                             ).then_inc(slds.h, 16)
                kb4 = kbuf.rearrange("(h s dd) n -> s dd h n", h=HG, s=2)
                sy.dma_start(out=KW[0:64, :].rearrange("d (h n) -> d h n", h=HG),
                             in_=kb4[1, :, :, cl:cl + TS]).then_inc(slds.h, 16)
                sy.dma_start(out=KW[64:128, :].rearrange("d (h n) -> d h n", h=HG),
                             in_=kb4[0, :, :, cl:cl + TS]).then_inc(slds.h, 16)
            on("sync", ldk)
            slds.n += 5 * 16
            W("tensor", slds, slds.n)
            on("tensor", lambda e, j=j: e.matmul(
                Rbp[:, :], ONERF[:, :], RSTD[:, j * TS:(j + 1) * TS],
                start=True, stop=True).then_inc(ste.h, 1))
            ste.n += 1
            W("vector", ste, ste.n)
            def vrope(e, cl=cl):
                e.tensor_tensor(out=CB[:, :], in0=CS[:, :], in1=Rbp[:, :], op=ALU.mult)
                e.tensor_tensor(out=SB[:, :], in0=SS[:, :], in1=Rbp[:, :], op=ALU.mult)
                for h in range(HG):
                    e.tensor_tensor(out=T1[:, :], in0=KT[:, h * TS:(h + 1) * TS],
                                    in1=CB[:, :], op=ALU.mult)
                    e.tensor_tensor(out=T2[:, :], in0=KW[:, h * TS:(h + 1) * TS],
                                    in1=SB[:, :], op=ALU.mult)
                    ins = e.tensor_tensor(out=KH[:, h * T + cl: h * T + cl + TS],
                                          in0=T1[:, :], in1=T2[:, :], op=ALU.add)
                ins.then_inc(sve.h, 1)
            on("vector", vrope)
            sve.n += 1
            W("sync", sve, sve.n)
            W("tensor", sve, sve.n)

        S2k.close()
        QT = e2(nc.sbuf_tensor("QT", [128, HG * TS], BF16))
        QW = e2(nc.sbuf_tensor("QW", [128, HG * TS], BF16))
        QH = e2(nc.sbuf_tensor("QH", [128, HG * TS], BF16))
        PT = [e2(nc.sbuf_tensor(f"PT{i}", [128, 512], BF16)) for i in range(3)]
        IDR = e2(nc.sbuf_tensor("IDR", [1, 512], F32))
        RBS = e2(nc.sbuf_tensor("RBS", [128, 512], F32))
        CTXS = e2(nc.sbuf_tensor("CTXS", [128, 512], BF16))
        Scp = [e2(nc.psum_tensor(f"ScpC{i}", [128, 512], F32)) for i in range(2)]
        Ctp = e2(nc.psum_tensor("CtpC", [128, 512], F32))
        Dnp = e2(nc.psum_tensor("DnpC", [1, 512], F32))
        Rb2 = e2(nc.psum_tensor("Rb2C", [128, 512], F32))

        W("tensor", slds, vs_ld)

        exp_hist = []
        cd_hist = []
        for cch in range(8):
            cl = cch * TS
            def ldq(sy, cl=cl):
                sy.dma_start(out=CS[:, :], in_=cosT[:, cl:cl + TS]).then_inc(sa[0].h, 16)
                sy.dma_start(out=SS[:, :], in_=sinT[:, cl:cl + TS]).then_inc(sa[0].h, 16)
                sy.dma_start(out=QT[:, :].rearrange("d (h n) -> d h n", h=HG),
                             in_=qbuf.rearrange("(h d) n -> d h n", d=128)[:, :, cl:cl + TS]
                             ).then_inc(sa[0].h, 16)
                qb4 = qbuf.rearrange("(h s dd) n -> s dd h n", h=HG, s=2)
                sy.dma_start(out=QW[0:64, :].rearrange("d (h n) -> d h n", h=HG),
                             in_=qb4[1, :, :, cl:cl + TS]).then_inc(sa[0].h, 16)
                sy.dma_start(out=QW[64:128, :].rearrange("d (h n) -> d h n", h=HG),
                             in_=qb4[0, :, :, cl:cl + TS]).then_inc(sa[0].h, 16)
            on("sync", ldq)
            sa[0].n += 5 * 16
            on("tensor", lambda e, cl=cl: e.matmul(
                Rbp[:, :], ONERF[:, :], RSTD[:, cl:cl + TS],
                start=True, stop=True).then_inc(ste.h, 1))
            ste.n += 1
            rbq_te = ste.n
            W("vector", sa[0], sa[0].n)
            W("vector", ste, rbq_te)
            if cch > 0:
                W("vector", ste, cd_hist[-1])
            def qrope(e, cl=cl):
                e.tensor_tensor(out=CB[:, :], in0=CS[:, :], in1=Rbp[:, :], op=ALU.mult)
                e.tensor_tensor(out=SB[:, :], in0=SS[:, :], in1=Rbp[:, :], op=ALU.mult)
                for h in range(HG):
                    e.tensor_tensor(out=T1[:, :], in0=QT[:, h * TS:(h + 1) * TS],
                                    in1=CB[:, :], op=ALU.mult)
                    e.tensor_tensor(out=T2[:, :], in0=QW[:, h * TS:(h + 1) * TS],
                                    in1=SB[:, :], op=ALU.mult)
                    ins = e.tensor_tensor(out=QH[:, h * TS:(h + 1) * TS],
                                          in0=T1[:, :], in1=T2[:, :], op=ALU.add)
                ins.then_inc(sve.h, 1)
            on("vector", qrope)
            sve.n += 1
            W("tensor", sve, sve.n)

            groups = ATTN_PLAN[cch]
            ng = len(groups)
            for h in range(HG):
                for gi, g in enumerate(groups):
                    bank = Scp[len(cd_hist) % 2]
                    q0, q1 = g["r0"] * 64, (g["r1"] + 1) * 64
                    if len(exp_hist) >= 2:
                        W("tensor", ssc, exp_hist[-2])
                    if g["kind"] == "pair":
                        def smm(e, h=h, j2=g["j"], q0=q0, q1=q1, bank=bank):
                            e.matmul(bank[:, q0:q1],
                                     KH[:, h * T + j2 * 128: h * T + (j2 + 1) * 128],
                                     QH[:, h * TS + q0: h * TS + q1],
                                     start=True, stop=True).then_inc(ste.h, 1)
                    else:
                        def smm(e, h=h, j2=g["j"], sd=g["side"], q0=q0, q1=q1, bank=bank):
                            e.matmul(bank[sd * 64:sd * 64 + 64, q0:q1],
                                     KH[:, h * T + j2 * 128 + sd * 64: h * T + j2 * 128 + sd * 64 + 64],
                                     QH[:, h * TS + q0: h * TS + q1],
                                     start=True, stop=True).then_inc(ste.h, 1)
                    on("tensor", smm)
                    ste.n += 1
                    W("scalar", ste, ste.n)
                    if len(cd_hist) >= 3:
                        W("scalar", ste, cd_hist[-3])
                    pt = PT[len(exp_hist) % 3]
                    if g["kind"] == "pair":
                        def sexp(e, g=g, bank=bank, pt=pt):
                            for (s0, s1, bi) in g["segs"]:
                                a0, a1 = s0 * 64, (s1 + 1) * 64
                                if bi == 0:
                                    ins = e.activation(out=pt[:, a0:a1], in_=bank[:, a0:a1],
                                                       func=AF.Exp, scale=SCALE)
                                else:
                                    ins = e.activation(out=pt[:, a0:a1], in_=bank[:, a0:a1],
                                                       func=AF.Exp, scale=SCALE,
                                                       bias=MASKC[:, bi:bi + 1])
                            ins.then_inc(ssc.h, 1)
                    else:
                        def sexp(e, g=g, bank=bank, pt=pt, q0=q0, q1=q1):
                            sd = g["side"]
                            e.activation(out=pt[sd * 64:sd * 64 + 64, q0:q1],
                                         in_=bank[sd * 64:sd * 64 + 64, q0:q1],
                                         func=AF.Exp, scale=SCALE).then_inc(ssc.h, 1)
                    on("scalar", sexp)
                    ssc.n += 1
                    exp_hist.append(ssc.n)
                    W("tensor", ssc, ssc.n)
                    first, last = (gi == 0), (gi == ng - 1)
                    if g["kind"] == "pair":
                        def cmm(e, j2=g["j"], h=h, q0=q0, q1=q1, pt=pt, first=first, last=last):
                            e.matmul(Ctp[:, q0:q1],
                                     VS[:, j2 * CG + h * 128: j2 * CG + (h + 1) * 128],
                                     pt[:, q0:q1], start=first, stop=last)
                            e.matmul(Dnp[:, q0:q1], ONEB[:, :], pt[:, q0:q1],
                                     start=first, stop=last).then_inc(ste.h, 1)
                    else:
                        def cmm(e, j2=g["j"], sd=g["side"], h=h, q0=q0, q1=q1, pt=pt,
                                first=first, last=last):
                            e.matmul(Ctp[:, q0:q1],
                                     VS[sd * 64:sd * 64 + 64, j2 * CG + h * 128: j2 * CG + (h + 1) * 128],
                                     pt[sd * 64:sd * 64 + 64, q0:q1], start=first, stop=last)
                            e.matmul(Dnp[:, q0:q1], ONEB[sd * 64:sd * 64 + 64, :],
                                     pt[sd * 64:sd * 64 + 64, q0:q1],
                                     start=first, stop=last).then_inc(ste.h, 1)
                    on("tensor", cmm)
                    ste.n += 1
                    cd_hist.append(ste.n)
                W("vector", ste, cd_hist[-1])
                W("vector", sst, sst.n)
                on("vector", lambda e: e.reciprocal(out=IDR[:, :], in_=Dnp[:, :]).then_inc(sve.h, 1))
                sve.n += 1
                W("tensor", sve, sve.n)
                on("tensor", lambda e: e.matmul(Rb2[:, :], ONERF[:, :], IDR[:, :],
                                                start=True, stop=True).then_inc(ste.h, 1))
                ste.n += 1
                W("vector", ste, ste.n)
                def vnorm(e):
                    e.tensor_copy(out=RBS[:, :], in_=Rb2[:, :])
                    e.tensor_tensor(out=CTXS[:, :], in0=Ctp[:, :], in1=RBS[:, :],
                                    op=ALU.mult).then_inc(sve.h, 1)
                on("vector", vnorm)
                sve.n += 1
                W("tensor", sve, sve.n)
                W("sync", sve, sve.n)
                def stctx(sy, h=h, cch=cch):
                    sy.dma_start(out=ctxin[cch * CG + h * 128: cch * CG + (h + 1) * 128, :],
                                 in_=CTXS[:, :]).then_inc(sst.h, 16)
                on("sync", stctx)
                sst.n += 16
            W("gpsimd", sst, sst.n)
            on("gpsimd", lambda g, cch=cch: g.collective_compute(
                "AllGather", ALU.bypass, replica_groups=PAIRS,
                ins=[ctxin[cch * CG:(cch + 1) * CG, :]],
                outs=[ctxout[cch * 2 * CG:(cch + 1) * 2 * CG, :]]).then_inc(scc.h, 1))
            scc.n += 1
            W("sync", sve, sve.n)
    barrier()

    CORR2 = E(nc.sbuf_tensor("CORR2", [1, OWN], F32))
    with ExitStack() as S5:
        e5 = S5.enter_context
        WO = e5(nc.sbuf_tensor("WO", [128, NCH * C], BF16))
        CTF = [e5(nc.sbuf_tensor(f"CTF{i}", [128, NCH * TS], BF16)) for i in range(2)]
        XO = e5(nc.sbuf_tensor("XO", [128, NCH * TS], F32))
        YF = e5(nc.sbuf_tensor("YF", [128, NCH * TS], F32))
        YSQ = e5(nc.sbuf_tensor("YSQ", [128, 512], BF16))
        CSO = e5(nc.sbuf_tensor("CSO", [128, NCH], BF16))
        CSOF = e5(nc.sbuf_tensor("CSOF", [128, NCH], F32))
        RB5 = e5(nc.sbuf_tensor("RB5", [128, 512], F32))
        Yp = [e5(nc.psum_tensor(f"YpD{i}", [128, 512], F32)) for i in range(2)]
        Mnp = e5(nc.psum_tensor("MnpD", [1, 512], F32))
        Sqp5 = e5(nc.psum_tensor("Sqp5D", [1, 512], F32))
        Rb5p = e5(nc.psum_tensor("Rb5D", [128, 512], F32))

        evac5 = {}
        on("gpsimd", lambda g: g.dma_start(
            out=WO[:, :].rearrange("d (k o) -> d k o", k=NCH),
            in_=woT.rearrange("(k d) o -> d k o", d=128)).then_inc(sldg.h, 16))
        sldg.n += 16
        W("vector", sldg, sldg.n)
        def vcso(e):
            for cc in range(NCH):
                ins = e.reduce_sum(out=CSOF[:, cc:cc + 1], in_=WO[:, cc * C:(cc + 1) * C],
                                   axis=mybir.AxisListType.X)
            ins.then_inc(sve.h, 1)
        on("vector", vcso)
        sve.n += 1
        W("vector", sve, sve.n)
        on("vector", lambda e: e.tensor_copy(out=CSO[:, :], in_=CSOF[:, :]).then_inc(sve.h, 1))
        sve.n += 1
        cso_ve = sve.n

        W("sync", scc, scc.n)
        if debug:
            W("gpsimd", scc, scc.n)
            on("gpsimd", lambda g: g.dma_start(out=ctxout_dbg, in_=ctxout[:, :]).then_inc(sst.h, 16))
            sst.n += 16

        ctf_ld = [None] * NOT_
        xo_ld = [None] * NOT_
        ymm_done = [None] * NOT_
        fin_ve = [None] * NOT_

        def ld5_ctf(i):
            if i >= 2:
                W("sync", ste, ymm_done[i - 2])
            def f(sy, i=i):
                pid = sy.partition_id()
                with sy.register(f"rp5{i}") as rp:
                    sy.reg_mov(rp, pid)
                    sy.reg_mod(rp, rp, 2)
                    with sy.If_eq(rp, 0):
                        ch = i
                        sy.dma_start(out=CTF[i % 2][:, :].rearrange("d (k n) -> d k n", k=NCH),
                                     in_=ctxout[ch * 2 * CG:(ch + 1) * 2 * CG, :]
                                     .rearrange("(k d) n -> d k n", d=128)).then_inc(sa[1].h, 16)
                    with sy.Else():
                        ch = 4 + i
                        sy.dma_start(out=CTF[i % 2][:, :].rearrange("d (k n) -> d k n", k=NCH),
                                     in_=ctxout[ch * 2 * CG:(ch + 1) * 2 * CG, :]
                                     .rearrange("(k d) n -> d k n", d=128)).then_inc(sa[1].h, 16)
            on("sync", f)
            sa[1].n += 16
            ctf_ld[i] = sa[1].n

        def ld5_xo(i):
            if i >= 1:
                W("sync", sve, fin_ve[i - 1])
            def f(sy, i=i):
                sy.dma_start(out=XO[:, :].rearrange("d (k n) -> d k n", k=NCH),
                             in_=xoT.rearrange("(k d) n -> d k n", d=128)[:, :, i * TS:(i + 1) * TS]
                             ).then_inc(sb[0].h, 16)
            on("sync", f)
            sb[0].n += 16
            xo_ld[i] = sb[0].n

        ld5_ctf(0)
        ld5_xo(0)
        ld5_ctf(1)
        W("tensor", sldg, sldg.n)

        for i in range(NOT_):
            W("tensor", sa[1], ctf_ld[i])
            W("tensor", sve, cso_ve)
            if i > 0:
                W("tensor", sve, fin_ve[i - 1])
            def mmm(e, i=i):
                for cc in range(NCH):
                    ins = e.matmul(Mnp[:, :], CSO[:, cc:cc + 1],
                                   CTF[i % 2][:, cc * TS:(cc + 1) * TS],
                                   start=(cc == 0), stop=(cc == NCH - 1))
                ins.then_inc(ste.h, 1)
            on("tensor", mmm)
            ste.n += 1
            sq_prev = None
            for oc in range(NCH):
                if (i, oc - 2) in evac5:
                    W("tensor", ssc, evac5[(i, oc - 2)])
                def ymm(e, i=i, oc=oc):
                    bank = Yp[oc % 2]
                    for cc in range(NCH):
                        ins = e.matmul(bank[:, :],
                                       WO[:, cc * C + oc * 128: cc * C + (oc + 1) * 128],
                                       CTF[i % 2][:, cc * TS:(cc + 1) * TS],
                                       start=(cc == 0), stop=(cc == NCH - 1))
                    ins.then_inc(ste.h, 1)
                on("tensor", ymm)
                ste.n += 1
                W("scalar", ste, ste.n)
                if sq_prev is not None:
                    W("scalar", ste, sq_prev)
                def yev(e, i=i, oc=oc):
                    bank = Yp[oc % 2]
                    e.activation(out=YSQ[:, :], in_=bank[:, :], func=AF.Square)
                    e.copy(out=YF[:, oc * TS:(oc + 1) * TS], in_=bank[:, :]).then_inc(ssc.h, 1)
                on("scalar", yev)
                ssc.n += 1
                evac5[(i, oc)] = ssc.n
                W("tensor", ssc, ssc.n)
                def sqm5(e, oc=oc):
                    e.matmul(Sqp5[:, :], ONEB[:, :], YSQ[:, :],
                             start=(oc == 0), stop=(oc == NCH - 1)).then_inc(ste.h, 1)
                on("tensor", sqm5)
                ste.n += 1
                sq_prev = ste.n
            ymm_done[i] = ste.n
            W("vector", ste, ste.n)
            def v5rows(e, i=i):
                e.tensor_scalar(out=ROWA[:, :], in0=Mnp[:, :], scalar1=1.0 / C,
                                scalar2=None, op0=ALU.mult)
                e.tensor_scalar(out=ROWB[:, :], in0=Sqp5[:, :], scalar1=1.0 / C,
                                scalar2=None, op0=ALU.mult)
                e.tensor_tensor(out=ROWC[:, :], in0=ROWA[:, :], in1=ROWA[:, :], op=ALU.mult)
                e.tensor_tensor(out=ROWB[:, :], in0=ROWB[:, :], in1=ROWC[:, :],
                                op=ALU.subtract).then_inc(sve.h, 1)
            on("vector", v5rows)
            sve.n += 1
            W("scalar", sve, sve.n)
            def s5rows(e):
                e.activation(out=ROWB[:, :], in_=ROWB[:, :], func=AF.Ln, bias=EPSC[:, 0:1])
                e.activation(out=ROWC[:, :], in_=ROWB[:, :], func=AF.Exp,
                             scale=-0.5).then_inc(ssc.h, 1)
            on("scalar", s5rows)
            ssc.n += 1
            W("vector", ssc, ssc.n)
            def v5corr(e, i=i):
                e.scalar_tensor_tensor(out=CORR2[:, i * TS:(i + 1) * TS], in0=ROWA[:, :],
                                       scalar=-1.0, in1=ROWC[:, :],
                                       op0=ALU.mult, op1=ALU.mult).then_inc(sve.h, 1)
            on("vector", v5corr)
            sve.n += 1
            W("tensor", sve, sve.n)
            on("tensor", lambda e: e.matmul(Rb5p[:, :], ONERF[:, :], ROWC[:, :],
                                            start=True, stop=True).then_inc(ste.h, 1))
            ste.n += 1
            W("vector", ste, ste.n)
            W("vector", sb[0], xo_ld[i])
            def v5fin(e, i=i):
                e.tensor_copy(out=RB5[:, :], in_=Rb5p[:, :])
                for cc in range(NCH):
                    e.tensor_tensor(out=YF[:, cc * TS:(cc + 1) * TS],
                                    in0=YF[:, cc * TS:(cc + 1) * TS], in1=RB5[:, :], op=ALU.mult)
                for cc in range(NCH):
                    ins = e.tensor_tensor(out=YF[:, cc * TS:(cc + 1) * TS],
                                          in0=YF[:, cc * TS:(cc + 1) * TS],
                                          in1=XO[:, cc * TS:(cc + 1) * TS], op=ALU.add)
                ins.then_inc(sve.h, 1)
            on("vector", v5fin)
            sve.n += 1
            fin_ve[i] = sve.n
            W("sync", sve, sve.n)
            def st5(sy, i=i):
                sy.dma_start(out=x2buf.rearrange("d (k n) -> d k n", k=NCH)[:, :, i * TS:(i + 1) * TS],
                             in_=YF[:, :].rearrange("d (k n) -> d k n", k=NCH)).then_inc(sst.h, 16)
            on("sync", st5)
            sst.n += 16
            if i + 1 < NOT_:
                ld5_xo(i + 1)
                if i + 2 < NOT_:
                    ld5_ctf(i + 2)
                W("scalar", sve, fin_ve[i])
                W("scalar", sst, sst.n)
        W("sync", sst, sst.n)
    barrier()

    with ExitStack() as S6:
        e6 = S6.enter_context
        WTS = e6(nc.sbuf_tensor("WTS", [128, NCH * C], BF16))
        AST = e6(nc.sbuf_tensor("AST", [128, NCH * 128], BF16))
        BST = e6(nc.sbuf_tensor("BST", [128, C], BF16))
        GWF = e6(nc.sbuf_tensor("GWF", [128, NCH * 8], F32))
        SGR = e6(nc.sbuf_tensor("SGR", [1, 8], F32))
        NM3M = e6(nc.sbuf_tensor("NM3M", [1, 512], F32))
        R3C = e6(nc.sbuf_tensor("R3C", [128, 4], F32))
        BR = e6(nc.sbuf_tensor("BR", [1, C], BF16))
        X2T = e6(nc.sbuf_tensor("X2T", [128, NCH * TS], F32))
        XH2 = e6(nc.sbuf_tensor("XH2", [128, NCH * TS], BF16))
        Y4F = e6(nc.sbuf_tensor("Y4F", [128, NCH * TS], F32))
        Y4SQ = e6(nc.sbuf_tensor("Y4SQ", [128, 512], BF16))
        CSW = e6(nc.sbuf_tensor("CSW", [128, NCH], BF16))
        CSWF = e6(nc.sbuf_tensor("CSWF", [128, NCH], F32))
        CSB = e6(nc.sbuf_tensor("CSB", [128, 1], BF16))
        CSBF = e6(nc.sbuf_tensor("CSBF", [128, 1], F32))
        SW = e6(nc.sbuf_tensor("SW", [1, C], BF16))
        SA = e6(nc.sbuf_tensor("SA", [1, 128], BF16))
        SBR = e6(nc.sbuf_tensor("SBR", [1, 1], F32))
        NM3 = e6(nc.sbuf_tensor("NM3", [1, 512], BF16))
        RB3 = e6(nc.sbuf_tensor("RB3", [128, 512], F32))
        RB6 = e6(nc.sbuf_tensor("RB6", [128, 512], F32))
        CBS = RB3
        LG = e6(nc.sbuf_tensor("LG", [128, 32], F32))
        M1 = e6(nc.sbuf_tensor("M1", [128, 4], F32))
        M2 = e6(nc.sbuf_tensor("M2", [128, 4], F32))
        IS1 = e6(nc.sbuf_tensor("IS1", [128, 32], F32))
        IS2 = e6(nc.sbuf_tensor("IS2", [128, 32], F32))
        G1 = e6(nc.sbuf_tensor("G1", [128, 4], F32))
        G2 = e6(nc.sbuf_tensor("G2", [128, 4], F32))
        GT = e6(nc.sbuf_tensor("GT", [128, 4], F32))
        WSEL = e6(nc.sbuf_tensor("WSEL", [128, 32], F32))
        W8 = e6(nc.sbuf_tensor("W8", [8, 512], F32))
        XAS = e6(nc.sbuf_tensor("XAS", [128, 512], BF16))
        XAB = e6(nc.sbuf_tensor("XAB", [128, 512], BF16))
        DG = e6(nc.sbuf_tensor("DG", [128, 64], F32)) if debug else None
        Y4p = [e6(nc.psum_tensor(f"Y4pE{i}", [128, 512], F32)) for i in range(2)]
        Sum6 = e6(nc.psum_tensor("Sum6E", [1, 512], F32))
        Sq6 = e6(nc.psum_tensor("Sq6E", [1, 512], F32))
        XAp = e6(nc.psum_tensor("XApE", [128, 512], F32))
        Gp = e6(nc.psum_tensor("GpE", [128, 32], F32))
        COL6 = e6(nc.psum_tensor("COL6E", [128, 8], F32))
        AUXp = e6(nc.psum_tensor("AUXpE", [128, 512], F32))

        evac6 = {}

        def wld6(g):
            g.dma_start(out=WTS[:, :].rearrange("d (k o) -> d k o", k=NCH),
                        in_=WT.rearrange("(k d) o -> d k o", d=128)).then_inc(sldg.h, 16)
            g.dma_start(out=AST[:, :].rearrange("d (k o) -> d k o", k=NCH),
                        in_=AstkT.rearrange("(k d) o -> d k o", d=128)).then_inc(sldg.h, 16)
            g.dma_start(out=BST[:, :], in_=BstkT).then_inc(sldg.h, 16)
            g.dma_start(out=BR[:, :], in_=brow).then_inc(sldg.h, 16)
        on("gpsimd", wld6)
        sldg.n += 4 * 16
        w6 = sldg.n
        on("sync", lambda sy: sy.dma_start(
            out=GWF[:, :].rearrange("d (k o) -> d k o", k=NCH),
            in_=gwT.rearrange("(k d) o -> d k o", d=128)).then_inc(slds.h, 16))
        slds.n += 16
        gwf_ld = slds.n

        W("vector", sldg, w6)
        def v6pre(e):
            for dc in range(NCH):
                e.reduce_sum(out=CSWF[:, dc:dc + 1], in_=WTS[:, dc * C:(dc + 1) * C],
                             axis=mybir.AxisListType.X)
            e.reduce_sum(out=CSBF[:, :], in_=BST[:, :], axis=mybir.AxisListType.X)
            e.reduce_sum(out=SBR[:, :], in_=BR[:, :],
                         axis=mybir.AxisListType.X).then_inc(sve.h, 1)
        on("vector", v6pre)
        sve.n += 1
        W("vector", sve, sve.n)
        def v6preb(e):
            e.tensor_copy(out=CSW[:, :], in_=CSWF[:, :])
            e.tensor_copy(out=CSB[:, :], in_=CSBF[:, :]).then_inc(sve.h, 1)
        on("vector", v6preb)
        sve.n += 1
        W("tensor", sldg, w6)
        for sl in range(4):
            def swmm(e, sl=sl):
                for dc in range(NCH):
                    ins = e.matmul(Sum6[:, :], ONEB[:, :],
                                   WTS[:, dc * C + sl * 512: dc * C + (sl + 1) * 512],
                                   start=(dc == 0), stop=(dc == NCH - 1))
                ins.then_inc(ste.h, 1)
            on("tensor", swmm)
            ste.n += 1
            W("vector", ste, ste.n)
            on("vector", lambda e, sl=sl: e.tensor_copy(
                out=SW[:, sl * 512:(sl + 1) * 512], in_=Sum6[:, :]).then_inc(sve.h, 1))
            sve.n += 1
            W("tensor", sve, sve.n)
        W("tensor", slds, gwf_ld)
        def sgmm2(e):
            for dc in range(NCH):
                e.matmul(COL6[0:1, 0:8], ONEF[:, :], GWF[:, dc * 8:(dc + 1) * 8],
                         start=(dc == 0), stop=(dc == NCH - 1))
            for dc in range(NCH):
                ins = e.matmul(Sq6[:, 0:128], ONEB[:, :], AST[:, dc * 128:(dc + 1) * 128],
                               start=(dc == 0), stop=(dc == NCH - 1))
            ins.then_inc(ste.h, 1)
        on("tensor", sgmm2)
        ste.n += 1
        W("vector", ste, ste.n)
        def v6pre2(e):
            e.tensor_copy(out=SGR[:, :], in_=COL6[0:1, 0:8])
            e.tensor_copy(out=SA[:, :], in_=Sq6[:, 0:128]).then_inc(sve.h, 1)
        on("vector", v6pre2)
        sve.n += 1
        pre_ve = sve.n
        W("tensor", sve, pre_ve)

        x2ld = [None] * NOT_
        fin6 = [None] * NOT_

        def ld6(i):
            if i >= 1:
                W("sync", sve, fin6[i - 1])
            def f(sy, i=i):
                sy.dma_start(out=X2T[:, :].rearrange("d (k n) -> d k n", k=NCH),
                             in_=x2buf.rearrange("d (k n) -> d k n", k=NCH)[:, :, i * TS:(i + 1) * TS]
                             ).then_inc(sb[0].h, 16)
            on("sync", f)
            sb[0].n += 16
            x2ld[i] = sb[0].n

        ld6(0)
        for i in range(NOT_):
            W("scalar", sb[0], x2ld[i])
            if i > 0:
                W("scalar", sve, fin6[i - 1])
                W("scalar", sst, sst.n)
            def s6a(e, i=i):
                e.activation(out=XH2[:, :], in_=X2T[:, :],
                             func=AF.Square).then_inc(ssc.h, 1)
            on("scalar", s6a)
            ssc.n += 1
            W("tensor", ssc, ssc.n)
            if i > 0:
                W("tensor", sve, fin6[i - 1])
            def t6stats(e, i=i):
                for dc in range(NCH):
                    e.matmul(Sum6[:, :], ONEF[:, :], X2T[:, dc * TS:(dc + 1) * TS],
                             start=(dc == 0), stop=(dc == NCH - 1))
                for dc in range(NCH):
                    ins = e.matmul(Sq6[:, :], ONEB[:, :], XH2[:, dc * TS:(dc + 1) * TS],
                                   start=(dc == 0), stop=(dc == NCH - 1))
                ins.then_inc(ste.h, 1)
            on("tensor", t6stats)
            ste.n += 1
            W("vector", ste, ste.n)
            def v6rows(e):
                e.tensor_scalar(out=ROWA[:, :], in0=Sum6[:, :], scalar1=1.0 / C,
                                scalar2=None, op0=ALU.mult)
                e.tensor_scalar(out=ROWB[:, :], in0=Sq6[:, :], scalar1=1.0 / C,
                                scalar2=None, op0=ALU.mult)
                e.tensor_scalar(out=NM3M[:, :], in0=ROWA[:, :], scalar1=-1.0,
                                scalar2=None, op0=ALU.mult)
                e.tensor_tensor(out=ROWC[:, :], in0=ROWA[:, :], in1=ROWA[:, :], op=ALU.mult)
                e.tensor_tensor(out=ROWB[:, :], in0=ROWB[:, :], in1=ROWC[:, :],
                                op=ALU.subtract).then_inc(sve.h, 1)
            on("vector", v6rows)
            sve.n += 1
            if debug and i == 0:
                W("sync", sve, sve.n)
                def drows3(sy):
                    sy.dma_start(out=d_rows[:, 0:TS], in_=ROWA[:, :]).then_inc(sst.h, 16)
                    sy.dma_start(out=d_rows[:, TS:2 * TS], in_=ROWB[:, :]).then_inc(sst.h, 16)
                on("sync", drows3)
                sst.n += 32
                W("scalar", sst, sst.n)
            W("scalar", sve, sve.n)
            def s6rows(e):
                e.activation(out=ROWB[:, :], in_=ROWB[:, :], func=AF.Ln, bias=EPSC[:, 0:1])
                e.activation(out=ROWC[:, :], in_=ROWB[:, :], func=AF.Exp,
                             scale=-0.5).then_inc(ssc.h, 1)
            on("scalar", s6rows)
            ssc.n += 1
            W("vector", ssc, ssc.n)
            def v6neg(e):
                e.scalar_tensor_tensor(out=NM3[:, :], in0=ROWA[:, :], scalar=-1.0,
                                       in1=ROWC[:, :], op0=ALU.mult,
                                       op1=ALU.mult).then_inc(sve.h, 1)
            on("vector", v6neg)
            sve.n += 1
            W("tensor", sve, sve.n)
            def t6cols(e):
                for tc in range(4):
                    e.matmul(COL6[:, tc:tc + 1],
                             ROWC[:, tc * 128:(tc + 1) * 128], ONE1F[:, :],
                             start=True, stop=True)
                e.matmul(AUXp[:, :], ONERF[:, :], ROWC[:, :],
                         start=True, stop=True).then_inc(ste.h, 1)
            on("tensor", t6cols)
            ste.n += 1
            W("vector", ste, ste.n)
            def v6xh(e, i=i):
                e.tensor_copy(out=R3C[:, :], in_=COL6[:, 0:4])
                e.tensor_copy(out=RB3[:, :], in_=AUXp[:, :])
                for dc in range(NCH):
                    ins = e.tensor_tensor(out=XH2[:, dc * TS:(dc + 1) * TS],
                                          in0=X2T[:, dc * TS:(dc + 1) * TS],
                                          in1=RB3[:, :], op=ALU.mult)
                ins.then_inc(sve.h, 1)
            on("vector", v6xh)
            sve.n += 1
            xh_ve = sve.n
            if debug and i == 0:
                W("sync", sve, xh_ve)
                on("sync", lambda sy: sy.dma_start(out=d_xh2, in_=XH2[:, :]).then_inc(sst.h, 16))
                sst.n += 16

            W("tensor", sve, xh_ve)
            def t6gate(e, i=i):
                for tc in range(4):
                    for dc in range(NCH):
                        e.matmul(Gp[:, tc * 8:(tc + 1) * 8],
                                 X2T[:, dc * TS + tc * 128: dc * TS + (tc + 1) * 128],
                                 GWF[:, dc * 8:(dc + 1) * 8],
                                 start=(dc == 0), stop=False)
                    e.matmul(Gp[:, tc * 8:(tc + 1) * 8],
                             NM3M[:, tc * 128:(tc + 1) * 128],
                             SGR[:, :], start=False, stop=True)
                for dc in range(NCH):
                    e.matmul(XAp[:, :], AST[:, dc * 128:(dc + 1) * 128],
                             XH2[:, dc * TS:(dc + 1) * TS],
                             start=(dc == 0), stop=False)
                e.matmul(XAp[:, :], SA[:, :], NM3[:, :],
                         start=False, stop=True).then_inc(ste.h, 1)
            on("tensor", t6gate)
            ste.n += 1
            gate_te = ste.n
            W("scalar", ste, gate_te)
            on("scalar", lambda e: e.copy(out=XAS[:, :], in_=XAp[:, :]).then_inc(ssc.h, 1))
            ssc.n += 1
            xas_sc = ssc.n
            def step(fn):
                on("vector", fn)
                sve.n += 1
                W("vector", sve, sve.n)

            W("vector", ste, gate_te)
            def g_s0(e, i=i):
                if debug and i == 0:
                    e.tensor_copy(out=DG[:, 44:48], in_=R3C[:, :])
                for tc in range(4):
                    ins = e.tensor_scalar(out=LG[:, tc * 8:(tc + 1) * 8],
                                          in0=Gp[:, tc * 8:(tc + 1) * 8],
                                          scalar1=R3C[:, tc:tc + 1], scalar2=None,
                                          op0=ALU.mult)
                ins.then_inc(sve.h, 1)
            step(g_s0)
            def g_s1(e, i=i):
                if debug and i == 0:
                    e.tensor_copy(out=DG[:, 0:8], in_=Gp[:, 0:8])
                for tc in range(4):
                    ins = e.tensor_tensor(out=LG[:, tc * 8:(tc + 1) * 8],
                                          in0=LG[:, tc * 8:(tc + 1) * 8],
                                          in1=GBB[:, :], op=ALU.add)
                ins.then_inc(sve.h, 1)
            step(g_s1)
            def g_s2(e, i=i):
                if debug and i == 0:
                    e.tensor_copy(out=DG[:, 8:16], in_=LG[:, 0:8])
                for tc in range(4):
                    ins = e.reduce_max(out=M1[:, tc:tc + 1], in_=LG[:, tc * 8:(tc + 1) * 8],
                                       axis=mybir.AxisListType.X)
                ins.then_inc(sve.h, 1)
            step(g_s2)
            def g_s3(e, i=i):
                if debug and i == 0:
                    e.tensor_copy(out=DG[:, 16:17], in_=M1[:, 0:1])
                for tc in range(4):
                    ins = e.tensor_tensor(out=IS1[:, tc * 8:(tc + 1) * 8],
                                          in0=LG[:, tc * 8:(tc + 1) * 8],
                                          in1=M1[:, tc:tc + 1].broadcast_to([128, 8]),
                                          op=ALU.is_ge)
                ins.then_inc(sve.h, 1)
            step(g_s3)
            def g_s4(e, i=i):
                if debug and i == 0:
                    e.tensor_copy(out=DG[:, 18:26], in_=IS1[:, 0:8])
                for tc in range(4):
                    ins = e.scalar_tensor_tensor(out=LG[:, tc * 8:(tc + 1) * 8],
                                                 in0=IS1[:, tc * 8:(tc + 1) * 8], scalar=NEG,
                                                 in1=LG[:, tc * 8:(tc + 1) * 8],
                                                 op0=ALU.mult, op1=ALU.add)
                ins.then_inc(sve.h, 1)
            step(g_s4)
            def g_s5(e, i=i):
                if debug and i == 0:
                    e.tensor_copy(out=DG[:, 26:34], in_=LG[:, 0:8])
                for tc in range(4):
                    ins = e.reduce_max(out=M2[:, tc:tc + 1], in_=LG[:, tc * 8:(tc + 1) * 8],
                                       axis=mybir.AxisListType.X)
                ins.then_inc(sve.h, 1)
            step(g_s5)
            def g_s6(e, i=i):
                if debug and i == 0:
                    e.tensor_copy(out=DG[:, 17:18], in_=M2[:, 0:1])
                e.tensor_tensor(out=G1[:, :], in0=M1[:, :], in1=M2[:, :],
                                op=ALU.subtract).then_inc(sve.h, 1)
            step(g_s6)
            W("scalar", sve, sve.n)
            on("scalar", lambda e: e.activation(out=GT[:, :], in_=G1[:, :], func=AF.Tanh,
                                                scale=0.5).then_inc(ssc.h, 1))
            ssc.n += 1
            W("vector", ssc, ssc.n)
            step(lambda e: e.tensor_scalar(out=G1[:, :], in0=GT[:, :], scalar1=0.5,
                                           scalar2=0.5, op0=ALU.mult,
                                           op1=ALU.add).then_inc(sve.h, 1))
            step(lambda e: e.tensor_scalar(out=G2[:, :], in0=GT[:, :], scalar1=-0.5,
                                           scalar2=0.5, op0=ALU.mult,
                                           op1=ALU.add).then_inc(sve.h, 1))
            def g_s9(e):
                for tc in range(4):
                    ins = e.tensor_tensor(out=IS2[:, tc * 8:(tc + 1) * 8],
                                          in0=LG[:, tc * 8:(tc + 1) * 8],
                                          in1=M2[:, tc:tc + 1].broadcast_to([128, 8]),
                                          op=ALU.is_ge)
                ins.then_inc(sve.h, 1)
            step(g_s9)
            def g_s10(e):
                for tc in range(4):
                    ins = e.tensor_tensor(out=IS1[:, tc * 8:(tc + 1) * 8],
                                          in0=IS1[:, tc * 8:(tc + 1) * 8],
                                          in1=G1[:, tc:tc + 1].broadcast_to([128, 8]),
                                          op=ALU.mult)
                ins.then_inc(sve.h, 1)
            step(g_s10)
            def g_s11(e):
                for tc in range(4):
                    ins = e.tensor_tensor(out=IS2[:, tc * 8:(tc + 1) * 8],
                                          in0=IS2[:, tc * 8:(tc + 1) * 8],
                                          in1=G2[:, tc:tc + 1].broadcast_to([128, 8]),
                                          op=ALU.mult)
                ins.then_inc(sve.h, 1)
            step(g_s11)
            def g_s12(e, i=i):
                for tc in range(4):
                    ins = e.tensor_tensor(out=WSEL[:, tc * 8:(tc + 1) * 8],
                                          in0=IS1[:, tc * 8:(tc + 1) * 8],
                                          in1=IS2[:, tc * 8:(tc + 1) * 8], op=ALU.add)
                if debug and i == 0:
                    e.tensor_copy(out=DG[:, 34:42], in_=WSEL[:, 0:8])
                    e.tensor_copy(out=DG[:, 42:43], in_=G1[:, 0:1])
                    e.tensor_copy(out=DG[:, 43:44], in_=G2[:, 0:1])
                    e.tensor_copy(out=DG[:, 48:56], in_=IS1[:, 0:8])
                    e.tensor_copy(out=DG[:, 56:64], in_=IS2[:, 0:8])
                ins.then_inc(sve.h, 1)
            step(g_s12)
            W("tensor", sve, sve.n)
            def g_tr(e):
                for tc in range(4):
                    ins = e.transpose(AUXp[0:8, tc * 128:(tc + 1) * 128],
                                      WSEL[:, tc * 8:(tc + 1) * 8], ID128[:, :])
                ins.then_inc(ste.h, 1)
            on("tensor", g_tr)
            ste.n += 1
            W("vector", ste, ste.n)
            def g_w8(e):
                for tc in range(4):
                    ins = e.tensor_copy(out=W8[:, tc * 128:(tc + 1) * 128],
                                        in_=AUXp[0:8, tc * 128:(tc + 1) * 128])
                ins.then_inc(sve.h, 1)
            on("vector", g_w8)
            sve.n += 1
            W("vector", sve, sve.n)
            w8_ve = sve.n
            if debug and i == 0:
                W("sync", sve, w8_ve)
                on("sync", lambda sy: sy.dma_start(out=d_w8, in_=W8[:, :]).then_inc(sst.h, 16))
                on("sync", lambda sy: sy.dma_start(out=d_gate, in_=DG[:, :]).then_inc(sst.h, 16))
                sst.n += 32
            W("tensor", sve, w8_ve)
            on("tensor", lambda e: e.matmul(AUXp[:, :], E8F[:, :], W8[:, :],
                                            start=True, stop=True).then_inc(ste.h, 1))
            ste.n += 1
            W("vector", ste, ste.n)
            W("vector", ssc, xas_sc)
            def v6xab(e):
                e.tensor_tensor(out=XAB[:, :], in0=XAS[:, :], in1=AUXp[:, :],
                                op=ALU.mult).then_inc(sve.h, 1)
            on("vector", v6xab)
            sve.n += 1
            xab_ve = sve.n
            if debug and i == 0:
                W("sync", sve, xab_ve)
                on("sync", lambda sy: sy.dma_start(out=d_xab, in_=XAB[:, :]).then_inc(sst.h, 16))
                sst.n += 16
            W("tensor", sve, xab_ve)

            sq_prev6 = None
            for oc in range(NCH):
                if (i, oc - 2) in evac6:
                    W("tensor", ssc, evac6[(i, oc - 2)])
                def y4mm(e, i=i, oc=oc):
                    bank = Y4p[oc % 2]
                    for dc in range(NCH):
                        e.matmul(bank[:, :],
                                 WTS[:, dc * C + oc * 128: dc * C + (oc + 1) * 128],
                                 XH2[:, dc * TS:(dc + 1) * TS],
                                 start=(dc == 0), stop=False)
                    e.matmul(bank[:, :], SW[:, oc * 128:(oc + 1) * 128], NM3[:, :],
                             start=False, stop=False)
                    e.matmul(bank[:, :], BR[:, oc * 128:(oc + 1) * 128], OR512B[:, :],
                             start=False, stop=False)
                    e.matmul(bank[:, :], BST[:, oc * 128:(oc + 1) * 128], XAB[:, :],
                             start=False, stop=True).then_inc(ste.h, 1)
                on("tensor", y4mm)
                ste.n += 1
                W("scalar", ste, ste.n)
                if sq_prev6 is not None:
                    W("scalar", ste, sq_prev6)
                def y4ev(e, i=i, oc=oc):
                    bank = Y4p[oc % 2]
                    e.activation(out=Y4SQ[:, :], in_=bank[:, :], func=AF.Square)
                    e.copy(out=Y4F[:, oc * TS:(oc + 1) * TS], in_=bank[:, :]).then_inc(ssc.h, 1)
                on("scalar", y4ev)
                ssc.n += 1
                evac6[(i, oc)] = ssc.n
                W("tensor", ssc, ssc.n)
                def sq6m(e, oc=oc):
                    e.matmul(Sq6[:, :], ONEB[:, :], Y4SQ[:, :],
                             start=(oc == 0), stop=(oc == NCH - 1)).then_inc(ste.h, 1)
                on("tensor", sq6m)
                ste.n += 1
                sq_prev6 = ste.n
            def m4mm(e, i=i):
                for dc in range(NCH):
                    e.matmul(Sum6[:, :], CSW[:, dc:dc + 1], XH2[:, dc * TS:(dc + 1) * TS],
                             start=(dc == 0), stop=False)
                e.matmul(Sum6[:, :], CSB[:, :], XAB[:, :],
                         start=False, stop=True).then_inc(ste.h, 1)
            on("tensor", m4mm)
            ste.n += 1
            W("vector", ste, ste.n)
            def v6r4(e):
                e.tensor_scalar(out=ROWA[:, :], in0=Sum6[:, :], scalar1=SBR[:, 0:1],
                                scalar2=1.0 / C, op0=ALU.add, op1=ALU.mult)
                e.tensor_scalar(out=ROWB[:, :], in0=Sq6[:, :], scalar1=1.0 / C,
                                scalar2=None, op0=ALU.mult)
                e.tensor_tensor(out=ROWC[:, :], in0=ROWA[:, :], in1=ROWA[:, :], op=ALU.mult)
                e.tensor_tensor(out=ROWB[:, :], in0=ROWB[:, :], in1=ROWC[:, :],
                                op=ALU.subtract).then_inc(sve.h, 1)
            on("vector", v6r4)
            sve.n += 1
            if debug and i == 0:
                W("sync", sve, sve.n)
                W("sync", ssc, evac6[(i, NCH - 1)])
                def dy4(sy):
                    sy.dma_start(out=d_y4, in_=Y4F[:, :]).then_inc(sst.h, 16)
                    sy.dma_start(out=d_rows[:, 2 * TS:3 * TS], in_=ROWA[:, :]).then_inc(sst.h, 16)
                    sy.dma_start(out=d_rows[:, 3 * TS:4 * TS], in_=ROWB[:, :]).then_inc(sst.h, 16)
                on("sync", dy4)
                sst.n += 48
                W("scalar", sst, sst.n)
                W("vector", sst, sst.n)
            W("scalar", sve, sve.n)
            def s6r4(e):
                e.activation(out=ROWB[:, :], in_=ROWB[:, :], func=AF.Ln, bias=EPSC[:, 0:1])
                e.activation(out=ROWC[:, :], in_=ROWB[:, :], func=AF.Exp,
                             scale=-0.5).then_inc(ssc.h, 1)
            on("scalar", s6r4)
            ssc.n += 1
            W("vector", ssc, ssc.n)
            def v6cf(e, i=i):
                e.scalar_tensor_tensor(out=ROWA[:, :], in0=ROWA[:, :], scalar=-1.0,
                                       in1=ROWC[:, :], op0=ALU.mult, op1=ALU.mult)
                e.tensor_tensor(out=ROWA[:, :], in0=ROWA[:, :],
                                in1=CORR2[:, i * TS:(i + 1) * TS], op=ALU.add).then_inc(sve.h, 1)
            on("vector", v6cf)
            sve.n += 1
            W("tensor", sve, sve.n)
            def t6bc2(e):
                e.matmul(AUXp[:, :], ONERF[:, :], ROWC[:, :], start=True, stop=True)
                e.matmul(XAp[:, :], ONERF[:, :], ROWA[:, :],
                         start=True, stop=True).then_inc(ste.h, 1)
            on("tensor", t6bc2)
            ste.n += 1
            W("vector", ste, ste.n)
            def v6fin(e, i=i):
                e.tensor_copy(out=RB6[:, :], in_=AUXp[:, :])
                e.tensor_copy(out=CBS[:, :], in_=XAp[:, :])
                for dc in range(NCH):
                    e.tensor_tensor(out=Y4F[:, dc * TS:(dc + 1) * TS],
                                    in0=Y4F[:, dc * TS:(dc + 1) * TS], in1=RB6[:, :], op=ALU.mult)
                for dc in range(NCH):
                    e.tensor_tensor(out=Y4F[:, dc * TS:(dc + 1) * TS],
                                    in0=Y4F[:, dc * TS:(dc + 1) * TS],
                                    in1=X2T[:, dc * TS:(dc + 1) * TS], op=ALU.add)
                for dc in range(NCH):
                    ins = e.tensor_tensor(out=Y4F[:, dc * TS:(dc + 1) * TS],
                                          in0=Y4F[:, dc * TS:(dc + 1) * TS],
                                          in1=CBS[:, :], op=ALU.add)
                ins.then_inc(sve.h, 1)
            on("vector", v6fin)
            sve.n += 1
            fin6[i] = sve.n
            W("sync", sve, sve.n)
            def st6(sy, i=i):
                sy.dma_start(out=outT.rearrange("(k d) n -> d k n", d=128)[:, :, i * TS:(i + 1) * TS],
                             in_=Y4F[:, :].rearrange("d (k n) -> d k n", k=NCH)).then_inc(sst.h, 16)
            on("sync", st6)
            sst.n += 16
            if i + 1 < NOT_:
                ld6(i + 1)
        W("sync", sst, sst.n)

    with nc.Block() as block:
        @block.tensor
        def _(t):
            for f in ops["tensor"]:
                f(t)

        @block.scalar
        def _(s):
            for f in ops["scalar"]:
                f(s)

        @block.vector
        def _(v):
            for f in ops["vector"]:
                f(v)

        @block.gpsimd
        def _(g):
            for f in ops["gpsimd"]:
                f(g)

        @block.sync
        def _(sy):
            for f in ops["sync"]:
                f(sy)

    ctx_stack.close()
    return nc


def _host_prep(inputs):
    x = np.asarray(inputs["L0_attn_x"], dtype=np.float32)
    wq = np.asarray(inputs["L0_attn_w_q"], dtype=np.float32)
    wk = np.asarray(inputs["L0_attn_w_k"], dtype=np.float32)
    wv = np.asarray(inputs["L0_attn_w_v"], dtype=np.float32)
    wo = np.asarray(inputs["L0_attn_w_o"], dtype=np.float32)
    Wf = np.asarray(inputs["L0_ffn_W"], dtype=np.float32)
    bf_ = np.asarray(inputs["L0_ffn_b"], dtype=np.float32)
    lA = np.asarray(inputs["L0_ffn_lora_A"], dtype=np.float32)
    lB = np.asarray(inputs["L0_ffn_lora_B"], dtype=np.float32)
    gw = np.asarray(inputs["L0_ffn_gate_w"], dtype=np.float32)
    gb = np.asarray(inputs["L0_ffn_gate_b"], dtype=np.float32)

    woT = np.ascontiguousarray(wo.T)
    WTt = np.ascontiguousarray(Wf.T)
    AstkT = np.ascontiguousarray(lA.reshape(128, C).T)
    BstkT = np.ascontiguousarray(lB.transpose(0, 2, 1).reshape(128, C))
    gwT = np.ascontiguousarray(gw.T)

    t_ = np.arange(T, dtype=np.float32)
    inv = (1.0 / (10000.0 ** (np.arange(0, D, 2, dtype=np.float32) / D))).astype(np.float32)
    freqs = np.outer(t_, inv).astype(np.float32)
    emb = np.concatenate([freqs, freqs], axis=1)
    cosT = np.ascontiguousarray(np.cos(emb).T.astype(np.float32))
    sinv = np.sin(emb).astype(np.float32)
    sgn = np.ones((128,), np.float32)
    sgn[:64] = -1.0
    sinT = np.ascontiguousarray((sinv * sgn[None, :]).T.astype(np.float32))

    c_ones = np.ones((128, 1), np.float32)
    c_masks = np.zeros((128, 3), np.float32)
    c_masks[64:, 1] = NEG
    c_masks[:64, 2] = NEG
    c_E8 = np.zeros((8, 128), np.float32)
    for e_ in range(8):
        c_E8[e_, e_ * 16:(e_ + 1) * 16] = 1.0
    c_id = np.eye(128, dtype=np.float32)
    c_ones512 = np.ones((1, 512), np.float32)

    in_maps = []
    for core in range(8):
        b, g = core // 2, core % 2
        xTb = np.ascontiguousarray(x[b].T)
        m = dict(
            xT=xTb,
            xoT=np.ascontiguousarray(xTb[:, g * OWN:(g + 1) * OWN]),
            wqT=np.ascontiguousarray(wq[g * CG:(g + 1) * CG, :].T),
            wkT=np.ascontiguousarray(wk[g * CG:(g + 1) * CG, :].T),
            wvT=np.ascontiguousarray(wv[g * CG:(g + 1) * CG, :].T),
            woT=woT, WT=WTt, brow=bf_[None, :],
            AstkT=AstkT, BstkT=BstkT, gwT=gwT, gbrow=gb[None, :],
            cosT=cosT, sinT=sinT,
            c_ones=c_ones, c_masks=c_masks, c_E8=c_E8, c_id=c_id,
            c_ones512=c_ones512, c_eps=np.full((1, 1), EPS, np.float32),
        )
        in_maps.append(m)
    return in_maps


def kernel(**inputs):
    in_maps = _host_prep(inputs)
    nc = build_nc(debug=False)
    res = run_bass_kernel_spmd(nc, in_maps, core_ids=list(range(8)))
    out = np.empty((B, T, C), dtype=np.float32)
    for core in range(8):
        b, g = core // 2, core % 2
        out[b, g * OWN:(g + 1) * OWN, :] = np.asarray(res.results[core]["outT"]).T
    return out


# version 03
# speedup vs baseline: 1.1276x; 1.1276x over previous
import numpy as np
from contextlib import ExitStack

import concourse.bass as bass
import concourse.mybir as mybir
from concourse.bass_utils import run_bass_kernel_spmd

F32 = mybir.dt.float32
BF16 = mybir.dt.bfloat16
AF = mybir.ActivationFunctionType
ALU = mybir.AluOpType

B, T, C, H, D = 4, 4096, 2048, 16, 128
HG = 8
CG = HG * D
NT, TS = 8, 512
NCH = C // 128
OWN = 2048
NOT_ = 4
SCALE = float(1.0 / np.sqrt(D))
EPS = 1e-5
NEG = -1e30

ENGS = ("tensor", "scalar", "vector", "gpsimd", "sync")
PAIRS = [[0, 1], [2, 3], [4, 5], [6, 7]]


def _attn_plan():
    nb = T // 64
    bp = np.arange(nb)
    qb, kb = bp[:, None], bp[None, :]
    dist = qb - kb
    bm = (((dist >= 0) & (dist <= 16)) | (kb < 2) | (kb % 4 == 0)) & (kb <= qb)
    plan = []
    for c in range(8):
        rows = bm[c * 8:(c + 1) * 8]
        groups = []
        for j in range(nb // 2):
            a = np.where(rows[:, 2 * j])[0]
            b = np.where(rows[:, 2 * j + 1])[0]
            if len(a) == 0 and len(b) == 0:
                continue
            if len(a) and len(b):
                r0 = int(min(a[0], b[0])); r1 = int(max(a[-1], b[-1]))
                segs = []
                cur = None
                for r in range(r0, r1 + 1):
                    key = (r in a, r in b)
                    if cur is None or cur[2] != key:
                        if cur is not None:
                            segs.append(tuple(cur))
                        cur = [r, r, key]
                    else:
                        cur[1] = r
                segs.append(tuple(cur))
                segs = [(s0, s1, 0 if k == (True, True) else (1 if k == (True, False) else 2))
                        for (s0, s1, k) in segs]
                groups.append(dict(kind="pair", j=j, r0=r0, r1=r1, segs=segs))
            else:
                side = 0 if len(a) else 1
                rr = a if len(a) else b
                groups.append(dict(kind="single", j=j, side=side,
                                   r0=int(rr[0]), r1=int(rr[-1])))
        plan.append(groups)
    return plan


ATTN_PLAN = _attn_plan()


class Sem:
    __slots__ = ("h", "n", "name")

    def __init__(self, handle, name):
        self.h = handle
        self.n = 0
        self.name = name


def build_nc(debug=False):
    nc = bass.Bass("TRN2", target_bir_lowering=False, debug=False)

    def din(name, shape, dt):
        return nc.dram_tensor(name, shape, dt, kind="ExternalInput").ap()

    def dout(name, shape, dt):
        return nc.dram_tensor(name, shape, dt, kind="ExternalOutput").ap()

    def dint(name, shape, dt):
        if debug:
            return dout(name, shape, dt)
        return nc.dram_tensor(name, shape, dt).ap()

    xT = din("xT", [C, T], F32)
    xoT = din("xoT", [C, OWN], F32)
    wqT = din("wqT", [C, CG], F32)
    wkT = din("wkT", [C, CG], F32)
    wvT = din("wvT", [C, CG], F32)
    woT = din("woT", [C, C], F32)
    WT = din("WT", [C, C], F32)
    brow = din("brow", [1, C], F32)
    AstkT = din("AstkT", [C, 128], F32)
    BstkT = din("BstkT", [128, C], F32)
    gwT = din("gwT", [C, 8], F32)
    gbrow = din("gbrow", [1, 8], F32)
    cosT = din("cosT", [128, T], F32)
    sinT = din("sinT", [128, T], F32)
    c_ones = din("c_ones", [128, 1], F32)
    c_masks = din("c_masks", [128, 3], F32)
    c_E8 = din("c_E8", [8, 128], F32)
    c_id = din("c_id", [128, 128], F32)
    c_ones512 = din("c_ones512", [1, 512], F32)
    c_eps = din("c_eps", [1, 1], F32)

    outT = dout("outT", [C, OWN], F32)

    qbuf = dint("qbuf", [CG, T], BF16)
    kbuf = dint("kbuf", [CG, T], BF16)
    vbuf = dint("vbuf", [128, 32 * CG], BF16)
    ctxin = nc.dram_tensor("ctxin", [8 * CG, TS], BF16).ap()
    ctxout = nc.dram_tensor("ctxout", [16 * CG, TS], BF16).ap()
    ctxout_dbg = dout("ctxout", [16 * CG, TS], BF16) if False else None
    if debug:
        ctxout_dbg = dout("ctxout_dbg", [16 * CG, TS], BF16)
        d_xh2 = dout("d_xh2", [128, NCH * TS], BF16)
        d_w8 = dout("d_w8", [8, TS], F32)
        d_xab = dout("d_xab", [128, TS], BF16)
        d_y4 = dout("d_y4", [128, NCH * TS], F32)
        d_rows = dout("d_rows", [1, 4 * TS], F32)
        d_gate = dout("d_gate", [128, 64], F32)
    x2buf = dint("x2buf", [128, NCH * OWN], F32)
    agw_i = nc.dram_tensor("agw_i", [8, 512], BF16).ap()
    agw_o = nc.dram_tensor("agw_o", [16, 512], BF16).ap()

    ops = {e: [] for e in ENGS}

    def on(eng, fn):
        ops[eng].append(fn)

    ctx_stack = ExitStack()
    E = ctx_stack.enter_context

    sems = {}
    for nm in ("ste", "ssc", "sve", "scc", "sbar",
               "sx0", "sx1", "sldg", "slds", "sst", "sa0", "sa1", "sb0", "sb1"):
        sems[nm] = Sem(E(nc.semaphore(nm)), nm)
    ste, ssc, sve = sems["ste"], sems["ssc"], sems["sve"]
    scc, sbar = sems["scc"], sems["sbar"]
    sx = [sems["sx0"], sems["sx1"]]
    sldg, slds, sst = sems["sldg"], sems["slds"], sems["sst"]
    sa = [sems["sa0"], sems["sa1"]]
    sb = [sems["sb0"], sems["sb1"]]

    def W(eng, sem, val):
        assert val <= sem.n, f"wait {sem.name}>={val} but only {sem.n} planned"
        on(eng, lambda e, s=sem.h, v=val: e.wait_ge(s, v))

    nbar = [0]

    def barrier():
        nbar[0] += 1
        tgt = 5 * nbar[0]
        for eng in ENGS:
            on(eng, lambda e: e.nop().then_inc(sbar.h, 1))
        sbar.n += 5
        for eng in ENGS:
            W(eng, sbar, tgt)

    ONEB = E(nc.sbuf_tensor("ONEB", [128, 1], BF16))
    ONEF = E(nc.sbuf_tensor("ONEF", [128, 1], F32))
    ONERF = E(nc.sbuf_tensor("ONERF", [1, 128], F32))
    ONE1F = E(nc.sbuf_tensor("ONE1F", [1, 1], F32))
    OR512B = E(nc.sbuf_tensor("OR512B", [1, 512], BF16))
    MASKC = E(nc.sbuf_tensor("MASKC", [128, 3], F32))
    E8F = E(nc.sbuf_tensor("E8F", [8, 128], F32))
    ID128 = E(nc.sbuf_tensor("ID128", [128, 128], F32))
    GBB = E(nc.sbuf_tensor("GBB", [128, 8], F32))
    GBR = E(nc.sbuf_tensor("GBR", [1, 8], F32))
    RSTD = E(nc.sbuf_tensor("RSTD", [1, T], F32))
    RSTDC = E(nc.sbuf_tensor("RSTDC", [128, 32], F32))
    ROWA = E(nc.sbuf_tensor("ROWA", [1, 512], F32))
    ROWB = E(nc.sbuf_tensor("ROWB", [1, 512], F32))
    ROWC = E(nc.sbuf_tensor("ROWC", [1, 512], F32))
    EPSC = E(nc.sbuf_tensor("EPSC", [1, 1], F32))

    def ld0(sy):
        sy.dma_start(out=ONEF[:, :], in_=c_ones).then_inc(slds.h, 16)
        sy.dma_start(out=ONERF[:, :], in_=c_ones.rearrange("p o -> o p")).then_inc(slds.h, 16)
        sy.dma_start(out=ONE1F[:, :], in_=c_ones[0:1, :]).then_inc(slds.h, 16)
        sy.dma_start(out=MASKC[:, :], in_=c_masks).then_inc(slds.h, 16)
        sy.dma_start(out=E8F[:, :], in_=c_E8).then_inc(slds.h, 16)
        sy.dma_start(out=ID128[:, :], in_=c_id).then_inc(slds.h, 16)
        sy.dma_start(out=GBR[:, :], in_=gbrow).then_inc(slds.h, 16)
        sy.dma_start(out=EPSC[:, :], in_=c_eps).then_inc(slds.h, 16)
    on("sync", ld0)
    slds.n += 8 * 16

    def ld0g(g):
        g.dma_start(out=ONEB[:, :], in_=c_ones).then_inc(sldg.h, 16)
        g.dma_start(out=OR512B[:, :], in_=c_ones512).then_inc(sldg.h, 16)
        g.collective_compute("AllGather", ALU.bypass, replica_groups=PAIRS,
                             ins=[agw_i[:, :]], outs=[agw_o[:, :]]).then_inc(scc.h, 1)
    on("gpsimd", ld0g)
    sldg.n += 2 * 16
    scc.n += 1

    S0p = ExitStack()
    PB0 = S0p.enter_context(nc.psum_tensor("PB0", [128, 8], F32))
    W("tensor", slds, slds.n)
    on("tensor", lambda e: e.matmul(PB0[:, :], ONERF[:, :], GBR[:, :],
                                    start=True, stop=True).then_inc(ste.h, 1))
    ste.n += 1
    W("vector", ste, ste.n)
    on("vector", lambda e: e.tensor_copy(out=GBB[:, :], in_=PB0[:, :]).then_inc(sve.h, 1))
    sve.n += 1
    S0p.close()

    with ExitStack() as S1:
        e1 = S1.enter_context
        WQ = e1(nc.sbuf_tensor("WQ", [128, NCH * CG], BF16))
        WK = e1(nc.sbuf_tensor("WK", [128, NCH * CG], BF16))
        WV = e1(nc.sbuf_tensor("WV", [128, NCH * CG], BF16))
        XBF = [e1(nc.sbuf_tensor(f"XBF{i}", [128, NCH * TS], BF16)) for i in range(2)]
        XSQ = e1(nc.sbuf_tensor("XSQ", [128, NCH * TS], BF16))
        QST = e1(nc.sbuf_tensor("QST", [128, HG * TS], BF16))
        KST = e1(nc.sbuf_tensor("KST", [128, HG * TS], BF16))
        VST = e1(nc.sbuf_tensor("VST", [128, 4 * CG], BF16))
        NEGM = [e1(nc.sbuf_tensor(f"NEGM{i}", [1, TS], BF16)) for i in range(2)]
        SQR = e1(nc.sbuf_tensor("SQR", [1, CG], BF16))
        SKR = e1(nc.sbuf_tensor("SKR", [1, CG], BF16))
        SVR = e1(nc.sbuf_tensor("SVR", [1, CG], BF16))
        Qp = [e1(nc.psum_tensor(f"QpA{i}", [128, 512], F32)) for i in range(2)]
        Kp = [e1(nc.psum_tensor(f"KpA{i}", [128, 512], F32)) for i in range(2)]
        Vp = e1(nc.psum_tensor("VpA", [128, 512], F32))
        Sump = e1(nc.psum_tensor("SumpA", [1, 512], F32))
        Sqp = e1(nc.psum_tensor("SqpA", [1, 512], F32))
        Colp = e1(nc.psum_tensor("ColpA", [128, 4], F32))

        def wload(g):
            g.dma_start(out=WQ[:, :].rearrange("d (k o) -> d k o", k=NCH),
                        in_=wqT.rearrange("(k d) o -> d k o", d=128)).then_inc(sldg.h, 16)
            g.dma_start(out=WK[:, :].rearrange("d (k o) -> d k o", k=NCH),
                        in_=wkT.rearrange("(k d) o -> d k o", d=128)).then_inc(sldg.h, 16)
            g.dma_start(out=WV[:, :].rearrange("d (k o) -> d k o", k=NCH),
                        in_=wvT.rearrange("(k d) o -> d k o", d=128)).then_inc(sldg.h, 16)
        on("gpsimd", wload)
        sldg.n += 3 * 16

        W("tensor", sldg, sldg.n)
        for (Wt, Srow) in ((WQ, SQR), (WK, SKR), (WV, SVR)):
            for sl in range(2):
                def smm(e, Wt=Wt, sl=sl):
                    for dc in range(NCH):
                        ins = e.matmul(Sump[:, :], ONEB[:, :],
                                       Wt[:, dc * CG + sl * 512: dc * CG + (sl + 1) * 512],
                                       start=(dc == 0), stop=(dc == NCH - 1))
                    ins.then_inc(ste.h, 1)
                on("tensor", smm)
                ste.n += 1
                W("vector", ste, ste.n)
                on("vector", lambda e, Srow=Srow, sl=sl: e.tensor_copy(
                    out=Srow[:, sl * 512:(sl + 1) * 512], in_=Sump[:, :]).then_inc(sve.h, 1))
                sve.n += 1
                W("tensor", sve, sve.n)

        xload = [None] * NT
        rows_sc = [None] * NT
        vrows_ve = [None] * NT
        stats_te = [None] * NT
        qkv_te_done = [None] * NT
        store_id = [None] * NT
        evacQ, evacK, evacV = {}, {}, {}

        def issue_xload(j):
            def f(g, j=j):
                g.dma_start(out=XBF[j % 2][:, :].rearrange("d (k n) -> d k n", k=NCH),
                            in_=xT.rearrange("(k d) n -> d k n", d=128)[:, :, j * TS:(j + 1) * TS]
                            ).then_inc(sx[j % 2].h, 16)
            on("gpsimd", f)
            sx[j % 2].n += 16
            xload[j] = sx[j % 2].n

        issue_xload(0)
        issue_xload(1)

        def emit_stats(j):
            W("scalar", sx[j % 2], xload[j])
            if j > 0:
                W("scalar", ste, stats_te[j - 1])
            on("scalar", lambda e, j=j: e.activation(
                out=XSQ[:, :], in_=XBF[j % 2][:, :], func=AF.Square).then_inc(ssc.h, 1))
            ssc.n += 1
            sq_done = ssc.n
            W("tensor", sx[j % 2], xload[j])
            if j > 0:
                W("tensor", sve, vrows_ve[j - 1])
            def summ(e, j=j):
                for dc in range(NCH):
                    ins = e.matmul(Sump[:, :], ONEB[:, :], XBF[j % 2][:, dc * TS:(dc + 1) * TS],
                                   start=(dc == 0), stop=(dc == NCH - 1))
                ins.then_inc(ste.h, 1)
            on("tensor", summ)
            ste.n += 1
            W("tensor", ssc, sq_done)
            def sqm(e):
                for dc in range(NCH):
                    ins = e.matmul(Sqp[:, :], ONEB[:, :], XSQ[:, dc * TS:(dc + 1) * TS],
                                   start=(dc == 0), stop=(dc == NCH - 1))
                ins.then_inc(ste.h, 1)
            on("tensor", sqm)
            ste.n += 1
            stats_te[j] = ste.n
            W("vector", ste, stats_te[j])
            if j > 0:
                W("vector", ssc, rows_sc[j - 1])
            def vrows(e, j=j):
                e.tensor_scalar(out=ROWC[:, :], in0=Sump[:, :],
                                scalar1=1.0 / C, scalar2=None, op0=ALU.mult)
                e.tensor_scalar(out=ROWA[:, :], in0=Sqp[:, :],
                                scalar1=1.0 / C, scalar2=None, op0=ALU.mult)
                e.tensor_tensor(out=ROWB[:, :], in0=ROWC[:, :], in1=ROWC[:, :], op=ALU.mult)
                e.tensor_tensor(out=ROWA[:, :], in0=ROWA[:, :], in1=ROWB[:, :], op=ALU.subtract)
                e.tensor_scalar(out=NEGM[j % 2][:, :], in0=ROWC[:, :],
                                scalar1=-1.0, scalar2=None, op0=ALU.mult).then_inc(sve.h, 1)
            on("vector", vrows)
            sve.n += 1
            vrows_ve[j] = sve.n
            W("scalar", sve, sve.n)
            def srows(e, j=j):
                e.activation(out=ROWB[:, :], in_=ROWA[:, :], func=AF.Ln, bias=EPSC[:, 0:1])
                e.activation(out=RSTD[:, j * TS:(j + 1) * TS], in_=ROWB[:, :],
                             func=AF.Exp, scale=-0.5).then_inc(ssc.h, 1)
            on("scalar", srows)
            ssc.n += 1
            rows_sc[j] = ssc.n

        def emit_qkv(j):
            W("tensor", ssc, rows_sc[j])
            def colmm(e, j=j):
                for tc in range(4):
                    ins = e.matmul(Colp[:, tc:tc + 1],
                                   RSTD[:, j * TS + tc * 128: j * TS + (tc + 1) * 128],
                                   ONE1F[:, :], start=True, stop=True)
                ins.then_inc(ste.h, 1)
            on("tensor", colmm)
            ste.n += 1
            W("vector", ste, ste.n)
            on("vector", lambda e, j=j: e.tensor_copy(
                out=RSTDC[:, j * 4:(j + 1) * 4], in_=Colp[:, :]).then_inc(sve.h, 1))
            sve.n += 1
            col_ve = sve.n
            if j > 0:
                W("scalar", sst, store_id[j - 1])
            for (Wt, Pp, DST, Srow, ed) in ((WQ, Qp, QST, SQR, evacQ), (WK, Kp, KST, SKR, evacK)):
                for oc in range(HG):
                    if (j, oc - 2) in ed:
                        W("tensor", ssc, ed[(j, oc - 2)])
                    def qmm(e, j=j, oc=oc, Wt=Wt, Pp=Pp, Srow=Srow):
                        bank = Pp[oc % 2]
                        for dc in range(NCH):
                            e.matmul(bank[:, :],
                                     Wt[:, dc * CG + oc * 128: dc * CG + (oc + 1) * 128],
                                     XBF[j % 2][:, dc * TS:(dc + 1) * TS],
                                     start=(dc == 0), stop=False)
                        e.matmul(bank[:, :], Srow[:, oc * 128:(oc + 1) * 128],
                                 NEGM[j % 2][:, :],
                                 start=False, stop=True).then_inc(ste.h, 1)
                    on("tensor", qmm)
                    ste.n += 1
                    W("scalar", ste, ste.n)
                    on("scalar", lambda e, oc=oc, Pp=Pp, DST=DST: e.copy(
                        out=DST[:, oc * TS:(oc + 1) * TS],
                        in_=Pp[oc % 2][:, :]).then_inc(ssc.h, 1))
                    ssc.n += 1
                    ed[(j, oc)] = ssc.n
            for tc in range(4):
                for os_ in range(2):
                    if (j, tc, os_) != (j, 0, 0) and (j, (tc, os_)) not in ():
                        pass
                    key_prev = (tc, os_ - 1) if os_ > 0 else (tc - 1, 1)
                    if key_prev[0] >= 0 and (j,) + key_prev in evacV:
                        W("tensor", ssc, evacV[(j,) + key_prev])
                    def vmm(e, j=j, tc=tc, os_=os_):
                        for dc in range(NCH):
                            e.matmul(Vp[:, :],
                                     XBF[j % 2][:, dc * TS + tc * 128: dc * TS + (tc + 1) * 128],
                                     WV[:, dc * CG + os_ * 512: dc * CG + (os_ + 1) * 512],
                                     start=(dc == 0), stop=False)
                        e.matmul(Vp[:, :],
                                 NEGM[j % 2][:, tc * 128:(tc + 1) * 128],
                                 SVR[:, os_ * 512:(os_ + 1) * 512],
                                 start=False, stop=True).then_inc(ste.h, 1)
                    on("tensor", vmm)
                    ste.n += 1
                    W("scalar", ste, ste.n)
                    W("scalar", sve, col_ve)
                    on("scalar", lambda e, j=j, tc=tc, os_=os_: e.activation(
                        out=VST[:, tc * CG + os_ * 512: tc * CG + (os_ + 1) * 512],
                        in_=Vp[:, :], func=AF.Copy,
                        scale=RSTDC[:, j * 4 + tc: j * 4 + tc + 1]).then_inc(ssc.h, 1))
                    ssc.n += 1
                    evacV[(j, tc, os_)] = ssc.n
            qkv_te_done[j] = ste.n
            W("sync", ssc, ssc.n)
            def store(sy, j=j):
                sy.dma_start(out=qbuf.rearrange("(h d) n -> d h n", d=128)[:, :, j * TS:(j + 1) * TS],
                             in_=QST[:, :].rearrange("d (h n) -> d h n", h=HG)).then_inc(sst.h, 16)
                sy.dma_start(out=kbuf.rearrange("(h d) n -> d h n", d=128)[:, :, j * TS:(j + 1) * TS],
                             in_=KST[:, :].rearrange("d (h n) -> d h n", h=HG)).then_inc(sst.h, 16)
                sy.dma_start(out=vbuf[:, j * 4 * CG:(j + 1) * 4 * CG],
                             in_=VST[:, :]).then_inc(sst.h, 16)
            on("sync", store)
            sst.n += 3 * 16
            store_id[j] = sst.n
            if j + 2 < NT:
                W("gpsimd", ste, qkv_te_done[j])
                issue_xload(j + 2)

        for j in range(NT):
            emit_stats(j)
            if j > 0:
                emit_qkv(j - 1)
        emit_qkv(NT - 1)
        W("sync", sst, sst.n)
    barrier()

    with ExitStack() as S2:
        e2 = S2.enter_context
        KH = e2(nc.sbuf_tensor("KH", [128, HG * T], BF16))
        VS = e2(nc.sbuf_tensor("VS", [128, 32 * CG], BF16))
        CB = e2(nc.sbuf_tensor("CB", [128, 512], F32))
        SB = e2(nc.sbuf_tensor("SB", [128, 512], F32))
        CS = e2(nc.sbuf_tensor("CS", [128, 512], F32))
        SS = e2(nc.sbuf_tensor("SS", [128, 512], F32))
        T1 = e2(nc.sbuf_tensor("T1", [128, 512], BF16))
        T2 = e2(nc.sbuf_tensor("T2", [128, 512], BF16))
        Rbp = e2(nc.psum_tensor("RbpB", [128, 512], F32))
        S2k = ExitStack()
        KT = S2k.enter_context(nc.sbuf_tensor("KT", [128, HG * TS], BF16))
        KW = S2k.enter_context(nc.sbuf_tensor("KW", [128, HG * TS], BF16))

        on("sync", lambda sy: sy.dma_start(out=VS[:, :], in_=vbuf[:, :]).then_inc(slds.h, 16))
        slds.n += 16
        vs_ld = slds.n

        for j in range(NT):
            cl = j * TS
            def ldk(sy, cl=cl):
                sy.dma_start(out=CS[:, :], in_=cosT[:, cl:cl + TS]).then_inc(slds.h, 16)
                sy.dma_start(out=SS[:, :], in_=sinT[:, cl:cl + TS]).then_inc(slds.h, 16)
                sy.dma_start(out=KT[:, :].rearrange("d (h n) -> d h n", h=HG),
                             in_=kbuf.rearrange("(h d) n -> d h n", d=128)[:, :, cl:cl + TS]
                             ).then_inc(slds.h, 16)
                kb4 = kbuf.rearrange("(h s dd) n -> s dd h n", h=HG, s=2)
                sy.dma_start(out=KW[0:64, :].rearrange("d (h n) -> d h n", h=HG),
                             in_=kb4[1, :, :, cl:cl + TS]).then_inc(slds.h, 16)
                sy.dma_start(out=KW[64:128, :].rearrange("d (h n) -> d h n", h=HG),
                             in_=kb4[0, :, :, cl:cl + TS]).then_inc(slds.h, 16)
            on("sync", ldk)
            slds.n += 5 * 16
            W("tensor", slds, slds.n)
            on("tensor", lambda e, j=j: e.matmul(
                Rbp[:, :], ONERF[:, :], RSTD[:, j * TS:(j + 1) * TS],
                start=True, stop=True).then_inc(ste.h, 1))
            ste.n += 1
            W("vector", ste, ste.n)
            def vrope(e, cl=cl):
                e.tensor_tensor(out=CB[:, :], in0=CS[:, :], in1=Rbp[:, :], op=ALU.mult)
                e.tensor_tensor(out=SB[:, :], in0=SS[:, :], in1=Rbp[:, :], op=ALU.mult)
                for h in range(HG):
                    e.tensor_tensor(out=T1[:, :], in0=KT[:, h * TS:(h + 1) * TS],
                                    in1=CB[:, :], op=ALU.mult)
                    e.tensor_tensor(out=T2[:, :], in0=KW[:, h * TS:(h + 1) * TS],
                                    in1=SB[:, :], op=ALU.mult)
                    ins = e.tensor_tensor(out=KH[:, h * T + cl: h * T + cl + TS],
                                          in0=T1[:, :], in1=T2[:, :], op=ALU.add)
                ins.then_inc(sve.h, 1)
            on("vector", vrope)
            sve.n += 1
            W("sync", sve, sve.n)
            W("tensor", sve, sve.n)

        S2k.close()
        QT = e2(nc.sbuf_tensor("QT", [128, HG * TS], BF16))
        QW = e2(nc.sbuf_tensor("QW", [128, HG * TS], BF16))
        QH = e2(nc.sbuf_tensor("QH", [128, HG * TS], BF16))
        PT = [e2(nc.sbuf_tensor(f"PT{i}", [128, 512], BF16)) for i in range(4)]
        IDR = [e2(nc.sbuf_tensor(f"IDR{i}", [1, 512], F32)) for i in range(2)]
        CTU = e2(nc.sbuf_tensor("CTU", [128, HG * TS], BF16))
        SC = [e2(nc.psum_tensor(f"ScpC{i}", [128, 512], F32)) for i in range(3)]
        Ctp = [e2(nc.psum_tensor(f"CtpC{i}", [128, 512], F32)) for i in range(2)]
        Dnp = [e2(nc.psum_tensor(f"DnpC{i}", [1, 512], F32)) for i in range(2)]

        W("tensor", slds, vs_ld)

        exp_hist = []
        smm_hist = []
        cd_hist = []
        smm_last = {}
        ldq_id = {}
        qrope_ve = {}
        recip_ve = {}
        rb2_te = {}
        vmul_ve = {}
        stctx_id = {}
        ctu_sc = {}
        cd_last = {}
        tail_q = []
        rbb_free = [0]

        def emit_tail_step(cur_gh=None):
            if tail_q and (cur_gh is None or tail_q[0][0] <= cur_gh - 2):
                tail_q.pop(0)[1]()

        def mk_tail(gh):
            def f(gh=gh):
                cch, h = gh // 8, gh % 8
                hb = gh % 2
                W("tensor", sve, recip_ve[gh])
                if rbb_free[0]:
                    W("tensor", sve, rbb_free[0])
                on("tensor", lambda e, hb=hb: e.matmul(
                    Rbp[:, :], ONERF[:, :], IDR[hb][:, :],
                    start=True, stop=True).then_inc(ste.h, 1))
                ste.n += 1
                rb2_te[gh] = ste.n
                W("vector", ste, ste.n)
                W("vector", ssc, ctu_sc[gh])
                on("vector", lambda e, h=h: e.tensor_tensor(
                    out=CTU[:, h * TS:(h + 1) * TS], in0=CTU[:, h * TS:(h + 1) * TS],
                    in1=Rbp[:, :], op=ALU.mult).then_inc(sve.h, 1))
                sve.n += 1
                vmul_ve[gh] = sve.n
                rbb_free[0] = sve.n
                W("sync", sve, sve.n)
                def stc(sy, cch=cch, h=h):
                    sy.dma_start(out=ctxin[cch * CG + h * 128: cch * CG + (h + 1) * 128, :],
                                 in_=CTU[:, h * TS:(h + 1) * TS]).then_inc(sst.h, 16)
                on("sync", stc)
                sst.n += 16
                stctx_id[gh] = sst.n
                if h == 7:
                    W("gpsimd", sst, sst.n)
                    on("gpsimd", lambda g, cch=cch: g.collective_compute(
                        "AllGather", ALU.bypass, replica_groups=PAIRS,
                        ins=[ctxin[cch * CG:(cch + 1) * CG, :]],
                        outs=[ctxout[cch * 2 * CG:(cch + 1) * 2 * CG, :]]).then_inc(scc.h, 1))
                    scc.n += 1
            return f

        def emit_ldq(c2):
            cl2 = c2 * TS
            def f(sy, cl=cl2):
                sy.dma_start(out=CS[:, :], in_=cosT[:, cl:cl + TS]).then_inc(sa[0].h, 16)
                sy.dma_start(out=SS[:, :], in_=sinT[:, cl:cl + TS]).then_inc(sa[0].h, 16)
                sy.dma_start(out=QT[:, :].rearrange("d (h n) -> d h n", h=HG),
                             in_=qbuf.rearrange("(h d) n -> d h n", d=128)[:, :, cl:cl + TS]
                             ).then_inc(sa[0].h, 16)
                qb4 = qbuf.rearrange("(h s dd) n -> s dd h n", h=HG, s=2)
                sy.dma_start(out=QW[0:64, :].rearrange("d (h n) -> d h n", h=HG),
                             in_=qb4[1, :, :, cl:cl + TS]).then_inc(sa[0].h, 16)
                sy.dma_start(out=QW[64:128, :].rearrange("d (h n) -> d h n", h=HG),
                             in_=qb4[0, :, :, cl:cl + TS]).then_inc(sa[0].h, 16)
            on("sync", f)
            sa[0].n += 5 * 16
            ldq_id[c2] = sa[0].n

        emit_ldq(0)
        for cch in range(8):
            if rbb_free[0]:
                W("tensor", sve, rbb_free[0])
            on("tensor", lambda e, cl=cch * TS: e.matmul(
                Rbp[:, :], ONERF[:, :], RSTD[:, cl:cl + TS],
                start=True, stop=True).then_inc(ste.h, 1))
            ste.n += 1
            rbq_te = ste.n
            W("vector", sa[0], ldq_id[cch])
            W("vector", ste, rbq_te)
            if cch >= 1:
                W("vector", ste, smm_last[cch - 1])
            def qrope(e):
                e.tensor_tensor(out=CB[:, :], in0=CS[:, :], in1=Rbp[:, :], op=ALU.mult)
                e.tensor_tensor(out=SB[:, :], in0=SS[:, :], in1=Rbp[:, :], op=ALU.mult)
                for h in range(HG):
                    e.tensor_tensor(out=T1[:, :], in0=QT[:, h * TS:(h + 1) * TS],
                                    in1=CB[:, :], op=ALU.mult)
                    e.tensor_tensor(out=T2[:, :], in0=QW[:, h * TS:(h + 1) * TS],
                                    in1=SB[:, :], op=ALU.mult)
                    ins = e.tensor_tensor(out=QH[:, h * TS:(h + 1) * TS],
                                          in0=T1[:, :], in1=T2[:, :], op=ALU.add)
                ins.then_inc(sve.h, 1)
            on("vector", qrope)
            sve.n += 1
            qrope_ve[cch] = sve.n
            rbb_free[0] = sve.n
            if cch + 1 < 8:
                W("sync", sve, qrope_ve[cch])
                emit_ldq(cch + 1)

            groups = ATTN_PLAN[cch]
            ng = len(groups)
            W("tensor", sve, qrope_ve[cch])
            for h in range(HG):
                gh = cch * 8 + h
                hb = gh % 2
                for gi, g in enumerate(groups):
                    if (gi % 2) == 0:
                        emit_tail_step(gh)
                    cnt = len(smm_hist)
                    bank = SC[cnt % 3]
                    q0, q1 = g["r0"] * 64, (g["r1"] + 1) * 64
                    if len(exp_hist) >= 3:
                        W("tensor", ssc, exp_hist[-3])
                    if g["kind"] == "pair":
                        def smm(e, h=h, j2=g["j"], q0=q0, q1=q1, bank=bank):
                            e.matmul(bank[:, q0:q1],
                                     KH[:, h * T + j2 * 128: h * T + (j2 + 1) * 128],
                                     QH[:, h * TS + q0: h * TS + q1],
                                     start=True, stop=True).then_inc(ste.h, 1)
                    else:
                        def smm(e, h=h, j2=g["j"], sd=g["side"], q0=q0, q1=q1, bank=bank):
                            e.matmul(bank[sd * 64:sd * 64 + 64, q0:q1],
                                     KH[:, h * T + j2 * 128 + sd * 64: h * T + j2 * 128 + sd * 64 + 64],
                                     QH[:, h * TS + q0: h * TS + q1],
                                     start=True, stop=True).then_inc(ste.h, 1)
                    on("tensor", smm)
                    ste.n += 1
                    smm_hist.append(ste.n)
                    W("scalar", ste, ste.n)
                    if len(cd_hist) >= 4:
                        W("scalar", ste, cd_hist[-4])
                    pt = PT[cnt % 4]
                    if g["kind"] == "pair":
                        def sexp(e, g=g, bank=bank, pt=pt):
                            for (s0, s1, bi) in g["segs"]:
                                a0, a1 = s0 * 64, (s1 + 1) * 64
                                if bi == 0:
                                    ins = e.activation(out=pt[:, a0:a1], in_=bank[:, a0:a1],
                                                       func=AF.Exp, scale=SCALE)
                                else:
                                    ins = e.activation(out=pt[:, a0:a1], in_=bank[:, a0:a1],
                                                       func=AF.Exp, scale=SCALE,
                                                       bias=MASKC[:, bi:bi + 1])
                            ins.then_inc(ssc.h, 1)
                    else:
                        def sexp(e, g=g, bank=bank, pt=pt, q0=q0, q1=q1):
                            sd = g["side"]
                            e.activation(out=pt[sd * 64:sd * 64 + 64, q0:q1],
                                         in_=bank[sd * 64:sd * 64 + 64, q0:q1],
                                         func=AF.Exp, scale=SCALE).then_inc(ssc.h, 1)
                    on("scalar", sexp)
                    ssc.n += 1
                    exp_hist.append(ssc.n)
                    W("tensor", ssc, ssc.n)
                    first, last = (gi == 0), (gi == ng - 1)
                    if first:
                        if (gh - 2) in ctu_sc:
                            W("tensor", ssc, ctu_sc[gh - 2])
                        if (gh - 2) in recip_ve:
                            W("tensor", sve, recip_ve[gh - 2])
                    if g["kind"] == "pair":
                        def cmm(e, j2=g["j"], h=h, q0=q0, q1=q1, pt=pt, first=first,
                                last=last, hb=hb):
                            e.matmul(Ctp[hb][:, q0:q1],
                                     VS[:, j2 * CG + h * 128: j2 * CG + (h + 1) * 128],
                                     pt[:, q0:q1], start=first, stop=last)
                            e.matmul(Dnp[hb][:, q0:q1], ONEB[:, :], pt[:, q0:q1],
                                     start=first, stop=last).then_inc(ste.h, 1)
                    else:
                        def cmm(e, j2=g["j"], sd=g["side"], h=h, q0=q0, q1=q1, pt=pt,
                                first=first, last=last, hb=hb):
                            e.matmul(Ctp[hb][:, q0:q1],
                                     VS[sd * 64:sd * 64 + 64, j2 * CG + h * 128: j2 * CG + (h + 1) * 128],
                                     pt[sd * 64:sd * 64 + 64, q0:q1], start=first, stop=last)
                            e.matmul(Dnp[hb][:, q0:q1], ONEB[sd * 64:sd * 64 + 64, :],
                                     pt[sd * 64:sd * 64 + 64, q0:q1],
                                     start=first, stop=last).then_inc(ste.h, 1)
                    on("tensor", cmm)
                    ste.n += 1
                    cd_hist.append(ste.n)
                cd_last[gh] = cd_hist[-1]
                W("scalar", ste, cd_last[gh])
                if (gh - 8) in stctx_id:
                    W("scalar", sst, stctx_id[gh - 8])
                on("scalar", lambda e, h=h, hb=hb: e.copy(
                    out=CTU[:, h * TS:(h + 1) * TS], in_=Ctp[hb][:, :]).then_inc(ssc.h, 1))
                ssc.n += 1
                ctu_sc[gh] = ssc.n
                W("vector", ste, cd_last[gh])
                if (gh - 2) in rb2_te:
                    W("vector", ste, rb2_te[gh - 2])
                on("vector", lambda e, hb=hb: e.reciprocal(
                    out=IDR[hb][:, :], in_=Dnp[hb][:, :]).then_inc(sve.h, 1))
                sve.n += 1
                recip_ve[gh] = sve.n
                tail_q.append((gh, mk_tail(gh)))
            smm_last[cch] = smm_hist[-1]
        while tail_q:
            emit_tail_step()
    barrier()

    CORR2 = E(nc.sbuf_tensor("CORR2", [1, OWN], F32))
    with ExitStack() as S5:
        e5 = S5.enter_context
        WO = e5(nc.sbuf_tensor("WO", [128, NCH * C], BF16))
        CTF = [e5(nc.sbuf_tensor(f"CTF{i}", [128, NCH * TS], BF16)) for i in range(2)]
        XO = e5(nc.sbuf_tensor("XO", [128, NCH * TS], F32))
        YF = e5(nc.sbuf_tensor("YF", [128, NCH * TS], F32))
        YSQ = [e5(nc.sbuf_tensor(f"YSQ{i}", [128, 512], BF16)) for i in range(2)]
        CSO = e5(nc.sbuf_tensor("CSO", [128, NCH], BF16))
        CSOF = e5(nc.sbuf_tensor("CSOF", [128, NCH], F32))
        RB5 = e5(nc.sbuf_tensor("RB5", [128, 512], F32))
        Yp = [e5(nc.psum_tensor(f"YpD{i}", [128, 512], F32)) for i in range(2)]
        Mnp = e5(nc.psum_tensor("MnpD", [1, 512], F32))
        Sqp5 = e5(nc.psum_tensor("Sqp5D", [1, 512], F32))
        Rb5p = e5(nc.psum_tensor("Rb5D", [128, 512], F32))

        evac5 = {}
        on("gpsimd", lambda g: g.dma_start(
            out=WO[:, :].rearrange("d (k o) -> d k o", k=NCH),
            in_=woT.rearrange("(k d) o -> d k o", d=128)).then_inc(sldg.h, 16))
        sldg.n += 16
        W("vector", sldg, sldg.n)
        def vcso(e):
            for cc in range(NCH):
                ins = e.reduce_sum(out=CSOF[:, cc:cc + 1], in_=WO[:, cc * C:(cc + 1) * C],
                                   axis=mybir.AxisListType.X)
            ins.then_inc(sve.h, 1)
        on("vector", vcso)
        sve.n += 1
        W("vector", sve, sve.n)
        on("vector", lambda e: e.tensor_copy(out=CSO[:, :], in_=CSOF[:, :]).then_inc(sve.h, 1))
        sve.n += 1
        cso_ve = sve.n

        W("sync", scc, scc.n)
        if debug:
            W("gpsimd", scc, scc.n)
            on("gpsimd", lambda g: g.dma_start(out=ctxout_dbg, in_=ctxout[:, :]).then_inc(sst.h, 16))
            sst.n += 16

        ctf_ld = [None] * NOT_
        xo_ld = [None] * NOT_
        ymm_done = [None] * NOT_
        fin_ve = [None] * NOT_

        def ld5_ctf(i):
            if i >= 2:
                W("sync", ste, ymm_done[i - 2])
            def f(sy, i=i):
                pid = sy.partition_id()
                with sy.register(f"rp5{i}") as rp:
                    sy.reg_mov(rp, pid)
                    sy.reg_mod(rp, rp, 2)
                    with sy.If_eq(rp, 0):
                        ch = i
                        sy.dma_start(out=CTF[i % 2][:, :].rearrange("d (k n) -> d k n", k=NCH),
                                     in_=ctxout[ch * 2 * CG:(ch + 1) * 2 * CG, :]
                                     .rearrange("(k d) n -> d k n", d=128)).then_inc(sa[1].h, 16)
                    with sy.Else():
                        ch = 4 + i
                        sy.dma_start(out=CTF[i % 2][:, :].rearrange("d (k n) -> d k n", k=NCH),
                                     in_=ctxout[ch * 2 * CG:(ch + 1) * 2 * CG, :]
                                     .rearrange("(k d) n -> d k n", d=128)).then_inc(sa[1].h, 16)
            on("sync", f)
            sa[1].n += 16
            ctf_ld[i] = sa[1].n

        def ld5_xo(i):
            if i >= 1:
                W("sync", sve, fin_ve[i - 1])
            def f(sy, i=i):
                sy.dma_start(out=XO[:, :].rearrange("d (k n) -> d k n", k=NCH),
                             in_=xoT.rearrange("(k d) n -> d k n", d=128)[:, :, i * TS:(i + 1) * TS]
                             ).then_inc(sb[0].h, 16)
            on("sync", f)
            sb[0].n += 16
            xo_ld[i] = sb[0].n

        ld5_ctf(0)
        ld5_xo(0)
        ld5_ctf(1)
        W("tensor", sldg, sldg.n)

        for i in range(NOT_):
            W("tensor", sa[1], ctf_ld[i])
            W("tensor", sve, cso_ve)
            if i > 0:
                W("tensor", sve, fin_ve[i - 1])
            def mmm(e, i=i):
                for cc in range(NCH):
                    ins = e.matmul(Mnp[:, :], CSO[:, cc:cc + 1],
                                   CTF[i % 2][:, cc * TS:(cc + 1) * TS],
                                   start=(cc == 0), stop=(cc == NCH - 1))
                ins.then_inc(ste.h, 1)
            on("tensor", mmm)
            ste.n += 1
            sq_te = {}
            for oc in range(NCH):
                if (i, oc - 2) in evac5:
                    W("tensor", ssc, evac5[(i, oc - 2)])
                def ymm(e, i=i, oc=oc):
                    bank = Yp[oc % 2]
                    for cc in range(NCH):
                        ins = e.matmul(bank[:, :],
                                       WO[:, cc * C + oc * 128: cc * C + (oc + 1) * 128],
                                       CTF[i % 2][:, cc * TS:(cc + 1) * TS],
                                       start=(cc == 0), stop=(cc == NCH - 1))
                    ins.then_inc(ste.h, 1)
                on("tensor", ymm)
                ste.n += 1
                W("scalar", ste, ste.n)
                if oc >= 2:
                    W("scalar", ste, sq_te[oc - 2])
                def yev(e, i=i, oc=oc):
                    bank = Yp[oc % 2]
                    e.activation(out=YSQ[oc % 2][:, :], in_=bank[:, :], func=AF.Square)
                    e.copy(out=YF[:, oc * TS:(oc + 1) * TS], in_=bank[:, :]).then_inc(ssc.h, 1)
                on("scalar", yev)
                ssc.n += 1
                evac5[(i, oc)] = ssc.n
                if oc >= 1:
                    W("tensor", ssc, evac5[(i, oc - 1)])
                    def sqm5(e, oc2=oc - 1):
                        e.matmul(Sqp5[:, :], ONEB[:, :], YSQ[oc2 % 2][:, :],
                                 start=(oc2 == 0), stop=False).then_inc(ste.h, 1)
                    on("tensor", sqm5)
                    ste.n += 1
                    sq_te[oc - 1] = ste.n
            W("tensor", ssc, evac5[(i, NCH - 1)])
            def sqm5f(e):
                e.matmul(Sqp5[:, :], ONEB[:, :], YSQ[(NCH - 1) % 2][:, :],
                         start=False, stop=True).then_inc(ste.h, 1)
            on("tensor", sqm5f)
            ste.n += 1
            sq_te[NCH - 1] = ste.n
            ymm_done[i] = ste.n
            W("vector", ste, ste.n)
            def v5rows(e, i=i):
                e.tensor_scalar(out=ROWA[:, :], in0=Mnp[:, :], scalar1=1.0 / C,
                                scalar2=None, op0=ALU.mult)
                e.tensor_scalar(out=ROWB[:, :], in0=Sqp5[:, :], scalar1=1.0 / C,
                                scalar2=None, op0=ALU.mult)
                e.tensor_tensor(out=ROWC[:, :], in0=ROWA[:, :], in1=ROWA[:, :], op=ALU.mult)
                e.tensor_tensor(out=ROWB[:, :], in0=ROWB[:, :], in1=ROWC[:, :],
                                op=ALU.subtract).then_inc(sve.h, 1)
            on("vector", v5rows)
            sve.n += 1
            W("scalar", sve, sve.n)
            def s5rows(e):
                e.activation(out=ROWB[:, :], in_=ROWB[:, :], func=AF.Ln, bias=EPSC[:, 0:1])
                e.activation(out=ROWC[:, :], in_=ROWB[:, :], func=AF.Exp,
                             scale=-0.5).then_inc(ssc.h, 1)
            on("scalar", s5rows)
            ssc.n += 1
            W("vector", ssc, ssc.n)
            def v5corr(e, i=i):
                e.scalar_tensor_tensor(out=CORR2[:, i * TS:(i + 1) * TS], in0=ROWA[:, :],
                                       scalar=-1.0, in1=ROWC[:, :],
                                       op0=ALU.mult, op1=ALU.mult).then_inc(sve.h, 1)
            on("vector", v5corr)
            sve.n += 1
            W("tensor", sve, sve.n)
            on("tensor", lambda e: e.matmul(Rb5p[:, :], ONERF[:, :], ROWC[:, :],
                                            start=True, stop=True).then_inc(ste.h, 1))
            ste.n += 1
            W("vector", ste, ste.n)
            W("vector", sb[0], xo_ld[i])
            def v5fin(e, i=i):
                e.tensor_copy(out=RB5[:, :], in_=Rb5p[:, :])
                for cc in range(NCH):
                    e.tensor_tensor(out=YF[:, cc * TS:(cc + 1) * TS],
                                    in0=YF[:, cc * TS:(cc + 1) * TS], in1=RB5[:, :], op=ALU.mult)
                for cc in range(NCH):
                    ins = e.tensor_tensor(out=YF[:, cc * TS:(cc + 1) * TS],
                                          in0=YF[:, cc * TS:(cc + 1) * TS],
                                          in1=XO[:, cc * TS:(cc + 1) * TS], op=ALU.add)
                ins.then_inc(sve.h, 1)
            on("vector", v5fin)
            sve.n += 1
            fin_ve[i] = sve.n
            W("sync", sve, sve.n)
            def st5(sy, i=i):
                sy.dma_start(out=x2buf.rearrange("d (k n) -> d k n", k=NCH)[:, :, i * TS:(i + 1) * TS],
                             in_=YF[:, :].rearrange("d (k n) -> d k n", k=NCH)).then_inc(sst.h, 16)
            on("sync", st5)
            sst.n += 16
            if i + 1 < NOT_:
                ld5_xo(i + 1)
                if i + 2 < NOT_:
                    ld5_ctf(i + 2)
                W("scalar", sve, fin_ve[i])
                W("scalar", sst, sst.n)
        W("sync", sst, sst.n)
    barrier()

    with ExitStack() as S6:
        e6 = S6.enter_context
        WTS = e6(nc.sbuf_tensor("WTS", [128, NCH * C], BF16))
        AST = e6(nc.sbuf_tensor("AST", [128, NCH * 128], BF16))
        BST = e6(nc.sbuf_tensor("BST", [128, C], BF16))
        GWF = e6(nc.sbuf_tensor("GWF", [128, NCH * 8], F32))
        SGR = e6(nc.sbuf_tensor("SGR", [1, 8], F32))
        NM3M = e6(nc.sbuf_tensor("NM3M", [1, 512], F32))
        R3C = e6(nc.sbuf_tensor("R3C", [128, 4], F32))
        BR = e6(nc.sbuf_tensor("BR", [1, C], BF16))
        X2T = e6(nc.sbuf_tensor("X2T", [128, NCH * TS], F32))
        XH2 = e6(nc.sbuf_tensor("XH2", [128, NCH * TS], BF16))
        Y4F = e6(nc.sbuf_tensor("Y4F", [128, NCH * TS], F32))
        Y4SQ = [e6(nc.sbuf_tensor(f"Y4SQ{i}", [128, 512], BF16)) for i in range(2)]
        CSW = e6(nc.sbuf_tensor("CSW", [128, NCH], BF16))
        CSWF = e6(nc.sbuf_tensor("CSWF", [128, NCH], F32))
        CSB = e6(nc.sbuf_tensor("CSB", [128, 1], BF16))
        CSBF = e6(nc.sbuf_tensor("CSBF", [128, 1], F32))
        SW = e6(nc.sbuf_tensor("SW", [1, C], BF16))
        SA = e6(nc.sbuf_tensor("SA", [1, 128], BF16))
        SBR = e6(nc.sbuf_tensor("SBR", [1, 1], F32))
        NM3 = e6(nc.sbuf_tensor("NM3", [1, 512], BF16))
        RB3 = e6(nc.sbuf_tensor("RB3", [128, 512], F32))
        RB6 = e6(nc.sbuf_tensor("RB6", [128, 512], F32))
        CBS = RB3
        LG = e6(nc.sbuf_tensor("LG", [128, 32], F32))
        M1 = e6(nc.sbuf_tensor("M1", [128, 4], F32))
        M2 = e6(nc.sbuf_tensor("M2", [128, 4], F32))
        IS1 = e6(nc.sbuf_tensor("IS1", [128, 32], F32))
        IS2 = e6(nc.sbuf_tensor("IS2", [128, 32], F32))
        G1 = e6(nc.sbuf_tensor("G1", [128, 4], F32))
        G2 = e6(nc.sbuf_tensor("G2", [128, 4], F32))
        GT = e6(nc.sbuf_tensor("GT", [128, 4], F32))
        WSEL = e6(nc.sbuf_tensor("WSEL", [128, 32], F32))
        W8 = e6(nc.sbuf_tensor("W8", [8, 512], F32))
        XAS = e6(nc.sbuf_tensor("XAS", [128, 512], BF16))
        XAB = e6(nc.sbuf_tensor("XAB", [128, 512], BF16))
        DG = e6(nc.sbuf_tensor("DG", [128, 64], F32)) if debug else None
        Y4p = [e6(nc.psum_tensor(f"Y4pE{i}", [128, 512], F32)) for i in range(2)]
        Sum6 = e6(nc.psum_tensor("Sum6E", [1, 512], F32))
        Sq6 = e6(nc.psum_tensor("Sq6E", [1, 512], F32))
        XAp = e6(nc.psum_tensor("XApE", [128, 512], F32))
        Gp = e6(nc.psum_tensor("GpE", [128, 32], F32))
        COL6 = e6(nc.psum_tensor("COL6E", [128, 8], F32))
        AUXp = e6(nc.psum_tensor("AUXpE", [128, 512], F32))

        evac6 = {}

        def wld6(g):
            g.dma_start(out=WTS[:, :].rearrange("d (k o) -> d k o", k=NCH),
                        in_=WT.rearrange("(k d) o -> d k o", d=128)).then_inc(sldg.h, 16)
            g.dma_start(out=AST[:, :].rearrange("d (k o) -> d k o", k=NCH),
                        in_=AstkT.rearrange("(k d) o -> d k o", d=128)).then_inc(sldg.h, 16)
            g.dma_start(out=BST[:, :], in_=BstkT).then_inc(sldg.h, 16)
            g.dma_start(out=BR[:, :], in_=brow).then_inc(sldg.h, 16)
        on("gpsimd", wld6)
        sldg.n += 4 * 16
        w6 = sldg.n
        on("sync", lambda sy: sy.dma_start(
            out=GWF[:, :].rearrange("d (k o) -> d k o", k=NCH),
            in_=gwT.rearrange("(k d) o -> d k o", d=128)).then_inc(slds.h, 16))
        slds.n += 16
        gwf_ld = slds.n

        W("vector", sldg, w6)
        def v6pre(e):
            for dc in range(NCH):
                e.reduce_sum(out=CSWF[:, dc:dc + 1], in_=WTS[:, dc * C:(dc + 1) * C],
                             axis=mybir.AxisListType.X)
            e.reduce_sum(out=CSBF[:, :], in_=BST[:, :], axis=mybir.AxisListType.X)
            e.reduce_sum(out=SBR[:, :], in_=BR[:, :],
                         axis=mybir.AxisListType.X).then_inc(sve.h, 1)
        on("vector", v6pre)
        sve.n += 1
        W("vector", sve, sve.n)
        def v6preb(e):
            e.tensor_copy(out=CSW[:, :], in_=CSWF[:, :])
            e.tensor_copy(out=CSB[:, :], in_=CSBF[:, :]).then_inc(sve.h, 1)
        on("vector", v6preb)
        sve.n += 1
        W("tensor", sldg, w6)
        for sl in range(4):
            def swmm(e, sl=sl):
                for dc in range(NCH):
                    ins = e.matmul(Sum6[:, :], ONEB[:, :],
                                   WTS[:, dc * C + sl * 512: dc * C + (sl + 1) * 512],
                                   start=(dc == 0), stop=(dc == NCH - 1))
                ins.then_inc(ste.h, 1)
            on("tensor", swmm)
            ste.n += 1
            W("vector", ste, ste.n)
            on("vector", lambda e, sl=sl: e.tensor_copy(
                out=SW[:, sl * 512:(sl + 1) * 512], in_=Sum6[:, :]).then_inc(sve.h, 1))
            sve.n += 1
            W("tensor", sve, sve.n)
        W("tensor", slds, gwf_ld)
        def sgmm2(e):
            for dc in range(NCH):
                e.matmul(COL6[0:1, 0:8], ONEF[:, :], GWF[:, dc * 8:(dc + 1) * 8],
                         start=(dc == 0), stop=(dc == NCH - 1))
            for dc in range(NCH):
                ins = e.matmul(Sq6[:, 0:128], ONEB[:, :], AST[:, dc * 128:(dc + 1) * 128],
                               start=(dc == 0), stop=(dc == NCH - 1))
            ins.then_inc(ste.h, 1)
        on("tensor", sgmm2)
        ste.n += 1
        W("vector", ste, ste.n)
        def v6pre2(e):
            e.tensor_copy(out=SGR[:, :], in_=COL6[0:1, 0:8])
            e.tensor_copy(out=SA[:, :], in_=Sq6[:, 0:128]).then_inc(sve.h, 1)
        on("vector", v6pre2)
        sve.n += 1
        pre_ve = sve.n
        W("tensor", sve, pre_ve)

        x2ld = [None] * NOT_
        fin6 = [None] * NOT_

        def ld6(i):
            if i >= 1:
                W("sync", sve, fin6[i - 1])
            def f(sy, i=i):
                sy.dma_start(out=X2T[:, :].rearrange("d (k n) -> d k n", k=NCH),
                             in_=x2buf.rearrange("d (k n) -> d k n", k=NCH)[:, :, i * TS:(i + 1) * TS]
                             ).then_inc(sb[0].h, 16)
            on("sync", f)
            sb[0].n += 16
            x2ld[i] = sb[0].n

        ld6(0)
        for i in range(NOT_):
            W("scalar", sb[0], x2ld[i])
            if i > 0:
                W("scalar", sve, fin6[i - 1])
                W("scalar", sst, sst.n)
            def s6a(e, i=i):
                e.activation(out=XH2[:, :], in_=X2T[:, :],
                             func=AF.Square).then_inc(ssc.h, 1)
            on("scalar", s6a)
            ssc.n += 1
            W("tensor", ssc, ssc.n)
            if i > 0:
                W("tensor", sve, fin6[i - 1])
            def t6stats(e, i=i):
                for dc in range(NCH):
                    e.matmul(Sum6[:, :], ONEF[:, :], X2T[:, dc * TS:(dc + 1) * TS],
                             start=(dc == 0), stop=(dc == NCH - 1))
                for dc in range(NCH):
                    ins = e.matmul(Sq6[:, :], ONEB[:, :], XH2[:, dc * TS:(dc + 1) * TS],
                                   start=(dc == 0), stop=(dc == NCH - 1))
                ins.then_inc(ste.h, 1)
            on("tensor", t6stats)
            ste.n += 1
            W("vector", ste, ste.n)
            def v6rows(e):
                e.tensor_scalar(out=ROWA[:, :], in0=Sum6[:, :], scalar1=1.0 / C,
                                scalar2=None, op0=ALU.mult)
                e.tensor_scalar(out=ROWB[:, :], in0=Sq6[:, :], scalar1=1.0 / C,
                                scalar2=None, op0=ALU.mult)
                e.tensor_scalar(out=NM3M[:, :], in0=ROWA[:, :], scalar1=-1.0,
                                scalar2=None, op0=ALU.mult)
                e.tensor_tensor(out=ROWC[:, :], in0=ROWA[:, :], in1=ROWA[:, :], op=ALU.mult)
                e.tensor_tensor(out=ROWB[:, :], in0=ROWB[:, :], in1=ROWC[:, :],
                                op=ALU.subtract).then_inc(sve.h, 1)
            on("vector", v6rows)
            sve.n += 1
            if debug and i == 0:
                W("sync", sve, sve.n)
                def drows3(sy):
                    sy.dma_start(out=d_rows[:, 0:TS], in_=ROWA[:, :]).then_inc(sst.h, 16)
                    sy.dma_start(out=d_rows[:, TS:2 * TS], in_=ROWB[:, :]).then_inc(sst.h, 16)
                on("sync", drows3)
                sst.n += 32
                W("scalar", sst, sst.n)
            W("scalar", sve, sve.n)
            def s6rows(e):
                e.activation(out=ROWB[:, :], in_=ROWB[:, :], func=AF.Ln, bias=EPSC[:, 0:1])
                e.activation(out=ROWC[:, :], in_=ROWB[:, :], func=AF.Exp,
                             scale=-0.5).then_inc(ssc.h, 1)
            on("scalar", s6rows)
            ssc.n += 1
            W("vector", ssc, ssc.n)
            def v6neg(e):
                e.scalar_tensor_tensor(out=NM3[:, :], in0=ROWA[:, :], scalar=-1.0,
                                       in1=ROWC[:, :], op0=ALU.mult,
                                       op1=ALU.mult).then_inc(sve.h, 1)
            on("vector", v6neg)
            sve.n += 1
            W("tensor", sve, sve.n)
            def t6cols(e):
                for tc in range(4):
                    e.matmul(COL6[:, tc:tc + 1],
                             ROWC[:, tc * 128:(tc + 1) * 128], ONE1F[:, :],
                             start=True, stop=True)
                e.matmul(AUXp[:, :], ONERF[:, :], ROWC[:, :],
                         start=True, stop=True).then_inc(ste.h, 1)
            on("tensor", t6cols)
            ste.n += 1
            W("vector", ste, ste.n)
            def v6xh(e, i=i):
                e.tensor_copy(out=R3C[:, :], in_=COL6[:, 0:4])
                e.tensor_copy(out=RB3[:, :], in_=AUXp[:, :])
                for dc in range(NCH):
                    ins = e.tensor_tensor(out=XH2[:, dc * TS:(dc + 1) * TS],
                                          in0=X2T[:, dc * TS:(dc + 1) * TS],
                                          in1=RB3[:, :], op=ALU.mult)
                ins.then_inc(sve.h, 1)
            on("vector", v6xh)
            sve.n += 1
            xh_ve = sve.n
            if debug and i == 0:
                W("sync", sve, xh_ve)
                on("sync", lambda sy: sy.dma_start(out=d_xh2, in_=XH2[:, :]).then_inc(sst.h, 16))
                sst.n += 16

            W("tensor", sve, xh_ve)
            def t6gate(e, i=i):
                for tc in range(4):
                    for dc in range(NCH):
                        e.matmul(Gp[:, tc * 8:(tc + 1) * 8],
                                 X2T[:, dc * TS + tc * 128: dc * TS + (tc + 1) * 128],
                                 GWF[:, dc * 8:(dc + 1) * 8],
                                 start=(dc == 0), stop=False)
                    e.matmul(Gp[:, tc * 8:(tc + 1) * 8],
                             NM3M[:, tc * 128:(tc + 1) * 128],
                             SGR[:, :], start=False, stop=True)
                for dc in range(NCH):
                    e.matmul(XAp[:, :], AST[:, dc * 128:(dc + 1) * 128],
                             XH2[:, dc * TS:(dc + 1) * TS],
                             start=(dc == 0), stop=False)
                e.matmul(XAp[:, :], SA[:, :], NM3[:, :],
                         start=False, stop=True).then_inc(ste.h, 1)
            on("tensor", t6gate)
            ste.n += 1
            gate_te = ste.n
            W("scalar", ste, gate_te)
            on("scalar", lambda e: e.copy(out=XAS[:, :], in_=XAp[:, :]).then_inc(ssc.h, 1))
            ssc.n += 1
            xas_sc = ssc.n
            def step(fn):
                on("vector", fn)
                sve.n += 1
                W("vector", sve, sve.n)

            W("vector", ste, gate_te)
            def g_s0(e, i=i):
                if debug and i == 0:
                    e.tensor_copy(out=DG[:, 44:48], in_=R3C[:, :])
                for tc in range(4):
                    ins = e.tensor_scalar(out=LG[:, tc * 8:(tc + 1) * 8],
                                          in0=Gp[:, tc * 8:(tc + 1) * 8],
                                          scalar1=R3C[:, tc:tc + 1], scalar2=None,
                                          op0=ALU.mult)
                ins.then_inc(sve.h, 1)
            step(g_s0)
            def g_s1(e, i=i):
                if debug and i == 0:
                    e.tensor_copy(out=DG[:, 0:8], in_=Gp[:, 0:8])
                for tc in range(4):
                    ins = e.tensor_tensor(out=LG[:, tc * 8:(tc + 1) * 8],
                                          in0=LG[:, tc * 8:(tc + 1) * 8],
                                          in1=GBB[:, :], op=ALU.add)
                ins.then_inc(sve.h, 1)
            step(g_s1)
            def g_s2(e, i=i):
                if debug and i == 0:
                    e.tensor_copy(out=DG[:, 8:16], in_=LG[:, 0:8])
                for tc in range(4):
                    ins = e.reduce_max(out=M1[:, tc:tc + 1], in_=LG[:, tc * 8:(tc + 1) * 8],
                                       axis=mybir.AxisListType.X)
                ins.then_inc(sve.h, 1)
            step(g_s2)
            def g_s3(e, i=i):
                if debug and i == 0:
                    e.tensor_copy(out=DG[:, 16:17], in_=M1[:, 0:1])
                for tc in range(4):
                    ins = e.tensor_tensor(out=IS1[:, tc * 8:(tc + 1) * 8],
                                          in0=LG[:, tc * 8:(tc + 1) * 8],
                                          in1=M1[:, tc:tc + 1].broadcast_to([128, 8]),
                                          op=ALU.is_ge)
                ins.then_inc(sve.h, 1)
            step(g_s3)
            def g_s4(e, i=i):
                if debug and i == 0:
                    e.tensor_copy(out=DG[:, 18:26], in_=IS1[:, 0:8])
                for tc in range(4):
                    ins = e.scalar_tensor_tensor(out=LG[:, tc * 8:(tc + 1) * 8],
                                                 in0=IS1[:, tc * 8:(tc + 1) * 8], scalar=NEG,
                                                 in1=LG[:, tc * 8:(tc + 1) * 8],
                                                 op0=ALU.mult, op1=ALU.add)
                ins.then_inc(sve.h, 1)
            step(g_s4)
            def g_s5(e, i=i):
                if debug and i == 0:
                    e.tensor_copy(out=DG[:, 26:34], in_=LG[:, 0:8])
                for tc in range(4):
                    ins = e.reduce_max(out=M2[:, tc:tc + 1], in_=LG[:, tc * 8:(tc + 1) * 8],
                                       axis=mybir.AxisListType.X)
                ins.then_inc(sve.h, 1)
            step(g_s5)
            def g_s6(e, i=i):
                if debug and i == 0:
                    e.tensor_copy(out=DG[:, 17:18], in_=M2[:, 0:1])
                e.tensor_tensor(out=G1[:, :], in0=M1[:, :], in1=M2[:, :],
                                op=ALU.subtract).then_inc(sve.h, 1)
            step(g_s6)
            W("scalar", sve, sve.n)
            on("scalar", lambda e: e.activation(out=GT[:, :], in_=G1[:, :], func=AF.Tanh,
                                                scale=0.5).then_inc(ssc.h, 1))
            ssc.n += 1
            W("vector", ssc, ssc.n)
            step(lambda e: e.tensor_scalar(out=G1[:, :], in0=GT[:, :], scalar1=0.5,
                                           scalar2=0.5, op0=ALU.mult,
                                           op1=ALU.add).then_inc(sve.h, 1))
            step(lambda e: e.tensor_scalar(out=G2[:, :], in0=GT[:, :], scalar1=-0.5,
                                           scalar2=0.5, op0=ALU.mult,
                                           op1=ALU.add).then_inc(sve.h, 1))
            def g_s9(e):
                for tc in range(4):
                    ins = e.tensor_tensor(out=IS2[:, tc * 8:(tc + 1) * 8],
                                          in0=LG[:, tc * 8:(tc + 1) * 8],
                                          in1=M2[:, tc:tc + 1].broadcast_to([128, 8]),
                                          op=ALU.is_ge)
                ins.then_inc(sve.h, 1)
            step(g_s9)
            def g_s10(e):
                for tc in range(4):
                    ins = e.tensor_tensor(out=IS1[:, tc * 8:(tc + 1) * 8],
                                          in0=IS1[:, tc * 8:(tc + 1) * 8],
                                          in1=G1[:, tc:tc + 1].broadcast_to([128, 8]),
                                          op=ALU.mult)
                ins.then_inc(sve.h, 1)
            step(g_s10)
            def g_s11(e):
                for tc in range(4):
                    ins = e.tensor_tensor(out=IS2[:, tc * 8:(tc + 1) * 8],
                                          in0=IS2[:, tc * 8:(tc + 1) * 8],
                                          in1=G2[:, tc:tc + 1].broadcast_to([128, 8]),
                                          op=ALU.mult)
                ins.then_inc(sve.h, 1)
            step(g_s11)
            def g_s12(e, i=i):
                for tc in range(4):
                    ins = e.tensor_tensor(out=WSEL[:, tc * 8:(tc + 1) * 8],
                                          in0=IS1[:, tc * 8:(tc + 1) * 8],
                                          in1=IS2[:, tc * 8:(tc + 1) * 8], op=ALU.add)
                if debug and i == 0:
                    e.tensor_copy(out=DG[:, 34:42], in_=WSEL[:, 0:8])
                    e.tensor_copy(out=DG[:, 42:43], in_=G1[:, 0:1])
                    e.tensor_copy(out=DG[:, 43:44], in_=G2[:, 0:1])
                    e.tensor_copy(out=DG[:, 48:56], in_=IS1[:, 0:8])
                    e.tensor_copy(out=DG[:, 56:64], in_=IS2[:, 0:8])
                ins.then_inc(sve.h, 1)
            step(g_s12)
            W("tensor", sve, sve.n)
            def g_tr(e):
                for tc in range(4):
                    ins = e.transpose(AUXp[0:8, tc * 128:(tc + 1) * 128],
                                      WSEL[:, tc * 8:(tc + 1) * 8], ID128[:, :])
                ins.then_inc(ste.h, 1)
            on("tensor", g_tr)
            ste.n += 1
            W("vector", ste, ste.n)
            def g_w8(e):
                for tc in range(4):
                    ins = e.tensor_copy(out=W8[:, tc * 128:(tc + 1) * 128],
                                        in_=AUXp[0:8, tc * 128:(tc + 1) * 128])
                ins.then_inc(sve.h, 1)
            on("vector", g_w8)
            sve.n += 1
            W("vector", sve, sve.n)
            w8_ve = sve.n
            if debug and i == 0:
                W("sync", sve, w8_ve)
                on("sync", lambda sy: sy.dma_start(out=d_w8, in_=W8[:, :]).then_inc(sst.h, 16))
                on("sync", lambda sy: sy.dma_start(out=d_gate, in_=DG[:, :]).then_inc(sst.h, 16))
                sst.n += 32
            W("tensor", sve, w8_ve)
            on("tensor", lambda e: e.matmul(AUXp[:, :], E8F[:, :], W8[:, :],
                                            start=True, stop=True).then_inc(ste.h, 1))
            ste.n += 1
            W("vector", ste, ste.n)
            W("vector", ssc, xas_sc)
            def v6xab(e):
                e.tensor_tensor(out=XAB[:, :], in0=XAS[:, :], in1=AUXp[:, :],
                                op=ALU.mult).then_inc(sve.h, 1)
            on("vector", v6xab)
            sve.n += 1
            xab_ve = sve.n
            if debug and i == 0:
                W("sync", sve, xab_ve)
                on("sync", lambda sy: sy.dma_start(out=d_xab, in_=XAB[:, :]).then_inc(sst.h, 16))
                sst.n += 16
            W("tensor", sve, xab_ve)

            sq_te6 = {}
            for oc in range(NCH):
                if (i, oc - 2) in evac6:
                    W("tensor", ssc, evac6[(i, oc - 2)])
                def y4mm(e, i=i, oc=oc):
                    bank = Y4p[oc % 2]
                    for dc in range(NCH):
                        e.matmul(bank[:, :],
                                 WTS[:, dc * C + oc * 128: dc * C + (oc + 1) * 128],
                                 XH2[:, dc * TS:(dc + 1) * TS],
                                 start=(dc == 0), stop=False)
                    e.matmul(bank[:, :], SW[:, oc * 128:(oc + 1) * 128], NM3[:, :],
                             start=False, stop=False)
                    e.matmul(bank[:, :], BR[:, oc * 128:(oc + 1) * 128], OR512B[:, :],
                             start=False, stop=False)
                    e.matmul(bank[:, :], BST[:, oc * 128:(oc + 1) * 128], XAB[:, :],
                             start=False, stop=True).then_inc(ste.h, 1)
                on("tensor", y4mm)
                ste.n += 1
                W("scalar", ste, ste.n)
                if oc >= 2:
                    W("scalar", ste, sq_te6[oc - 2])
                def y4ev(e, i=i, oc=oc):
                    bank = Y4p[oc % 2]
                    e.activation(out=Y4SQ[oc % 2][:, :], in_=bank[:, :], func=AF.Square)
                    e.copy(out=Y4F[:, oc * TS:(oc + 1) * TS], in_=bank[:, :]).then_inc(ssc.h, 1)
                on("scalar", y4ev)
                ssc.n += 1
                evac6[(i, oc)] = ssc.n
                if oc >= 1:
                    W("tensor", ssc, evac6[(i, oc - 1)])
                    def sq6m(e, oc2=oc - 1):
                        e.matmul(Sq6[:, :], ONEB[:, :], Y4SQ[oc2 % 2][:, :],
                                 start=(oc2 == 0), stop=False).then_inc(ste.h, 1)
                    on("tensor", sq6m)
                    ste.n += 1
                    sq_te6[oc - 1] = ste.n
            W("tensor", ssc, evac6[(i, NCH - 1)])
            def sq6mf(e):
                e.matmul(Sq6[:, :], ONEB[:, :], Y4SQ[(NCH - 1) % 2][:, :],
                         start=False, stop=True).then_inc(ste.h, 1)
            on("tensor", sq6mf)
            ste.n += 1
            sq_te6[NCH - 1] = ste.n
            def m4mm(e, i=i):
                for dc in range(NCH):
                    e.matmul(Sum6[:, :], CSW[:, dc:dc + 1], XH2[:, dc * TS:(dc + 1) * TS],
                             start=(dc == 0), stop=False)
                e.matmul(Sum6[:, :], CSB[:, :], XAB[:, :],
                         start=False, stop=True).then_inc(ste.h, 1)
            on("tensor", m4mm)
            ste.n += 1
            W("vector", ste, ste.n)
            def v6r4(e):
                e.tensor_scalar(out=ROWA[:, :], in0=Sum6[:, :], scalar1=SBR[:, 0:1],
                                scalar2=1.0 / C, op0=ALU.add, op1=ALU.mult)
                e.tensor_scalar(out=ROWB[:, :], in0=Sq6[:, :], scalar1=1.0 / C,
                                scalar2=None, op0=ALU.mult)
                e.tensor_tensor(out=ROWC[:, :], in0=ROWA[:, :], in1=ROWA[:, :], op=ALU.mult)
                e.tensor_tensor(out=ROWB[:, :], in0=ROWB[:, :], in1=ROWC[:, :],
                                op=ALU.subtract).then_inc(sve.h, 1)
            on("vector", v6r4)
            sve.n += 1
            if debug and i == 0:
                W("sync", sve, sve.n)
                W("sync", ssc, evac6[(i, NCH - 1)])
                def dy4(sy):
                    sy.dma_start(out=d_y4, in_=Y4F[:, :]).then_inc(sst.h, 16)
                    sy.dma_start(out=d_rows[:, 2 * TS:3 * TS], in_=ROWA[:, :]).then_inc(sst.h, 16)
                    sy.dma_start(out=d_rows[:, 3 * TS:4 * TS], in_=ROWB[:, :]).then_inc(sst.h, 16)
                on("sync", dy4)
                sst.n += 48
                W("scalar", sst, sst.n)
                W("vector", sst, sst.n)
            W("scalar", sve, sve.n)
            def s6r4(e):
                e.activation(out=ROWB[:, :], in_=ROWB[:, :], func=AF.Ln, bias=EPSC[:, 0:1])
                e.activation(out=ROWC[:, :], in_=ROWB[:, :], func=AF.Exp,
                             scale=-0.5).then_inc(ssc.h, 1)
            on("scalar", s6r4)
            ssc.n += 1
            W("vector", ssc, ssc.n)
            def v6cf(e, i=i):
                e.scalar_tensor_tensor(out=ROWA[:, :], in0=ROWA[:, :], scalar=-1.0,
                                       in1=ROWC[:, :], op0=ALU.mult, op1=ALU.mult)
                e.tensor_tensor(out=ROWA[:, :], in0=ROWA[:, :],
                                in1=CORR2[:, i * TS:(i + 1) * TS], op=ALU.add).then_inc(sve.h, 1)
            on("vector", v6cf)
            sve.n += 1
            W("tensor", sve, sve.n)
            def t6bc2(e):
                e.matmul(AUXp[:, :], ONERF[:, :], ROWC[:, :], start=True, stop=True)
                e.matmul(XAp[:, :], ONERF[:, :], ROWA[:, :],
                         start=True, stop=True).then_inc(ste.h, 1)
            on("tensor", t6bc2)
            ste.n += 1
            W("vector", ste, ste.n)
            def v6fin(e, i=i):
                e.tensor_copy(out=RB6[:, :], in_=AUXp[:, :])
                e.tensor_copy(out=CBS[:, :], in_=XAp[:, :])
                for dc in range(NCH):
                    e.tensor_tensor(out=Y4F[:, dc * TS:(dc + 1) * TS],
                                    in0=Y4F[:, dc * TS:(dc + 1) * TS], in1=RB6[:, :], op=ALU.mult)
                for dc in range(NCH):
                    e.tensor_tensor(out=Y4F[:, dc * TS:(dc + 1) * TS],
                                    in0=Y4F[:, dc * TS:(dc + 1) * TS],
                                    in1=X2T[:, dc * TS:(dc + 1) * TS], op=ALU.add)
                for dc in range(NCH):
                    ins = e.tensor_tensor(out=Y4F[:, dc * TS:(dc + 1) * TS],
                                          in0=Y4F[:, dc * TS:(dc + 1) * TS],
                                          in1=CBS[:, :], op=ALU.add)
                ins.then_inc(sve.h, 1)
            on("vector", v6fin)
            sve.n += 1
            fin6[i] = sve.n
            W("sync", sve, sve.n)
            def st6(sy, i=i):
                sy.dma_start(out=outT.rearrange("(k d) n -> d k n", d=128)[:, :, i * TS:(i + 1) * TS],
                             in_=Y4F[:, :].rearrange("d (k n) -> d k n", k=NCH)).then_inc(sst.h, 16)
            on("sync", st6)
            sst.n += 16
            if i + 1 < NOT_:
                ld6(i + 1)
        W("sync", sst, sst.n)

    with nc.Block() as block:
        @block.tensor
        def _(t):
            for f in ops["tensor"]:
                f(t)

        @block.scalar
        def _(s):
            for f in ops["scalar"]:
                f(s)

        @block.vector
        def _(v):
            for f in ops["vector"]:
                f(v)

        @block.gpsimd
        def _(g):
            for f in ops["gpsimd"]:
                f(g)

        @block.sync
        def _(sy):
            for f in ops["sync"]:
                f(sy)

    ctx_stack.close()
    return nc


def _host_prep(inputs):
    x = np.asarray(inputs["L0_attn_x"], dtype=np.float32)
    wq = np.asarray(inputs["L0_attn_w_q"], dtype=np.float32)
    wk = np.asarray(inputs["L0_attn_w_k"], dtype=np.float32)
    wv = np.asarray(inputs["L0_attn_w_v"], dtype=np.float32)
    wo = np.asarray(inputs["L0_attn_w_o"], dtype=np.float32)
    Wf = np.asarray(inputs["L0_ffn_W"], dtype=np.float32)
    bf_ = np.asarray(inputs["L0_ffn_b"], dtype=np.float32)
    lA = np.asarray(inputs["L0_ffn_lora_A"], dtype=np.float32)
    lB = np.asarray(inputs["L0_ffn_lora_B"], dtype=np.float32)
    gw = np.asarray(inputs["L0_ffn_gate_w"], dtype=np.float32)
    gb = np.asarray(inputs["L0_ffn_gate_b"], dtype=np.float32)

    woT = np.ascontiguousarray(wo.T)
    WTt = np.ascontiguousarray(Wf.T)
    AstkT = np.ascontiguousarray(lA.reshape(128, C).T)
    BstkT = np.ascontiguousarray(lB.transpose(0, 2, 1).reshape(128, C))
    gwT = np.ascontiguousarray(gw.T)

    t_ = np.arange(T, dtype=np.float32)
    inv = (1.0 / (10000.0 ** (np.arange(0, D, 2, dtype=np.float32) / D))).astype(np.float32)
    freqs = np.outer(t_, inv).astype(np.float32)
    emb = np.concatenate([freqs, freqs], axis=1)
    cosT = np.ascontiguousarray(np.cos(emb).T.astype(np.float32))
    sinv = np.sin(emb).astype(np.float32)
    sgn = np.ones((128,), np.float32)
    sgn[:64] = -1.0
    sinT = np.ascontiguousarray((sinv * sgn[None, :]).T.astype(np.float32))

    c_ones = np.ones((128, 1), np.float32)
    c_masks = np.zeros((128, 3), np.float32)
    c_masks[64:, 1] = NEG
    c_masks[:64, 2] = NEG
    c_E8 = np.zeros((8, 128), np.float32)
    for e_ in range(8):
        c_E8[e_, e_ * 16:(e_ + 1) * 16] = 1.0
    c_id = np.eye(128, dtype=np.float32)
    c_ones512 = np.ones((1, 512), np.float32)

    in_maps = []
    for core in range(8):
        b, g = core // 2, core % 2
        xTb = np.ascontiguousarray(x[b].T)
        m = dict(
            xT=xTb,
            xoT=np.ascontiguousarray(xTb[:, g * OWN:(g + 1) * OWN]),
            wqT=np.ascontiguousarray(wq[g * CG:(g + 1) * CG, :].T),
            wkT=np.ascontiguousarray(wk[g * CG:(g + 1) * CG, :].T),
            wvT=np.ascontiguousarray(wv[g * CG:(g + 1) * CG, :].T),
            woT=woT, WT=WTt, brow=bf_[None, :],
            AstkT=AstkT, BstkT=BstkT, gwT=gwT, gbrow=gb[None, :],
            cosT=cosT, sinT=sinT,
            c_ones=c_ones, c_masks=c_masks, c_E8=c_E8, c_id=c_id,
            c_ones512=c_ones512,
            c_eps=np.full((1, 1), EPS, np.float32),
        )
        in_maps.append(m)
    return in_maps


def kernel(**inputs):
    in_maps = _host_prep(inputs)
    nc = build_nc(debug=False)
    res = run_bass_kernel_spmd(nc, in_maps, core_ids=list(range(8)))
    out = np.empty((B, T, C), dtype=np.float32)
    for core in range(8):
        b, g = core // 2, core % 2
        out[b, g * OWN:(g + 1) * OWN, :] = np.asarray(res.results[core]["outT"]).T
    return out


# version 04
# speedup vs baseline: 1.1595x; 1.0284x over previous
import numpy as np
import ml_dtypes
from contextlib import ExitStack

import concourse.bass as bass
import concourse.mybir as mybir
from concourse.bass_utils import run_bass_kernel_spmd

F32 = mybir.dt.float32
BF16 = mybir.dt.bfloat16
AF = mybir.ActivationFunctionType
ALU = mybir.AluOpType

B, T, C, H, D = 4, 4096, 2048, 16, 128
HG = 8
CG = HG * D
NT, TS = 8, 512
NCH = C // 128
OWN = 2048
NOT_ = 4
SCALE = float(1.0 / np.sqrt(D))
EPS = 1e-5
NEG = -1e30

ENGS = ("tensor", "scalar", "vector", "gpsimd", "sync")
PAIRS = [[0, 1], [2, 3], [4, 5], [6, 7]]


def _attn_plan():
    nb = T // 64
    bp = np.arange(nb)
    qb, kb = bp[:, None], bp[None, :]
    dist = qb - kb
    bm = (((dist >= 0) & (dist <= 16)) | (kb < 2) | (kb % 4 == 0)) & (kb <= qb)
    plan = []
    for c in range(8):
        rows = bm[c * 8:(c + 1) * 8]
        groups = []
        for j in range(nb // 2):
            a = np.where(rows[:, 2 * j])[0]
            b = np.where(rows[:, 2 * j + 1])[0]
            if len(a) == 0 and len(b) == 0:
                continue
            if len(a) and len(b):
                r0 = int(min(a[0], b[0])); r1 = int(max(a[-1], b[-1]))
                segs = []
                cur = None
                for r in range(r0, r1 + 1):
                    key = (r in a, r in b)
                    if cur is None or cur[2] != key:
                        if cur is not None:
                            segs.append(tuple(cur))
                        cur = [r, r, key]
                    else:
                        cur[1] = r
                segs.append(tuple(cur))
                segs = [(s0, s1, 0 if k == (True, True) else (1 if k == (True, False) else 2))
                        for (s0, s1, k) in segs]
                groups.append(dict(kind="pair", j=j, r0=r0, r1=r1, segs=segs))
            else:
                side = 0 if len(a) else 1
                rr = a if len(a) else b
                groups.append(dict(kind="single", j=j, side=side,
                                   r0=int(rr[0]), r1=int(rr[-1])))
        plan.append(groups)
    return plan


ATTN_PLAN = _attn_plan()


class Sem:
    __slots__ = ("h", "n", "name")

    def __init__(self, handle, name):
        self.h = handle
        self.n = 0
        self.name = name


def build_nc(debug=False):
    nc = bass.Bass("TRN2", target_bir_lowering=False, debug=False)

    def din(name, shape, dt):
        return nc.dram_tensor(name, shape, dt, kind="ExternalInput").ap()

    def dout(name, shape, dt):
        return nc.dram_tensor(name, shape, dt, kind="ExternalOutput").ap()

    def dint(name, shape, dt):
        if debug:
            return dout(name, shape, dt)
        return nc.dram_tensor(name, shape, dt).ap()

    xT = din("xT", [C, T], F32)
    xoT = din("xoT", [C, OWN], F32)
    wqT = din("wqT", [C, CG], F32)
    wkT = din("wkT", [C, CG], F32)
    wvT = din("wvT", [C, CG], F32)
    woT = din("woT", [C, C], F32)
    WT = din("WT", [C, C], F32)
    brow = din("brow", [1, C], F32)
    AstkT = din("AstkT", [C, 128], F32)
    BstkT = din("BstkT", [128, C], F32)
    gwT = din("gwT", [C, 8], F32)
    gbrow = din("gbrow", [1, 8], F32)
    cosT = din("cosT", [128, T], BF16)
    sinT = din("sinT", [128, T], BF16)
    c_ones = din("c_ones", [128, 1], F32)
    c_masks = din("c_masks", [128, 3], F32)
    c_E8 = din("c_E8", [8, 128], F32)
    c_id = din("c_id", [128, 128], F32)
    c_ones512 = din("c_ones512", [1, 512], F32)
    c_eps = din("c_eps", [1, 1], F32)

    outT = dout("outT", [C, OWN], F32)

    qbuf = dint("qbuf", [CG, T], BF16)
    kbuf = dint("kbuf", [CG, T], BF16)
    vbuf = dint("vbuf", [128, 32 * CG], BF16)
    ctxin = nc.dram_tensor("ctxin", [8 * CG, TS], BF16).ap()
    ctxout = nc.dram_tensor("ctxout", [16 * CG, TS], BF16).ap()
    ctxout_dbg = dout("ctxout", [16 * CG, TS], BF16) if False else None
    if debug:
        ctxout_dbg = dout("ctxout_dbg", [16 * CG, TS], BF16)
        d_xh2 = dout("d_xh2", [128, NCH * TS], BF16)
        d_w8 = dout("d_w8", [8, TS], F32)
        d_xab = dout("d_xab", [128, TS], BF16)
        d_y4 = dout("d_y4", [128, NCH * TS], F32)
        d_rows = dout("d_rows", [1, 4 * TS], F32)
        d_gate = dout("d_gate", [128, 64], F32)
    x2buf = dint("x2buf", [128, NCH * OWN], F32)
    agw_i = nc.dram_tensor("agw_i", [8, 512], BF16).ap()
    agw_o = nc.dram_tensor("agw_o", [16, 512], BF16).ap()

    ops = {e: [] for e in ENGS}

    def on(eng, fn):
        ops[eng].append(fn)

    ctx_stack = ExitStack()
    E = ctx_stack.enter_context

    sems = {}
    for nm in ("ste", "ssc", "sve", "scc", "sbar",
               "sx0", "sx1", "sldg", "slds", "sst", "sa0", "sa1", "sb0", "sb1"):
        sems[nm] = Sem(E(nc.semaphore(nm)), nm)
    ste, ssc, sve = sems["ste"], sems["ssc"], sems["sve"]
    scc, sbar = sems["scc"], sems["sbar"]
    sx = [sems["sx0"], sems["sx1"]]
    sldg, slds, sst = sems["sldg"], sems["slds"], sems["sst"]
    sa = [sems["sa0"], sems["sa1"]]
    sb = [sems["sb0"], sems["sb1"]]

    def W(eng, sem, val):
        assert val <= sem.n, f"wait {sem.name}>={val} but only {sem.n} planned"
        on(eng, lambda e, s=sem.h, v=val: e.wait_ge(s, v))

    nbar = [0]

    def barrier():
        nbar[0] += 1
        tgt = 5 * nbar[0]
        for eng in ENGS:
            on(eng, lambda e: e.nop().then_inc(sbar.h, 1))
        sbar.n += 5
        for eng in ENGS:
            W(eng, sbar, tgt)

    ONEB = E(nc.sbuf_tensor("ONEB", [128, 1], BF16))
    ONEF = E(nc.sbuf_tensor("ONEF", [128, 1], F32))
    ONERF = E(nc.sbuf_tensor("ONERF", [1, 128], F32))
    ONE1F = E(nc.sbuf_tensor("ONE1F", [1, 1], F32))
    OR512B = E(nc.sbuf_tensor("OR512B", [1, 512], BF16))
    MASKC = E(nc.sbuf_tensor("MASKC", [128, 3], F32))
    E8F = E(nc.sbuf_tensor("E8F", [8, 128], F32))
    ID128 = E(nc.sbuf_tensor("ID128", [128, 128], F32))
    GBB = E(nc.sbuf_tensor("GBB", [128, 8], F32))
    GBR = E(nc.sbuf_tensor("GBR", [1, 8], F32))
    RSTD = E(nc.sbuf_tensor("RSTD", [1, T], F32))
    RSTDC = E(nc.sbuf_tensor("RSTDC", [128, 32], F32))
    ROWA = E(nc.sbuf_tensor("ROWA", [1, 512], F32))
    ROWB = E(nc.sbuf_tensor("ROWB", [1, 512], F32))
    ROWC = E(nc.sbuf_tensor("ROWC", [1, 512], F32))
    EPSC = E(nc.sbuf_tensor("EPSC", [1, 1], F32))

    def ld0(sy):
        sy.dma_start(out=ONEF[:, :], in_=c_ones).then_inc(slds.h, 16)
        sy.dma_start(out=ONERF[:, :], in_=c_ones.rearrange("p o -> o p")).then_inc(slds.h, 16)
        sy.dma_start(out=ONE1F[:, :], in_=c_ones[0:1, :]).then_inc(slds.h, 16)
        sy.dma_start(out=MASKC[:, :], in_=c_masks).then_inc(slds.h, 16)
        sy.dma_start(out=E8F[:, :], in_=c_E8).then_inc(slds.h, 16)
        sy.dma_start(out=ID128[:, :], in_=c_id).then_inc(slds.h, 16)
        sy.dma_start(out=GBR[:, :], in_=gbrow).then_inc(slds.h, 16)
        sy.dma_start(out=EPSC[:, :], in_=c_eps).then_inc(slds.h, 16)
    on("sync", ld0)
    slds.n += 8 * 16

    def ld0g(g):
        g.dma_start(out=ONEB[:, :], in_=c_ones).then_inc(sldg.h, 16)
        g.dma_start(out=OR512B[:, :], in_=c_ones512).then_inc(sldg.h, 16)
        g.collective_compute("AllGather", ALU.bypass, replica_groups=PAIRS,
                             ins=[agw_i[:, :]], outs=[agw_o[:, :]]).then_inc(scc.h, 1)
    on("gpsimd", ld0g)
    sldg.n += 2 * 16
    scc.n += 1

    S0p = ExitStack()
    PB0 = S0p.enter_context(nc.psum_tensor("PB0", [128, 8], F32))
    W("tensor", slds, slds.n)
    on("tensor", lambda e: e.matmul(PB0[:, :], ONERF[:, :], GBR[:, :],
                                    start=True, stop=True).then_inc(ste.h, 1))
    ste.n += 1
    W("vector", ste, ste.n)
    on("vector", lambda e: e.tensor_copy(out=GBB[:, :], in_=PB0[:, :]).then_inc(sve.h, 1))
    sve.n += 1
    S0p.close()

    with ExitStack() as S1:
        e1 = S1.enter_context
        WQ = e1(nc.sbuf_tensor("WQ", [128, NCH * CG], BF16))
        WK = e1(nc.sbuf_tensor("WK", [128, NCH * CG], BF16))
        WV = e1(nc.sbuf_tensor("WV", [128, NCH * CG], BF16))
        XBF = [e1(nc.sbuf_tensor(f"XBF{i}", [128, NCH * TS], BF16)) for i in range(2)]
        XSQ = e1(nc.sbuf_tensor("XSQ", [128, NCH * TS], BF16))
        QST = e1(nc.sbuf_tensor("QST", [128, HG * TS], BF16))
        KST = e1(nc.sbuf_tensor("KST", [128, HG * TS], BF16))
        VST = e1(nc.sbuf_tensor("VST", [128, 4 * CG], BF16))
        NEGM = [e1(nc.sbuf_tensor(f"NEGM{i}", [1, TS], BF16)) for i in range(2)]
        SQR = e1(nc.sbuf_tensor("SQR", [1, CG], BF16))
        SKR = e1(nc.sbuf_tensor("SKR", [1, CG], BF16))
        SVR = e1(nc.sbuf_tensor("SVR", [1, CG], BF16))
        Qp = [e1(nc.psum_tensor(f"QpA{i}", [128, 512], F32)) for i in range(2)]
        Kp = [e1(nc.psum_tensor(f"KpA{i}", [128, 512], F32)) for i in range(2)]
        Vp = e1(nc.psum_tensor("VpA", [128, 512], F32))
        Sump = e1(nc.psum_tensor("SumpA", [1, 512], F32))
        Sqp = e1(nc.psum_tensor("SqpA", [1, 512], F32))
        Colp = e1(nc.psum_tensor("ColpA", [128, 4], F32))

        def wload(g):
            g.dma_start(out=WQ[:, :].rearrange("d (k o) -> d k o", k=NCH),
                        in_=wqT.rearrange("(k d) o -> d k o", d=128)).then_inc(sldg.h, 16)
            g.dma_start(out=WK[:, :].rearrange("d (k o) -> d k o", k=NCH),
                        in_=wkT.rearrange("(k d) o -> d k o", d=128)).then_inc(sldg.h, 16)
            g.dma_start(out=WV[:, :].rearrange("d (k o) -> d k o", k=NCH),
                        in_=wvT.rearrange("(k d) o -> d k o", d=128)).then_inc(sldg.h, 16)
        on("gpsimd", wload)
        sldg.n += 3 * 16

        W("tensor", sldg, sldg.n)
        for (Wt, Srow) in ((WQ, SQR), (WK, SKR), (WV, SVR)):
            for sl in range(2):
                def smm(e, Wt=Wt, sl=sl):
                    for dc in range(NCH):
                        ins = e.matmul(Sump[:, :], ONEB[:, :],
                                       Wt[:, dc * CG + sl * 512: dc * CG + (sl + 1) * 512],
                                       start=(dc == 0), stop=(dc == NCH - 1))
                    ins.then_inc(ste.h, 1)
                on("tensor", smm)
                ste.n += 1
                W("vector", ste, ste.n)
                on("vector", lambda e, Srow=Srow, sl=sl: e.tensor_copy(
                    out=Srow[:, sl * 512:(sl + 1) * 512], in_=Sump[:, :]).then_inc(sve.h, 1))
                sve.n += 1
                W("tensor", sve, sve.n)

        xload = [None] * NT
        rows_sc = [None] * NT
        vrows_ve = [None] * NT
        stats_te = [None] * NT
        qkv_te_done = [None] * NT
        store_id = [None] * NT
        evacQ, evacK, evacV = {}, {}, {}

        def issue_xload(j):
            def f(g, j=j):
                g.dma_start(out=XBF[j % 2][:, :].rearrange("d (k n) -> d k n", k=NCH),
                            in_=xT.rearrange("(k d) n -> d k n", d=128)[:, :, j * TS:(j + 1) * TS]
                            ).then_inc(sx[j % 2].h, 16)
            on("gpsimd", f)
            sx[j % 2].n += 16
            xload[j] = sx[j % 2].n

        issue_xload(0)
        issue_xload(1)

        def emit_stats(j):
            W("scalar", sx[j % 2], xload[j])
            if j > 0:
                W("scalar", ste, stats_te[j - 1])
            on("scalar", lambda e, j=j: e.activation(
                out=XSQ[:, :], in_=XBF[j % 2][:, :], func=AF.Square).then_inc(ssc.h, 1))
            ssc.n += 1
            sq_done = ssc.n
            W("tensor", sx[j % 2], xload[j])
            if j > 0:
                W("tensor", sve, vrows_ve[j - 1])
            def summ(e, j=j):
                for dc in range(NCH):
                    ins = e.matmul(Sump[:, :], ONEB[:, :], XBF[j % 2][:, dc * TS:(dc + 1) * TS],
                                   start=(dc == 0), stop=(dc == NCH - 1))
                ins.then_inc(ste.h, 1)
            on("tensor", summ)
            ste.n += 1
            W("tensor", ssc, sq_done)
            def sqm(e):
                for dc in range(NCH):
                    ins = e.matmul(Sqp[:, :], ONEB[:, :], XSQ[:, dc * TS:(dc + 1) * TS],
                                   start=(dc == 0), stop=(dc == NCH - 1))
                ins.then_inc(ste.h, 1)
            on("tensor", sqm)
            ste.n += 1
            stats_te[j] = ste.n
            W("vector", ste, stats_te[j])
            if j > 0:
                W("vector", ssc, rows_sc[j - 1])
            def vrows(e, j=j):
                e.tensor_scalar(out=ROWC[:, :], in0=Sump[:, :],
                                scalar1=1.0 / C, scalar2=None, op0=ALU.mult)
                e.tensor_scalar(out=ROWA[:, :], in0=Sqp[:, :],
                                scalar1=1.0 / C, scalar2=None, op0=ALU.mult)
                e.tensor_tensor(out=ROWB[:, :], in0=ROWC[:, :], in1=ROWC[:, :], op=ALU.mult)
                e.tensor_tensor(out=ROWA[:, :], in0=ROWA[:, :], in1=ROWB[:, :], op=ALU.subtract)
                e.tensor_scalar(out=NEGM[j % 2][:, :], in0=ROWC[:, :],
                                scalar1=-1.0, scalar2=None, op0=ALU.mult).then_inc(sve.h, 1)
            on("vector", vrows)
            sve.n += 1
            vrows_ve[j] = sve.n
            W("scalar", sve, sve.n)
            def srows(e, j=j):
                e.activation(out=ROWB[:, :], in_=ROWA[:, :], func=AF.Ln, bias=EPSC[:, 0:1])
                e.activation(out=RSTD[:, j * TS:(j + 1) * TS], in_=ROWB[:, :],
                             func=AF.Exp, scale=-0.5).then_inc(ssc.h, 1)
            on("scalar", srows)
            ssc.n += 1
            rows_sc[j] = ssc.n

        def emit_qkv(j):
            W("tensor", ssc, rows_sc[j])
            def colmm(e, j=j):
                for tc in range(4):
                    ins = e.matmul(Colp[:, tc:tc + 1],
                                   RSTD[:, j * TS + tc * 128: j * TS + (tc + 1) * 128],
                                   ONE1F[:, :], start=True, stop=True)
                ins.then_inc(ste.h, 1)
            on("tensor", colmm)
            ste.n += 1
            W("vector", ste, ste.n)
            on("vector", lambda e, j=j: e.tensor_copy(
                out=RSTDC[:, j * 4:(j + 1) * 4], in_=Colp[:, :]).then_inc(sve.h, 1))
            sve.n += 1
            col_ve = sve.n
            if j > 0:
                W("scalar", sst, store_id[j - 1])
            for (Wt, Pp, DST, Srow, ed) in ((WQ, Qp, QST, SQR, evacQ), (WK, Kp, KST, SKR, evacK)):
                for oc in range(HG):
                    if (j, oc - 2) in ed:
                        W("tensor", ssc, ed[(j, oc - 2)])
                    def qmm(e, j=j, oc=oc, Wt=Wt, Pp=Pp, Srow=Srow):
                        bank = Pp[oc % 2]
                        for dc in range(NCH):
                            e.matmul(bank[:, :],
                                     Wt[:, dc * CG + oc * 128: dc * CG + (oc + 1) * 128],
                                     XBF[j % 2][:, dc * TS:(dc + 1) * TS],
                                     start=(dc == 0), stop=False)
                        e.matmul(bank[:, :], Srow[:, oc * 128:(oc + 1) * 128],
                                 NEGM[j % 2][:, :],
                                 start=False, stop=True).then_inc(ste.h, 1)
                    on("tensor", qmm)
                    ste.n += 1
                    W("scalar", ste, ste.n)
                    on("scalar", lambda e, oc=oc, Pp=Pp, DST=DST: e.copy(
                        out=DST[:, oc * TS:(oc + 1) * TS],
                        in_=Pp[oc % 2][:, :]).then_inc(ssc.h, 1))
                    ssc.n += 1
                    ed[(j, oc)] = ssc.n
            for tc in range(4):
                for os_ in range(2):
                    if (j, tc, os_) != (j, 0, 0) and (j, (tc, os_)) not in ():
                        pass
                    key_prev = (tc, os_ - 1) if os_ > 0 else (tc - 1, 1)
                    if key_prev[0] >= 0 and (j,) + key_prev in evacV:
                        W("tensor", ssc, evacV[(j,) + key_prev])
                    def vmm(e, j=j, tc=tc, os_=os_):
                        for dc in range(NCH):
                            e.matmul(Vp[:, :],
                                     XBF[j % 2][:, dc * TS + tc * 128: dc * TS + (tc + 1) * 128],
                                     WV[:, dc * CG + os_ * 512: dc * CG + (os_ + 1) * 512],
                                     start=(dc == 0), stop=False)
                        e.matmul(Vp[:, :],
                                 NEGM[j % 2][:, tc * 128:(tc + 1) * 128],
                                 SVR[:, os_ * 512:(os_ + 1) * 512],
                                 start=False, stop=True).then_inc(ste.h, 1)
                    on("tensor", vmm)
                    ste.n += 1
                    W("scalar", ste, ste.n)
                    W("scalar", sve, col_ve)
                    on("scalar", lambda e, j=j, tc=tc, os_=os_: e.activation(
                        out=VST[:, tc * CG + os_ * 512: tc * CG + (os_ + 1) * 512],
                        in_=Vp[:, :], func=AF.Copy,
                        scale=RSTDC[:, j * 4 + tc: j * 4 + tc + 1]).then_inc(ssc.h, 1))
                    ssc.n += 1
                    evacV[(j, tc, os_)] = ssc.n
            qkv_te_done[j] = ste.n
            W("sync", ssc, ssc.n)
            def store(sy, j=j):
                sy.dma_start(out=qbuf.rearrange("(h d) n -> d h n", d=128)[:, :, j * TS:(j + 1) * TS],
                             in_=QST[:, :].rearrange("d (h n) -> d h n", h=HG)).then_inc(sst.h, 16)
                sy.dma_start(out=kbuf.rearrange("(h d) n -> d h n", d=128)[:, :, j * TS:(j + 1) * TS],
                             in_=KST[:, :].rearrange("d (h n) -> d h n", h=HG)).then_inc(sst.h, 16)
                sy.dma_start(out=vbuf[:, j * 4 * CG:(j + 1) * 4 * CG],
                             in_=VST[:, :]).then_inc(sst.h, 16)
            on("sync", store)
            sst.n += 3 * 16
            store_id[j] = sst.n
            if j + 2 < NT:
                W("gpsimd", ste, qkv_te_done[j])
                issue_xload(j + 2)

        for j in range(NT):
            emit_stats(j)
            if j > 0:
                emit_qkv(j - 1)
        emit_qkv(NT - 1)
        W("sync", sst, sst.n)
    barrier()

    with ExitStack() as S2:
        e2 = S2.enter_context
        KH = e2(nc.sbuf_tensor("KH", [128, HG * T], BF16))
        VS = e2(nc.sbuf_tensor("VS", [128, 32 * CG], BF16))
        CB = e2(nc.sbuf_tensor("CB", [128, 512], BF16))
        SB = e2(nc.sbuf_tensor("SB", [128, 512], BF16))
        CS = e2(nc.sbuf_tensor("CS", [128, 512], BF16))
        SS = e2(nc.sbuf_tensor("SS", [128, 512], BF16))
        CSK = e2(nc.sbuf_tensor("CSK", [128, 512], BF16))
        SSK = e2(nc.sbuf_tensor("SSK", [128, 512], BF16))
        T1 = e2(nc.sbuf_tensor("T1", [128, 512], BF16))
        T2 = e2(nc.sbuf_tensor("T2", [128, 512], BF16))
        KW = e2(nc.sbuf_tensor("KW", [128, HG * TS], BF16))
        Rbp = e2(nc.psum_tensor("RbpB", [128, 512], F32))

        on("sync", lambda sy: sy.dma_start(out=VS[:, :], in_=vbuf[:, :]).then_inc(slds.h, 16))
        slds.n += 16
        vs_ld = slds.n

        QT = e2(nc.sbuf_tensor("QT", [128, HG * TS], BF16))
        QW = e2(nc.sbuf_tensor("QW", [128, HG * TS], BF16))
        QH = e2(nc.sbuf_tensor("QH", [128, HG * TS], BF16))
        PT = [e2(nc.sbuf_tensor(f"PT{i}", [128, 512], BF16)) for i in range(3)]
        IDR = [e2(nc.sbuf_tensor(f"IDR{i}", [1, 512], F32)) for i in range(2)]
        CTU = e2(nc.sbuf_tensor("CTU", [128, HG * TS], BF16))
        SC = [e2(nc.psum_tensor(f"ScpC{i}", [128, 512], F32)) for i in range(3)]
        Ctp = [e2(nc.psum_tensor(f"CtpC{i}", [128, 512], F32)) for i in range(2)]
        Dnp = [e2(nc.psum_tensor(f"DnpC{i}", [1, 512], F32)) for i in range(2)]

        W("tensor", slds, vs_ld)

        exp_hist = []
        smm_hist = []
        cd_hist = []
        smm_last = {}
        ldq_id = {}
        qrope_ve = {}
        recip_ve = {}
        rb2_te = {}
        vmul_ve = {}
        stctx_id = {}
        ctu_sc = {}
        cd_last = {}
        tail_q = []
        rbb_free = [0]

        def emit_tail_step(cur_gh=None):
            if tail_q and (cur_gh is None or tail_q[0][0] <= cur_gh - 2):
                tail_q.pop(0)[1]()

        def mk_tail(gh):
            def f(gh=gh):
                cch, h = gh // 8, gh % 8
                hb = gh % 2
                W("tensor", sve, recip_ve[gh])
                if rbb_free[0]:
                    W("tensor", sve, rbb_free[0])
                on("tensor", lambda e, hb=hb: e.matmul(
                    Rbp[:, :], ONERF[:, :], IDR[hb][:, :],
                    start=True, stop=True).then_inc(ste.h, 1))
                ste.n += 1
                rb2_te[gh] = ste.n
                W("vector", ste, ste.n)
                W("vector", ssc, ctu_sc[gh])
                on("vector", lambda e, h=h: e.tensor_tensor(
                    out=CTU[:, h * TS:(h + 1) * TS], in0=CTU[:, h * TS:(h + 1) * TS],
                    in1=Rbp[:, :], op=ALU.mult).then_inc(sve.h, 1))
                sve.n += 1
                vmul_ve[gh] = sve.n
                rbb_free[0] = sve.n
                W("sync", sve, sve.n)
                def stc(sy, cch=cch, h=h):
                    sy.dma_start(out=ctxin[cch * CG + h * 128: cch * CG + (h + 1) * 128, :],
                                 in_=CTU[:, h * TS:(h + 1) * TS]).then_inc(sst.h, 16)
                on("sync", stc)
                sst.n += 16
                stctx_id[gh] = sst.n
                if h == 7:
                    W("gpsimd", sst, sst.n)
                    on("gpsimd", lambda g, cch=cch: g.collective_compute(
                        "AllGather", ALU.bypass, replica_groups=PAIRS,
                        ins=[ctxin[cch * CG:(cch + 1) * CG, :]],
                        outs=[ctxout[cch * 2 * CG:(cch + 1) * 2 * CG, :]]).then_inc(scc.h, 1))
                    scc.n += 1
            return f

        def emit_ldq(c2):
            cl2 = c2 * TS
            def f(sy, cl=cl2):
                sy.dma_start(out=CS[:, :], in_=cosT[:, cl:cl + TS]).then_inc(sa[0].h, 16)
                sy.dma_start(out=SS[:, :], in_=sinT[:, cl:cl + TS]).then_inc(sa[0].h, 16)
                sy.dma_start(out=QT[:, :].rearrange("d (h n) -> d h n", h=HG),
                             in_=qbuf.rearrange("(h d) n -> d h n", d=128)[:, :, cl:cl + TS]
                             ).then_inc(sa[0].h, 16)
                qb4 = qbuf.rearrange("(h s dd) n -> s dd h n", h=HG, s=2)
                sy.dma_start(out=QW[0:64, :].rearrange("d (h n) -> d h n", h=HG),
                             in_=qb4[1, :, :, cl:cl + TS]).then_inc(sa[0].h, 16)
                sy.dma_start(out=QW[64:128, :].rearrange("d (h n) -> d h n", h=HG),
                             in_=qb4[0, :, :, cl:cl + TS]).then_inc(sa[0].h, 16)
            on("sync", f)
            sa[0].n += 5 * 16
            ldq_id[c2] = sa[0].n

        ldk_id = {}
        ropeK_ve = {}
        rbK_te = {}

        def emit_ldk(j):
            cl2 = j * TS
            if (j - 1) in ropeK_ve:
                W("sync", sve, ropeK_ve[j - 1])
            def f(sy, cl=cl2):
                sy.dma_start(out=CSK[:, :], in_=cosT[:, cl:cl + TS]).then_inc(slds.h, 16)
                sy.dma_start(out=SSK[:, :], in_=sinT[:, cl:cl + TS]).then_inc(slds.h, 16)
                sy.dma_start(out=KH[:, :].rearrange("d (h n) -> d h n", h=HG)[:, :, cl:cl + TS],
                             in_=kbuf.rearrange("(h d) n -> d h n", d=128)[:, :, cl:cl + TS]
                             ).then_inc(slds.h, 16)
                kb4 = kbuf.rearrange("(h s dd) n -> s dd h n", h=HG, s=2)
                sy.dma_start(out=KW[0:64, :].rearrange("d (h n) -> d h n", h=HG),
                             in_=kb4[1, :, :, cl:cl + TS]).then_inc(slds.h, 16)
                sy.dma_start(out=KW[64:128, :].rearrange("d (h n) -> d h n", h=HG),
                             in_=kb4[0, :, :, cl:cl + TS]).then_inc(slds.h, 16)
            on("sync", f)
            slds.n += 5 * 16
            ldk_id[j] = slds.n

        def emit_ropeK(j):
            cl2 = j * TS
            if rbb_free[0]:
                W("tensor", sve, rbb_free[0])
            on("tensor", lambda e, cl=cl2: e.matmul(
                Rbp[:, :], ONERF[:, :], RSTD[:, cl:cl + TS],
                start=True, stop=True).then_inc(ste.h, 1))
            ste.n += 1
            rbK_te[j] = ste.n
            W("vector", slds, ldk_id[j])
            W("vector", ste, rbK_te[j])
            def vropeK(e, cl=cl2):
                e.tensor_tensor(out=CB[:, :], in0=CSK[:, :], in1=Rbp[:, :], op=ALU.mult)
                e.tensor_tensor(out=SB[:, :], in0=SSK[:, :], in1=Rbp[:, :], op=ALU.mult)
                for h in range(HG):
                    e.tensor_tensor(out=T1[:, :], in0=KH[:, h * T + cl: h * T + cl + TS],
                                    in1=CB[:, :], op=ALU.mult)
                    e.tensor_tensor(out=T2[:, :], in0=KW[:, h * TS:(h + 1) * TS],
                                    in1=SB[:, :], op=ALU.mult)
                    ins = e.tensor_tensor(out=KH[:, h * T + cl: h * T + cl + TS],
                                          in0=T1[:, :], in1=T2[:, :], op=ALU.add)
                ins.then_inc(sve.h, 1)
            on("vector", vropeK)
            sve.n += 1
            ropeK_ve[j] = sve.n
            rbb_free[0] = sve.n

        emit_ldk(0)
        emit_ropeK(0)
        emit_ldq(0)
        for cch in range(8):
            if rbb_free[0]:
                W("tensor", sve, rbb_free[0])
            on("tensor", lambda e, cl=cch * TS: e.matmul(
                Rbp[:, :], ONERF[:, :], RSTD[:, cl:cl + TS],
                start=True, stop=True).then_inc(ste.h, 1))
            ste.n += 1
            rbq_te = ste.n
            W("vector", sa[0], ldq_id[cch])
            W("vector", ste, rbq_te)
            if cch >= 1:
                W("vector", ste, smm_last[cch - 1])
            def qrope(e):
                e.tensor_tensor(out=CB[:, :], in0=CS[:, :], in1=Rbp[:, :], op=ALU.mult)
                e.tensor_tensor(out=SB[:, :], in0=SS[:, :], in1=Rbp[:, :], op=ALU.mult)
                for h in range(HG):
                    e.tensor_tensor(out=T1[:, :], in0=QT[:, h * TS:(h + 1) * TS],
                                    in1=CB[:, :], op=ALU.mult)
                    e.tensor_tensor(out=T2[:, :], in0=QW[:, h * TS:(h + 1) * TS],
                                    in1=SB[:, :], op=ALU.mult)
                    ins = e.tensor_tensor(out=QH[:, h * TS:(h + 1) * TS],
                                          in0=T1[:, :], in1=T2[:, :], op=ALU.add)
                ins.then_inc(sve.h, 1)
            on("vector", qrope)
            sve.n += 1
            qrope_ve[cch] = sve.n
            rbb_free[0] = sve.n
            if cch + 1 < 8:
                W("sync", sve, qrope_ve[cch])
                emit_ldq(cch + 1)

            groups = ATTN_PLAN[cch]
            ng = len(groups)
            W("tensor", sve, qrope_ve[cch])
            W("tensor", sve, ropeK_ve[cch])
            for h in range(HG):
                gh = cch * 8 + h
                hb = gh % 2
                if h == 1 and cch + 1 < 8:
                    emit_ldk(cch + 1)
                    emit_ropeK(cch + 1)
                for gi, g in enumerate(groups):
                    if (gi % 2) == 0:
                        emit_tail_step(gh)
                    cnt = len(smm_hist)
                    bank = SC[cnt % 3]
                    q0, q1 = g["r0"] * 64, (g["r1"] + 1) * 64
                    if len(exp_hist) >= 3:
                        W("tensor", ssc, exp_hist[-3])
                    if g["kind"] == "pair":
                        def smm(e, h=h, j2=g["j"], q0=q0, q1=q1, bank=bank):
                            e.matmul(bank[:, q0:q1],
                                     KH[:, h * T + j2 * 128: h * T + (j2 + 1) * 128],
                                     QH[:, h * TS + q0: h * TS + q1],
                                     start=True, stop=True).then_inc(ste.h, 1)
                    else:
                        def smm(e, h=h, j2=g["j"], sd=g["side"], q0=q0, q1=q1, bank=bank):
                            e.matmul(bank[sd * 64:sd * 64 + 64, q0:q1],
                                     KH[:, h * T + j2 * 128 + sd * 64: h * T + j2 * 128 + sd * 64 + 64],
                                     QH[:, h * TS + q0: h * TS + q1],
                                     start=True, stop=True).then_inc(ste.h, 1)
                    on("tensor", smm)
                    ste.n += 1
                    smm_hist.append(ste.n)
                    W("scalar", ste, ste.n)
                    if len(cd_hist) >= 3:
                        W("scalar", ste, cd_hist[-3])
                    pt = PT[cnt % 3]
                    if g["kind"] == "pair":
                        def sexp(e, g=g, bank=bank, pt=pt):
                            for (s0, s1, bi) in g["segs"]:
                                a0, a1 = s0 * 64, (s1 + 1) * 64
                                if bi == 0:
                                    ins = e.activation(out=pt[:, a0:a1], in_=bank[:, a0:a1],
                                                       func=AF.Exp, scale=SCALE)
                                else:
                                    ins = e.activation(out=pt[:, a0:a1], in_=bank[:, a0:a1],
                                                       func=AF.Exp, scale=SCALE,
                                                       bias=MASKC[:, bi:bi + 1])
                            ins.then_inc(ssc.h, 1)
                    else:
                        def sexp(e, g=g, bank=bank, pt=pt, q0=q0, q1=q1):
                            sd = g["side"]
                            e.activation(out=pt[sd * 64:sd * 64 + 64, q0:q1],
                                         in_=bank[sd * 64:sd * 64 + 64, q0:q1],
                                         func=AF.Exp, scale=SCALE).then_inc(ssc.h, 1)
                    on("scalar", sexp)
                    ssc.n += 1
                    exp_hist.append(ssc.n)
                    W("tensor", ssc, ssc.n)
                    first, last = (gi == 0), (gi == ng - 1)
                    if first:
                        if (gh - 2) in ctu_sc:
                            W("tensor", ssc, ctu_sc[gh - 2])
                        if (gh - 2) in recip_ve:
                            W("tensor", sve, recip_ve[gh - 2])
                    if g["kind"] == "pair":
                        def cmm(e, j2=g["j"], h=h, q0=q0, q1=q1, pt=pt, first=first,
                                last=last, hb=hb):
                            e.matmul(Ctp[hb][:, q0:q1],
                                     VS[:, j2 * CG + h * 128: j2 * CG + (h + 1) * 128],
                                     pt[:, q0:q1], start=first, stop=last)
                            e.matmul(Dnp[hb][:, q0:q1], ONEB[:, :], pt[:, q0:q1],
                                     start=first, stop=last).then_inc(ste.h, 1)
                    else:
                        def cmm(e, j2=g["j"], sd=g["side"], h=h, q0=q0, q1=q1, pt=pt,
                                first=first, last=last, hb=hb):
                            e.matmul(Ctp[hb][:, q0:q1],
                                     VS[sd * 64:sd * 64 + 64, j2 * CG + h * 128: j2 * CG + (h + 1) * 128],
                                     pt[sd * 64:sd * 64 + 64, q0:q1], start=first, stop=last)
                            e.matmul(Dnp[hb][:, q0:q1], ONEB[sd * 64:sd * 64 + 64, :],
                                     pt[sd * 64:sd * 64 + 64, q0:q1],
                                     start=first, stop=last).then_inc(ste.h, 1)
                    on("tensor", cmm)
                    ste.n += 1
                    cd_hist.append(ste.n)
                cd_last[gh] = cd_hist[-1]
                W("scalar", ste, cd_last[gh])
                if (gh - 8) in stctx_id:
                    W("scalar", sst, stctx_id[gh - 8])
                on("scalar", lambda e, h=h, hb=hb: e.copy(
                    out=CTU[:, h * TS:(h + 1) * TS], in_=Ctp[hb][:, :]).then_inc(ssc.h, 1))
                ssc.n += 1
                ctu_sc[gh] = ssc.n
                W("vector", ste, cd_last[gh])
                if (gh - 2) in rb2_te:
                    W("vector", ste, rb2_te[gh - 2])
                on("vector", lambda e, hb=hb: e.reciprocal(
                    out=IDR[hb][:, :], in_=Dnp[hb][:, :]).then_inc(sve.h, 1))
                sve.n += 1
                recip_ve[gh] = sve.n
                tail_q.append((gh, mk_tail(gh)))
            smm_last[cch] = smm_hist[-1]
        while tail_q:
            emit_tail_step()
    barrier()

    CORR2 = E(nc.sbuf_tensor("CORR2", [1, OWN], F32))
    with ExitStack() as S5:
        e5 = S5.enter_context
        WO = e5(nc.sbuf_tensor("WO", [128, NCH * C], BF16))
        CTF = [e5(nc.sbuf_tensor(f"CTF{i}", [128, NCH * TS], BF16)) for i in range(2)]
        XO = e5(nc.sbuf_tensor("XO", [128, NCH * TS], F32))
        YF = e5(nc.sbuf_tensor("YF", [128, NCH * TS], F32))
        YSQ = [e5(nc.sbuf_tensor(f"YSQ{i}", [128, 512], BF16)) for i in range(2)]
        CSO = e5(nc.sbuf_tensor("CSO", [128, NCH], BF16))
        CSOF = e5(nc.sbuf_tensor("CSOF", [128, NCH], F32))
        RB5 = e5(nc.sbuf_tensor("RB5", [128, 512], F32))
        Yp = [e5(nc.psum_tensor(f"YpD{i}", [128, 512], F32)) for i in range(2)]
        Mnp = e5(nc.psum_tensor("MnpD", [1, 512], F32))
        Sqp5 = e5(nc.psum_tensor("Sqp5D", [1, 512], F32))
        Rb5p = e5(nc.psum_tensor("Rb5D", [128, 512], F32))

        evac5 = {}
        on("gpsimd", lambda g: g.dma_start(
            out=WO[:, :].rearrange("d (k o) -> d k o", k=NCH),
            in_=woT.rearrange("(k d) o -> d k o", d=128)).then_inc(sldg.h, 16))
        sldg.n += 16
        W("vector", sldg, sldg.n)
        def vcso(e):
            for cc in range(NCH):
                ins = e.reduce_sum(out=CSOF[:, cc:cc + 1], in_=WO[:, cc * C:(cc + 1) * C],
                                   axis=mybir.AxisListType.X)
            ins.then_inc(sve.h, 1)
        on("vector", vcso)
        sve.n += 1
        W("vector", sve, sve.n)
        on("vector", lambda e: e.tensor_copy(out=CSO[:, :], in_=CSOF[:, :]).then_inc(sve.h, 1))
        sve.n += 1
        cso_ve = sve.n

        W("sync", scc, scc.n)
        if debug:
            W("gpsimd", scc, scc.n)
            on("gpsimd", lambda g: g.dma_start(out=ctxout_dbg, in_=ctxout[:, :]).then_inc(sst.h, 16))
            sst.n += 16

        ctf_ld = [None] * NOT_
        xo_ld = [None] * NOT_
        ymm_done = [None] * NOT_
        fin_ve = [None] * NOT_

        def ld5_ctf(i):
            if i >= 2:
                W("sync", ste, ymm_done[i - 2])
            def f(sy, i=i):
                pid = sy.partition_id()
                with sy.register(f"rp5{i}") as rp:
                    sy.reg_mov(rp, pid)
                    sy.reg_mod(rp, rp, 2)
                    with sy.If_eq(rp, 0):
                        ch = i
                        sy.dma_start(out=CTF[i % 2][:, :].rearrange("d (k n) -> d k n", k=NCH),
                                     in_=ctxout[ch * 2 * CG:(ch + 1) * 2 * CG, :]
                                     .rearrange("(k d) n -> d k n", d=128)).then_inc(sa[1].h, 16)
                    with sy.Else():
                        ch = 4 + i
                        sy.dma_start(out=CTF[i % 2][:, :].rearrange("d (k n) -> d k n", k=NCH),
                                     in_=ctxout[ch * 2 * CG:(ch + 1) * 2 * CG, :]
                                     .rearrange("(k d) n -> d k n", d=128)).then_inc(sa[1].h, 16)
            on("sync", f)
            sa[1].n += 16
            ctf_ld[i] = sa[1].n

        def ld5_xo(i):
            if i >= 1:
                W("sync", sve, fin_ve[i - 1])
            def f(sy, i=i):
                sy.dma_start(out=XO[:, :].rearrange("d (k n) -> d k n", k=NCH),
                             in_=xoT.rearrange("(k d) n -> d k n", d=128)[:, :, i * TS:(i + 1) * TS]
                             ).then_inc(sb[0].h, 16)
            on("sync", f)
            sb[0].n += 16
            xo_ld[i] = sb[0].n

        ld5_ctf(0)
        ld5_xo(0)
        ld5_ctf(1)
        W("tensor", sldg, sldg.n)

        for i in range(NOT_):
            W("tensor", sa[1], ctf_ld[i])
            W("tensor", sve, cso_ve)
            if i > 0:
                W("tensor", sve, fin_ve[i - 1])
            def mmm(e, i=i):
                for cc in range(NCH):
                    ins = e.matmul(Mnp[:, :], CSO[:, cc:cc + 1],
                                   CTF[i % 2][:, cc * TS:(cc + 1) * TS],
                                   start=(cc == 0), stop=(cc == NCH - 1))
                ins.then_inc(ste.h, 1)
            on("tensor", mmm)
            ste.n += 1
            sq_te = {}
            for oc in range(NCH):
                if (i, oc - 2) in evac5:
                    W("tensor", ssc, evac5[(i, oc - 2)])
                def ymm(e, i=i, oc=oc):
                    bank = Yp[oc % 2]
                    for cc in range(NCH):
                        ins = e.matmul(bank[:, :],
                                       WO[:, cc * C + oc * 128: cc * C + (oc + 1) * 128],
                                       CTF[i % 2][:, cc * TS:(cc + 1) * TS],
                                       start=(cc == 0), stop=(cc == NCH - 1))
                    ins.then_inc(ste.h, 1)
                on("tensor", ymm)
                ste.n += 1
                W("scalar", ste, ste.n)
                if oc >= 2:
                    W("scalar", ste, sq_te[oc - 2])
                def yev(e, i=i, oc=oc):
                    bank = Yp[oc % 2]
                    e.activation(out=YSQ[oc % 2][:, :], in_=bank[:, :], func=AF.Square)
                    e.copy(out=YF[:, oc * TS:(oc + 1) * TS], in_=bank[:, :]).then_inc(ssc.h, 1)
                on("scalar", yev)
                ssc.n += 1
                evac5[(i, oc)] = ssc.n
                if oc >= 1:
                    W("tensor", ssc, evac5[(i, oc - 1)])
                    def sqm5(e, oc2=oc - 1):
                        e.matmul(Sqp5[:, :], ONEB[:, :], YSQ[oc2 % 2][:, :],
                                 start=(oc2 == 0), stop=False).then_inc(ste.h, 1)
                    on("tensor", sqm5)
                    ste.n += 1
                    sq_te[oc - 1] = ste.n
            W("tensor", ssc, evac5[(i, NCH - 1)])
            def sqm5f(e):
                e.matmul(Sqp5[:, :], ONEB[:, :], YSQ[(NCH - 1) % 2][:, :],
                         start=False, stop=True).then_inc(ste.h, 1)
            on("tensor", sqm5f)
            ste.n += 1
            sq_te[NCH - 1] = ste.n
            ymm_done[i] = ste.n
            W("vector", ste, ste.n)
            def v5rows(e, i=i):
                e.tensor_scalar(out=ROWA[:, :], in0=Mnp[:, :], scalar1=1.0 / C,
                                scalar2=None, op0=ALU.mult)
                e.tensor_scalar(out=ROWB[:, :], in0=Sqp5[:, :], scalar1=1.0 / C,
                                scalar2=None, op0=ALU.mult)
                e.tensor_tensor(out=ROWC[:, :], in0=ROWA[:, :], in1=ROWA[:, :], op=ALU.mult)
                e.tensor_tensor(out=ROWB[:, :], in0=ROWB[:, :], in1=ROWC[:, :],
                                op=ALU.subtract).then_inc(sve.h, 1)
            on("vector", v5rows)
            sve.n += 1
            W("scalar", sve, sve.n)
            def s5rows(e):
                e.activation(out=ROWB[:, :], in_=ROWB[:, :], func=AF.Ln, bias=EPSC[:, 0:1])
                e.activation(out=ROWC[:, :], in_=ROWB[:, :], func=AF.Exp,
                             scale=-0.5).then_inc(ssc.h, 1)
            on("scalar", s5rows)
            ssc.n += 1
            W("vector", ssc, ssc.n)
            def v5corr(e, i=i):
                e.scalar_tensor_tensor(out=CORR2[:, i * TS:(i + 1) * TS], in0=ROWA[:, :],
                                       scalar=-1.0, in1=ROWC[:, :],
                                       op0=ALU.mult, op1=ALU.mult).then_inc(sve.h, 1)
            on("vector", v5corr)
            sve.n += 1
            W("tensor", sve, sve.n)
            on("tensor", lambda e: e.matmul(Rb5p[:, :], ONERF[:, :], ROWC[:, :],
                                            start=True, stop=True).then_inc(ste.h, 1))
            ste.n += 1
            W("vector", ste, ste.n)
            W("vector", sb[0], xo_ld[i])
            def v5fin(e, i=i):
                e.tensor_copy(out=RB5[:, :], in_=Rb5p[:, :])
                for cc in range(NCH):
                    e.tensor_tensor(out=YF[:, cc * TS:(cc + 1) * TS],
                                    in0=YF[:, cc * TS:(cc + 1) * TS], in1=RB5[:, :], op=ALU.mult)
                for cc in range(NCH):
                    ins = e.tensor_tensor(out=YF[:, cc * TS:(cc + 1) * TS],
                                          in0=YF[:, cc * TS:(cc + 1) * TS],
                                          in1=XO[:, cc * TS:(cc + 1) * TS], op=ALU.add)
                ins.then_inc(sve.h, 1)
            on("vector", v5fin)
            sve.n += 1
            fin_ve[i] = sve.n
            W("sync", sve, sve.n)
            def st5(sy, i=i):
                sy.dma_start(out=x2buf.rearrange("d (k n) -> d k n", k=NCH)[:, :, i * TS:(i + 1) * TS],
                             in_=YF[:, :].rearrange("d (k n) -> d k n", k=NCH)).then_inc(sst.h, 16)
            on("sync", st5)
            sst.n += 16
            if i + 1 < NOT_:
                ld5_xo(i + 1)
                if i + 2 < NOT_:
                    ld5_ctf(i + 2)
                W("scalar", sve, fin_ve[i])
                W("scalar", sst, sst.n)
        W("sync", sst, sst.n)
    barrier()

    with ExitStack() as S6:
        e6 = S6.enter_context
        WTS = e6(nc.sbuf_tensor("WTS", [128, NCH * C], BF16))
        AST = e6(nc.sbuf_tensor("AST", [128, NCH * 128], BF16))
        BST = e6(nc.sbuf_tensor("BST", [128, C], BF16))
        GWF = e6(nc.sbuf_tensor("GWF", [128, NCH * 8], F32))
        SGR = e6(nc.sbuf_tensor("SGR", [1, 8], F32))
        NM3M = e6(nc.sbuf_tensor("NM3M", [1, 512], F32))
        R3C = e6(nc.sbuf_tensor("R3C", [128, 4], F32))
        BR = e6(nc.sbuf_tensor("BR", [1, C], BF16))
        X2T = e6(nc.sbuf_tensor("X2T", [128, NCH * TS], F32))
        XH2 = e6(nc.sbuf_tensor("XH2", [128, NCH * TS], BF16))
        Y4F = e6(nc.sbuf_tensor("Y4F", [128, NCH * TS], F32))
        Y4SQ = [e6(nc.sbuf_tensor(f"Y4SQ{i}", [128, 512], BF16)) for i in range(2)]
        CSW = e6(nc.sbuf_tensor("CSW", [128, NCH], BF16))
        CSWF = e6(nc.sbuf_tensor("CSWF", [128, NCH], F32))
        CSB = e6(nc.sbuf_tensor("CSB", [128, 1], BF16))
        CSBF = e6(nc.sbuf_tensor("CSBF", [128, 1], F32))
        SW = e6(nc.sbuf_tensor("SW", [1, C], BF16))
        SA = e6(nc.sbuf_tensor("SA", [1, 128], BF16))
        SBR = e6(nc.sbuf_tensor("SBR", [1, 1], F32))
        NM3 = e6(nc.sbuf_tensor("NM3", [1, 512], BF16))
        RB3 = e6(nc.sbuf_tensor("RB3", [128, 512], F32))
        RB6 = e6(nc.sbuf_tensor("RB6", [128, 512], F32))
        CBS = RB3
        LG = e6(nc.sbuf_tensor("LG", [128, 32], F32))
        M1 = e6(nc.sbuf_tensor("M1", [128, 4], F32))
        M2 = e6(nc.sbuf_tensor("M2", [128, 4], F32))
        IS1 = e6(nc.sbuf_tensor("IS1", [128, 32], F32))
        IS2 = e6(nc.sbuf_tensor("IS2", [128, 32], F32))
        G1 = e6(nc.sbuf_tensor("G1", [128, 4], F32))
        G2 = e6(nc.sbuf_tensor("G2", [128, 4], F32))
        GT = e6(nc.sbuf_tensor("GT", [128, 4], F32))
        WSEL = e6(nc.sbuf_tensor("WSEL", [128, 32], F32))
        W8 = e6(nc.sbuf_tensor("W8", [8, 512], F32))
        XAS = e6(nc.sbuf_tensor("XAS", [128, 512], BF16))
        XAB = e6(nc.sbuf_tensor("XAB", [128, 512], BF16))
        DG = e6(nc.sbuf_tensor("DG", [128, 64], F32)) if debug else None
        Y4p = [e6(nc.psum_tensor(f"Y4pE{i}", [128, 512], F32)) for i in range(2)]
        Sum6 = e6(nc.psum_tensor("Sum6E", [1, 512], F32))
        Sq6 = e6(nc.psum_tensor("Sq6E", [1, 512], F32))
        XAp = e6(nc.psum_tensor("XApE", [128, 512], F32))
        Gp = e6(nc.psum_tensor("GpE", [128, 32], F32))
        COL6 = e6(nc.psum_tensor("COL6E", [128, 8], F32))
        AUXp = e6(nc.psum_tensor("AUXpE", [128, 512], F32))

        evac6 = {}

        def wld6(g):
            g.dma_start(out=WTS[:, :].rearrange("d (k o) -> d k o", k=NCH),
                        in_=WT.rearrange("(k d) o -> d k o", d=128)).then_inc(sldg.h, 16)
            g.dma_start(out=AST[:, :].rearrange("d (k o) -> d k o", k=NCH),
                        in_=AstkT.rearrange("(k d) o -> d k o", d=128)).then_inc(sldg.h, 16)
            g.dma_start(out=BST[:, :], in_=BstkT).then_inc(sldg.h, 16)
            g.dma_start(out=BR[:, :], in_=brow).then_inc(sldg.h, 16)
        on("gpsimd", wld6)
        sldg.n += 4 * 16
        w6 = sldg.n
        on("sync", lambda sy: sy.dma_start(
            out=GWF[:, :].rearrange("d (k o) -> d k o", k=NCH),
            in_=gwT.rearrange("(k d) o -> d k o", d=128)).then_inc(slds.h, 16))
        slds.n += 16
        gwf_ld = slds.n

        W("vector", sldg, w6)
        def v6pre(e):
            for dc in range(NCH):
                e.reduce_sum(out=CSWF[:, dc:dc + 1], in_=WTS[:, dc * C:(dc + 1) * C],
                             axis=mybir.AxisListType.X)
            e.reduce_sum(out=CSBF[:, :], in_=BST[:, :], axis=mybir.AxisListType.X)
            e.reduce_sum(out=SBR[:, :], in_=BR[:, :],
                         axis=mybir.AxisListType.X).then_inc(sve.h, 1)
        on("vector", v6pre)
        sve.n += 1
        W("vector", sve, sve.n)
        def v6preb(e):
            e.tensor_copy(out=CSW[:, :], in_=CSWF[:, :])
            e.tensor_copy(out=CSB[:, :], in_=CSBF[:, :]).then_inc(sve.h, 1)
        on("vector", v6preb)
        sve.n += 1
        W("tensor", sldg, w6)
        for sl in range(4):
            def swmm(e, sl=sl):
                for dc in range(NCH):
                    ins = e.matmul(Sum6[:, :], ONEB[:, :],
                                   WTS[:, dc * C + sl * 512: dc * C + (sl + 1) * 512],
                                   start=(dc == 0), stop=(dc == NCH - 1))
                ins.then_inc(ste.h, 1)
            on("tensor", swmm)
            ste.n += 1
            W("vector", ste, ste.n)
            on("vector", lambda e, sl=sl: e.tensor_copy(
                out=SW[:, sl * 512:(sl + 1) * 512], in_=Sum6[:, :]).then_inc(sve.h, 1))
            sve.n += 1
            W("tensor", sve, sve.n)
        W("tensor", slds, gwf_ld)
        def sgmm2(e):
            for dc in range(NCH):
                e.matmul(COL6[0:1, 0:8], ONEF[:, :], GWF[:, dc * 8:(dc + 1) * 8],
                         start=(dc == 0), stop=(dc == NCH - 1))
            for dc in range(NCH):
                ins = e.matmul(Sq6[:, 0:128], ONEB[:, :], AST[:, dc * 128:(dc + 1) * 128],
                               start=(dc == 0), stop=(dc == NCH - 1))
            ins.then_inc(ste.h, 1)
        on("tensor", sgmm2)
        ste.n += 1
        W("vector", ste, ste.n)
        def v6pre2(e):
            e.tensor_copy(out=SGR[:, :], in_=COL6[0:1, 0:8])
            e.tensor_copy(out=SA[:, :], in_=Sq6[:, 0:128]).then_inc(sve.h, 1)
        on("vector", v6pre2)
        sve.n += 1
        pre_ve = sve.n
        W("tensor", sve, pre_ve)

        x2ld = [None] * NOT_
        fin6 = [None] * NOT_

        def ld6(i):
            if i >= 1:
                W("sync", sve, fin6[i - 1])
            def f(sy, i=i):
                sy.dma_start(out=X2T[:, :].rearrange("d (k n) -> d k n", k=NCH),
                             in_=x2buf.rearrange("d (k n) -> d k n", k=NCH)[:, :, i * TS:(i + 1) * TS]
                             ).then_inc(sb[0].h, 16)
            on("sync", f)
            sb[0].n += 16
            x2ld[i] = sb[0].n

        ld6(0)
        for i in range(NOT_):
            W("scalar", sb[0], x2ld[i])
            if i > 0:
                W("scalar", sve, fin6[i - 1])
                W("scalar", sst, sst.n)
            def s6a(e, i=i):
                e.activation(out=XH2[:, :], in_=X2T[:, :],
                             func=AF.Square).then_inc(ssc.h, 1)
            on("scalar", s6a)
            ssc.n += 1
            W("tensor", ssc, ssc.n)
            if i > 0:
                W("tensor", sve, fin6[i - 1])
            def t6stats(e, i=i):
                for dc in range(NCH):
                    e.matmul(Sum6[:, :], ONEF[:, :], X2T[:, dc * TS:(dc + 1) * TS],
                             start=(dc == 0), stop=(dc == NCH - 1))
                for dc in range(NCH):
                    ins = e.matmul(Sq6[:, :], ONEB[:, :], XH2[:, dc * TS:(dc + 1) * TS],
                                   start=(dc == 0), stop=(dc == NCH - 1))
                ins.then_inc(ste.h, 1)
            on("tensor", t6stats)
            ste.n += 1
            W("vector", ste, ste.n)
            def v6rows(e):
                e.tensor_scalar(out=ROWA[:, :], in0=Sum6[:, :], scalar1=1.0 / C,
                                scalar2=None, op0=ALU.mult)
                e.tensor_scalar(out=ROWB[:, :], in0=Sq6[:, :], scalar1=1.0 / C,
                                scalar2=None, op0=ALU.mult)
                e.tensor_scalar(out=NM3M[:, :], in0=ROWA[:, :], scalar1=-1.0,
                                scalar2=None, op0=ALU.mult)
                e.tensor_tensor(out=ROWC[:, :], in0=ROWA[:, :], in1=ROWA[:, :], op=ALU.mult)
                e.tensor_tensor(out=ROWB[:, :], in0=ROWB[:, :], in1=ROWC[:, :],
                                op=ALU.subtract).then_inc(sve.h, 1)
            on("vector", v6rows)
            sve.n += 1
            if debug and i == 0:
                W("sync", sve, sve.n)
                def drows3(sy):
                    sy.dma_start(out=d_rows[:, 0:TS], in_=ROWA[:, :]).then_inc(sst.h, 16)
                    sy.dma_start(out=d_rows[:, TS:2 * TS], in_=ROWB[:, :]).then_inc(sst.h, 16)
                on("sync", drows3)
                sst.n += 32
                W("scalar", sst, sst.n)
            W("scalar", sve, sve.n)
            def s6rows(e):
                e.activation(out=ROWB[:, :], in_=ROWB[:, :], func=AF.Ln, bias=EPSC[:, 0:1])
                e.activation(out=ROWC[:, :], in_=ROWB[:, :], func=AF.Exp,
                             scale=-0.5).then_inc(ssc.h, 1)
            on("scalar", s6rows)
            ssc.n += 1
            W("vector", ssc, ssc.n)
            def v6neg(e):
                e.scalar_tensor_tensor(out=NM3[:, :], in0=ROWA[:, :], scalar=-1.0,
                                       in1=ROWC[:, :], op0=ALU.mult,
                                       op1=ALU.mult).then_inc(sve.h, 1)
            on("vector", v6neg)
            sve.n += 1
            W("tensor", sve, sve.n)
            def t6cols(e):
                for tc in range(4):
                    e.matmul(COL6[:, tc:tc + 1],
                             ROWC[:, tc * 128:(tc + 1) * 128], ONE1F[:, :],
                             start=True, stop=True)
                e.matmul(AUXp[:, :], ONERF[:, :], ROWC[:, :],
                         start=True, stop=True).then_inc(ste.h, 1)
            on("tensor", t6cols)
            ste.n += 1
            W("vector", ste, ste.n)
            def v6xh(e, i=i):
                e.tensor_copy(out=R3C[:, :], in_=COL6[:, 0:4])
                e.tensor_copy(out=RB3[:, :], in_=AUXp[:, :])
                for dc in range(NCH):
                    ins = e.tensor_tensor(out=XH2[:, dc * TS:(dc + 1) * TS],
                                          in0=X2T[:, dc * TS:(dc + 1) * TS],
                                          in1=RB3[:, :], op=ALU.mult)
                ins.then_inc(sve.h, 1)
            on("vector", v6xh)
            sve.n += 1
            xh_ve = sve.n
            if debug and i == 0:
                W("sync", sve, xh_ve)
                on("sync", lambda sy: sy.dma_start(out=d_xh2, in_=XH2[:, :]).then_inc(sst.h, 16))
                sst.n += 16

            W("tensor", sve, xh_ve)
            def t6gate(e, i=i):
                for tc in range(4):
                    for dc in range(NCH):
                        e.matmul(Gp[:, tc * 8:(tc + 1) * 8],
                                 X2T[:, dc * TS + tc * 128: dc * TS + (tc + 1) * 128],
                                 GWF[:, dc * 8:(dc + 1) * 8],
                                 start=(dc == 0), stop=False)
                    e.matmul(Gp[:, tc * 8:(tc + 1) * 8],
                             NM3M[:, tc * 128:(tc + 1) * 128],
                             SGR[:, :], start=False, stop=True)
                for dc in range(NCH):
                    e.matmul(XAp[:, :], AST[:, dc * 128:(dc + 1) * 128],
                             XH2[:, dc * TS:(dc + 1) * TS],
                             start=(dc == 0), stop=False)
                e.matmul(XAp[:, :], SA[:, :], NM3[:, :],
                         start=False, stop=True).then_inc(ste.h, 1)
            on("tensor", t6gate)
            ste.n += 1
            gate_te = ste.n
            W("scalar", ste, gate_te)
            on("scalar", lambda e: e.copy(out=XAS[:, :], in_=XAp[:, :]).then_inc(ssc.h, 1))
            ssc.n += 1
            xas_sc = ssc.n
            def step(fn):
                on("vector", fn)
                sve.n += 1
                W("vector", sve, sve.n)

            W("vector", ste, gate_te)
            def g_s0(e, i=i):
                if debug and i == 0:
                    e.tensor_copy(out=DG[:, 44:48], in_=R3C[:, :])
                for tc in range(4):
                    ins = e.tensor_scalar(out=LG[:, tc * 8:(tc + 1) * 8],
                                          in0=Gp[:, tc * 8:(tc + 1) * 8],
                                          scalar1=R3C[:, tc:tc + 1], scalar2=None,
                                          op0=ALU.mult)
                ins.then_inc(sve.h, 1)
            step(g_s0)
            def g_s1(e, i=i):
                if debug and i == 0:
                    e.tensor_copy(out=DG[:, 0:8], in_=Gp[:, 0:8])
                for tc in range(4):
                    ins = e.tensor_tensor(out=LG[:, tc * 8:(tc + 1) * 8],
                                          in0=LG[:, tc * 8:(tc + 1) * 8],
                                          in1=GBB[:, :], op=ALU.add)
                ins.then_inc(sve.h, 1)
            step(g_s1)
            def g_s2(e, i=i):
                if debug and i == 0:
                    e.tensor_copy(out=DG[:, 8:16], in_=LG[:, 0:8])
                for tc in range(4):
                    ins = e.reduce_max(out=M1[:, tc:tc + 1], in_=LG[:, tc * 8:(tc + 1) * 8],
                                       axis=mybir.AxisListType.X)
                ins.then_inc(sve.h, 1)
            step(g_s2)
            def g_s3(e, i=i):
                if debug and i == 0:
                    e.tensor_copy(out=DG[:, 16:17], in_=M1[:, 0:1])
                for tc in range(4):
                    ins = e.tensor_tensor(out=IS1[:, tc * 8:(tc + 1) * 8],
                                          in0=LG[:, tc * 8:(tc + 1) * 8],
                                          in1=M1[:, tc:tc + 1].broadcast_to([128, 8]),
                                          op=ALU.is_ge)
                ins.then_inc(sve.h, 1)
            step(g_s3)
            def g_s4(e, i=i):
                if debug and i == 0:
                    e.tensor_copy(out=DG[:, 18:26], in_=IS1[:, 0:8])
                for tc in range(4):
                    ins = e.scalar_tensor_tensor(out=LG[:, tc * 8:(tc + 1) * 8],
                                                 in0=IS1[:, tc * 8:(tc + 1) * 8], scalar=NEG,
                                                 in1=LG[:, tc * 8:(tc + 1) * 8],
                                                 op0=ALU.mult, op1=ALU.add)
                ins.then_inc(sve.h, 1)
            step(g_s4)
            def g_s5(e, i=i):
                if debug and i == 0:
                    e.tensor_copy(out=DG[:, 26:34], in_=LG[:, 0:8])
                for tc in range(4):
                    ins = e.reduce_max(out=M2[:, tc:tc + 1], in_=LG[:, tc * 8:(tc + 1) * 8],
                                       axis=mybir.AxisListType.X)
                ins.then_inc(sve.h, 1)
            step(g_s5)
            def g_s6(e, i=i):
                if debug and i == 0:
                    e.tensor_copy(out=DG[:, 17:18], in_=M2[:, 0:1])
                e.tensor_tensor(out=G1[:, :], in0=M1[:, :], in1=M2[:, :],
                                op=ALU.subtract).then_inc(sve.h, 1)
            step(g_s6)
            W("scalar", sve, sve.n)
            on("scalar", lambda e: e.activation(out=GT[:, :], in_=G1[:, :], func=AF.Tanh,
                                                scale=0.5).then_inc(ssc.h, 1))
            ssc.n += 1
            W("vector", ssc, ssc.n)
            step(lambda e: e.tensor_scalar(out=G1[:, :], in0=GT[:, :], scalar1=0.5,
                                           scalar2=0.5, op0=ALU.mult,
                                           op1=ALU.add).then_inc(sve.h, 1))
            step(lambda e: e.tensor_scalar(out=G2[:, :], in0=GT[:, :], scalar1=-0.5,
                                           scalar2=0.5, op0=ALU.mult,
                                           op1=ALU.add).then_inc(sve.h, 1))
            def g_s9(e):
                for tc in range(4):
                    ins = e.tensor_tensor(out=IS2[:, tc * 8:(tc + 1) * 8],
                                          in0=LG[:, tc * 8:(tc + 1) * 8],
                                          in1=M2[:, tc:tc + 1].broadcast_to([128, 8]),
                                          op=ALU.is_ge)
                ins.then_inc(sve.h, 1)
            step(g_s9)
            def g_s10(e):
                for tc in range(4):
                    ins = e.tensor_tensor(out=IS1[:, tc * 8:(tc + 1) * 8],
                                          in0=IS1[:, tc * 8:(tc + 1) * 8],
                                          in1=G1[:, tc:tc + 1].broadcast_to([128, 8]),
                                          op=ALU.mult)
                ins.then_inc(sve.h, 1)
            step(g_s10)
            def g_s11(e):
                for tc in range(4):
                    ins = e.tensor_tensor(out=IS2[:, tc * 8:(tc + 1) * 8],
                                          in0=IS2[:, tc * 8:(tc + 1) * 8],
                                          in1=G2[:, tc:tc + 1].broadcast_to([128, 8]),
                                          op=ALU.mult)
                ins.then_inc(sve.h, 1)
            step(g_s11)
            def g_s12(e, i=i):
                for tc in range(4):
                    ins = e.tensor_tensor(out=WSEL[:, tc * 8:(tc + 1) * 8],
                                          in0=IS1[:, tc * 8:(tc + 1) * 8],
                                          in1=IS2[:, tc * 8:(tc + 1) * 8], op=ALU.add)
                if debug and i == 0:
                    e.tensor_copy(out=DG[:, 34:42], in_=WSEL[:, 0:8])
                    e.tensor_copy(out=DG[:, 42:43], in_=G1[:, 0:1])
                    e.tensor_copy(out=DG[:, 43:44], in_=G2[:, 0:1])
                    e.tensor_copy(out=DG[:, 48:56], in_=IS1[:, 0:8])
                    e.tensor_copy(out=DG[:, 56:64], in_=IS2[:, 0:8])
                ins.then_inc(sve.h, 1)
            step(g_s12)
            W("tensor", sve, sve.n)
            def g_tr(e):
                for tc in range(4):
                    ins = e.transpose(AUXp[0:8, tc * 128:(tc + 1) * 128],
                                      WSEL[:, tc * 8:(tc + 1) * 8], ID128[:, :])
                ins.then_inc(ste.h, 1)
            on("tensor", g_tr)
            ste.n += 1
            W("vector", ste, ste.n)
            def g_w8(e):
                for tc in range(4):
                    ins = e.tensor_copy(out=W8[:, tc * 128:(tc + 1) * 128],
                                        in_=AUXp[0:8, tc * 128:(tc + 1) * 128])
                ins.then_inc(sve.h, 1)
            on("vector", g_w8)
            sve.n += 1
            W("vector", sve, sve.n)
            w8_ve = sve.n
            if debug and i == 0:
                W("sync", sve, w8_ve)
                on("sync", lambda sy: sy.dma_start(out=d_w8, in_=W8[:, :]).then_inc(sst.h, 16))
                on("sync", lambda sy: sy.dma_start(out=d_gate, in_=DG[:, :]).then_inc(sst.h, 16))
                sst.n += 32
            W("tensor", sve, w8_ve)
            on("tensor", lambda e: e.matmul(AUXp[:, :], E8F[:, :], W8[:, :],
                                            start=True, stop=True).then_inc(ste.h, 1))
            ste.n += 1
            W("vector", ste, ste.n)
            W("vector", ssc, xas_sc)
            def v6xab(e):
                e.tensor_tensor(out=XAB[:, :], in0=XAS[:, :], in1=AUXp[:, :],
                                op=ALU.mult).then_inc(sve.h, 1)
            on("vector", v6xab)
            sve.n += 1
            xab_ve = sve.n
            if debug and i == 0:
                W("sync", sve, xab_ve)
                on("sync", lambda sy: sy.dma_start(out=d_xab, in_=XAB[:, :]).then_inc(sst.h, 16))
                sst.n += 16
            W("tensor", sve, xab_ve)

            sq_te6 = {}
            for oc in range(NCH):
                if (i, oc - 2) in evac6:
                    W("tensor", ssc, evac6[(i, oc - 2)])
                def y4mm(e, i=i, oc=oc):
                    bank = Y4p[oc % 2]
                    for dc in range(NCH):
                        e.matmul(bank[:, :],
                                 WTS[:, dc * C + oc * 128: dc * C + (oc + 1) * 128],
                                 XH2[:, dc * TS:(dc + 1) * TS],
                                 start=(dc == 0), stop=False)
                    e.matmul(bank[:, :], SW[:, oc * 128:(oc + 1) * 128], NM3[:, :],
                             start=False, stop=False)
                    e.matmul(bank[:, :], BR[:, oc * 128:(oc + 1) * 128], OR512B[:, :],
                             start=False, stop=False)
                    e.matmul(bank[:, :], BST[:, oc * 128:(oc + 1) * 128], XAB[:, :],
                             start=False, stop=True).then_inc(ste.h, 1)
                on("tensor", y4mm)
                ste.n += 1
                W("scalar", ste, ste.n)
                if oc >= 2:
                    W("scalar", ste, sq_te6[oc - 2])
                def y4ev(e, i=i, oc=oc):
                    bank = Y4p[oc % 2]
                    e.activation(out=Y4SQ[oc % 2][:, :], in_=bank[:, :], func=AF.Square)
                    e.copy(out=Y4F[:, oc * TS:(oc + 1) * TS], in_=bank[:, :]).then_inc(ssc.h, 1)
                on("scalar", y4ev)
                ssc.n += 1
                evac6[(i, oc)] = ssc.n
                if oc >= 1:
                    W("tensor", ssc, evac6[(i, oc - 1)])
                    def sq6m(e, oc2=oc - 1):
                        e.matmul(Sq6[:, :], ONEB[:, :], Y4SQ[oc2 % 2][:, :],
                                 start=(oc2 == 0), stop=False).then_inc(ste.h, 1)
                    on("tensor", sq6m)
                    ste.n += 1
                    sq_te6[oc - 1] = ste.n
            W("tensor", ssc, evac6[(i, NCH - 1)])
            def sq6mf(e):
                e.matmul(Sq6[:, :], ONEB[:, :], Y4SQ[(NCH - 1) % 2][:, :],
                         start=False, stop=True).then_inc(ste.h, 1)
            on("tensor", sq6mf)
            ste.n += 1
            sq_te6[NCH - 1] = ste.n
            def m4mm(e, i=i):
                for dc in range(NCH):
                    e.matmul(Sum6[:, :], CSW[:, dc:dc + 1], XH2[:, dc * TS:(dc + 1) * TS],
                             start=(dc == 0), stop=False)
                e.matmul(Sum6[:, :], CSB[:, :], XAB[:, :],
                         start=False, stop=True).then_inc(ste.h, 1)
            on("tensor", m4mm)
            ste.n += 1
            W("vector", ste, ste.n)
            def v6r4(e):
                e.tensor_scalar(out=ROWA[:, :], in0=Sum6[:, :], scalar1=SBR[:, 0:1],
                                scalar2=1.0 / C, op0=ALU.add, op1=ALU.mult)
                e.tensor_scalar(out=ROWB[:, :], in0=Sq6[:, :], scalar1=1.0 / C,
                                scalar2=None, op0=ALU.mult)
                e.tensor_tensor(out=ROWC[:, :], in0=ROWA[:, :], in1=ROWA[:, :], op=ALU.mult)
                e.tensor_tensor(out=ROWB[:, :], in0=ROWB[:, :], in1=ROWC[:, :],
                                op=ALU.subtract).then_inc(sve.h, 1)
            on("vector", v6r4)
            sve.n += 1
            if debug and i == 0:
                W("sync", sve, sve.n)
                W("sync", ssc, evac6[(i, NCH - 1)])
                def dy4(sy):
                    sy.dma_start(out=d_y4, in_=Y4F[:, :]).then_inc(sst.h, 16)
                    sy.dma_start(out=d_rows[:, 2 * TS:3 * TS], in_=ROWA[:, :]).then_inc(sst.h, 16)
                    sy.dma_start(out=d_rows[:, 3 * TS:4 * TS], in_=ROWB[:, :]).then_inc(sst.h, 16)
                on("sync", dy4)
                sst.n += 48
                W("scalar", sst, sst.n)
                W("vector", sst, sst.n)
            W("scalar", sve, sve.n)
            def s6r4(e):
                e.activation(out=ROWB[:, :], in_=ROWB[:, :], func=AF.Ln, bias=EPSC[:, 0:1])
                e.activation(out=ROWC[:, :], in_=ROWB[:, :], func=AF.Exp,
                             scale=-0.5).then_inc(ssc.h, 1)
            on("scalar", s6r4)
            ssc.n += 1
            W("vector", ssc, ssc.n)
            def v6cf(e, i=i):
                e.scalar_tensor_tensor(out=ROWA[:, :], in0=ROWA[:, :], scalar=-1.0,
                                       in1=ROWC[:, :], op0=ALU.mult, op1=ALU.mult)
                e.tensor_tensor(out=ROWA[:, :], in0=ROWA[:, :],
                                in1=CORR2[:, i * TS:(i + 1) * TS], op=ALU.add).then_inc(sve.h, 1)
            on("vector", v6cf)
            sve.n += 1
            W("tensor", sve, sve.n)
            def t6bc2(e):
                e.matmul(AUXp[:, :], ONERF[:, :], ROWC[:, :], start=True, stop=True)
                e.matmul(XAp[:, :], ONERF[:, :], ROWA[:, :],
                         start=True, stop=True).then_inc(ste.h, 1)
            on("tensor", t6bc2)
            ste.n += 1
            W("vector", ste, ste.n)
            def v6fin(e, i=i):
                e.tensor_copy(out=RB6[:, :], in_=AUXp[:, :])
                e.tensor_copy(out=CBS[:, :], in_=XAp[:, :])
                for dc in range(NCH):
                    e.tensor_tensor(out=Y4F[:, dc * TS:(dc + 1) * TS],
                                    in0=Y4F[:, dc * TS:(dc + 1) * TS], in1=RB6[:, :], op=ALU.mult)
                for dc in range(NCH):
                    e.tensor_tensor(out=Y4F[:, dc * TS:(dc + 1) * TS],
                                    in0=Y4F[:, dc * TS:(dc + 1) * TS],
                                    in1=X2T[:, dc * TS:(dc + 1) * TS], op=ALU.add)
                for dc in range(NCH):
                    ins = e.tensor_tensor(out=Y4F[:, dc * TS:(dc + 1) * TS],
                                          in0=Y4F[:, dc * TS:(dc + 1) * TS],
                                          in1=CBS[:, :], op=ALU.add)
                ins.then_inc(sve.h, 1)
            on("vector", v6fin)
            sve.n += 1
            fin6[i] = sve.n
            W("sync", sve, sve.n)
            def st6(sy, i=i):
                sy.dma_start(out=outT.rearrange("(k d) n -> d k n", d=128)[:, :, i * TS:(i + 1) * TS],
                             in_=Y4F[:, :].rearrange("d (k n) -> d k n", k=NCH)).then_inc(sst.h, 16)
            on("sync", st6)
            sst.n += 16
            if i + 1 < NOT_:
                ld6(i + 1)
        W("sync", sst, sst.n)

    with nc.Block() as block:
        @block.tensor
        def _(t):
            for f in ops["tensor"]:
                f(t)

        @block.scalar
        def _(s):
            for f in ops["scalar"]:
                f(s)

        @block.vector
        def _(v):
            for f in ops["vector"]:
                f(v)

        @block.gpsimd
        def _(g):
            for f in ops["gpsimd"]:
                f(g)

        @block.sync
        def _(sy):
            for f in ops["sync"]:
                f(sy)

    ctx_stack.close()
    return nc


def _host_prep(inputs):
    x = np.asarray(inputs["L0_attn_x"], dtype=np.float32)
    wq = np.asarray(inputs["L0_attn_w_q"], dtype=np.float32)
    wk = np.asarray(inputs["L0_attn_w_k"], dtype=np.float32)
    wv = np.asarray(inputs["L0_attn_w_v"], dtype=np.float32)
    wo = np.asarray(inputs["L0_attn_w_o"], dtype=np.float32)
    Wf = np.asarray(inputs["L0_ffn_W"], dtype=np.float32)
    bf_ = np.asarray(inputs["L0_ffn_b"], dtype=np.float32)
    lA = np.asarray(inputs["L0_ffn_lora_A"], dtype=np.float32)
    lB = np.asarray(inputs["L0_ffn_lora_B"], dtype=np.float32)
    gw = np.asarray(inputs["L0_ffn_gate_w"], dtype=np.float32)
    gb = np.asarray(inputs["L0_ffn_gate_b"], dtype=np.float32)

    woT = np.ascontiguousarray(wo.T)
    WTt = np.ascontiguousarray(Wf.T)
    AstkT = np.ascontiguousarray(lA.reshape(128, C).T)
    BstkT = np.ascontiguousarray(lB.transpose(0, 2, 1).reshape(128, C))
    gwT = np.ascontiguousarray(gw.T)

    t_ = np.arange(T, dtype=np.float32)
    inv = (1.0 / (10000.0 ** (np.arange(0, D, 2, dtype=np.float32) / D))).astype(np.float32)
    freqs = np.outer(t_, inv).astype(np.float32)
    emb = np.concatenate([freqs, freqs], axis=1)
    cosT = np.ascontiguousarray(np.cos(emb).T.astype(ml_dtypes.bfloat16))
    sinv = np.sin(emb).astype(np.float32)
    sgn = np.ones((128,), np.float32)
    sgn[:64] = -1.0
    sinT = np.ascontiguousarray((sinv * sgn[None, :]).T.astype(ml_dtypes.bfloat16))

    c_ones = np.ones((128, 1), np.float32)
    c_masks = np.zeros((128, 3), np.float32)
    c_masks[64:, 1] = NEG
    c_masks[:64, 2] = NEG
    c_E8 = np.zeros((8, 128), np.float32)
    for e_ in range(8):
        c_E8[e_, e_ * 16:(e_ + 1) * 16] = 1.0
    c_id = np.eye(128, dtype=np.float32)
    c_ones512 = np.ones((1, 512), np.float32)

    in_maps = []
    for core in range(8):
        b, g = core // 2, core % 2
        xTb = np.ascontiguousarray(x[b].T)
        m = dict(
            xT=xTb,
            xoT=np.ascontiguousarray(xTb[:, g * OWN:(g + 1) * OWN]),
            wqT=np.ascontiguousarray(wq[g * CG:(g + 1) * CG, :].T),
            wkT=np.ascontiguousarray(wk[g * CG:(g + 1) * CG, :].T),
            wvT=np.ascontiguousarray(wv[g * CG:(g + 1) * CG, :].T),
            woT=woT, WT=WTt, brow=bf_[None, :],
            AstkT=AstkT, BstkT=BstkT, gwT=gwT, gbrow=gb[None, :],
            cosT=cosT, sinT=sinT,
            c_ones=c_ones, c_masks=c_masks, c_E8=c_E8, c_id=c_id,
            c_ones512=c_ones512,
            c_eps=np.full((1, 1), EPS, np.float32),
        )
        in_maps.append(m)
    return in_maps


def kernel(**inputs):
    in_maps = _host_prep(inputs)
    nc = build_nc(debug=False)
    res = run_bass_kernel_spmd(nc, in_maps, core_ids=list(range(8)))
    out = np.empty((B, T, C), dtype=np.float32)
    for core in range(8):
        b, g = core // 2, core % 2
        out[b, g * OWN:(g + 1) * OWN, :] = np.asarray(res.results[core]["outT"]).T
    return out


# version 05
# speedup vs baseline: 1.1652x; 1.0049x over previous
import numpy as np
import ml_dtypes
from contextlib import ExitStack

import concourse.bass as bass
import concourse.mybir as mybir
from concourse.bass_utils import run_bass_kernel_spmd

F32 = mybir.dt.float32
BF16 = mybir.dt.bfloat16
AF = mybir.ActivationFunctionType
ALU = mybir.AluOpType

B, T, C, H, D = 4, 4096, 2048, 16, 128
HG = 8
CG = HG * D
NT, TS = 8, 512
NCH = C // 128
OWN = 2048
NOT_ = 4
SCALE = float(1.0 / np.sqrt(D))
EPS = 1e-5
NEG = -1e30

ENGS = ("tensor", "scalar", "vector", "gpsimd", "sync")
PAIRS = [[0, 1], [2, 3], [4, 5], [6, 7]]


def _attn_plan():
    nb = T // 64
    bp = np.arange(nb)
    qb, kb = bp[:, None], bp[None, :]
    dist = qb - kb
    bm = (((dist >= 0) & (dist <= 16)) | (kb < 2) | (kb % 4 == 0)) & (kb <= qb)
    plan = []
    for c in range(8):
        rows = bm[c * 8:(c + 1) * 8]
        groups = []
        for j in range(nb // 2):
            a = np.where(rows[:, 2 * j])[0]
            b = np.where(rows[:, 2 * j + 1])[0]
            if len(a) == 0 and len(b) == 0:
                continue
            if len(a) and len(b):
                r0 = int(min(a[0], b[0])); r1 = int(max(a[-1], b[-1]))
                segs = []
                cur = None
                for r in range(r0, r1 + 1):
                    key = (r in a, r in b)
                    if cur is None or cur[2] != key:
                        if cur is not None:
                            segs.append(tuple(cur))
                        cur = [r, r, key]
                    else:
                        cur[1] = r
                segs.append(tuple(cur))
                segs = [(s0, s1, 0 if k == (True, True) else (1 if k == (True, False) else 2))
                        for (s0, s1, k) in segs]
                groups.append(dict(kind="pair", j=j, r0=r0, r1=r1, segs=segs))
            else:
                side = 0 if len(a) else 1
                rr = a if len(a) else b
                groups.append(dict(kind="single", j=j, side=side,
                                   r0=int(rr[0]), r1=int(rr[-1])))
        plan.append(groups)
    return plan


ATTN_PLAN = _attn_plan()


class Sem:
    __slots__ = ("h", "n", "name")

    def __init__(self, handle, name):
        self.h = handle
        self.n = 0
        self.name = name


def build_nc(debug=False):
    nc = bass.Bass("TRN2", target_bir_lowering=False, debug=False)

    def din(name, shape, dt):
        return nc.dram_tensor(name, shape, dt, kind="ExternalInput").ap()

    def dout(name, shape, dt):
        return nc.dram_tensor(name, shape, dt, kind="ExternalOutput").ap()

    def dint(name, shape, dt):
        if debug:
            return dout(name, shape, dt)
        return nc.dram_tensor(name, shape, dt).ap()

    xT = din("xT", [C, T], F32)
    xoT = din("xoT", [C, OWN], F32)
    wqT = din("wqT", [C, CG], F32)
    wkT = din("wkT", [C, CG], F32)
    wvT = din("wvT", [C, CG], F32)
    woT = din("woT", [C, C], F32)
    WT = din("WT", [C, C], F32)
    brow = din("brow", [1, C], F32)
    AstkT = din("AstkT", [C, 128], F32)
    BstkT = din("BstkT", [128, C], F32)
    gwT = din("gwT", [C, 8], F32)
    gbrow = din("gbrow", [1, 8], F32)
    cosT = din("cosT", [128, T], BF16)
    sinT = din("sinT", [128, T], BF16)
    c_ones = din("c_ones", [128, 1], F32)
    c_masks = din("c_masks", [128, 3], F32)
    c_E8 = din("c_E8", [8, 128], F32)
    c_id = din("c_id", [128, 128], F32)
    c_ones512 = din("c_ones512", [1, 512], F32)
    c_eps = din("c_eps", [1, 1], F32)

    outT = dout("outT", [C, OWN], F32)

    qbuf = dint("qbuf", [CG, T], BF16)
    kbuf = dint("kbuf", [CG, T], BF16)
    vbuf = dint("vbuf", [128, 32 * CG], BF16)
    ctxin = nc.dram_tensor("ctxin", [8 * CG, TS], BF16).ap()
    ctxout = nc.dram_tensor("ctxout", [16 * CG, TS], BF16).ap()
    ctxout_dbg = dout("ctxout", [16 * CG, TS], BF16) if False else None
    if debug:
        ctxout_dbg = dout("ctxout_dbg", [16 * CG, TS], BF16)
        d_xh2 = dout("d_xh2", [128, NCH * TS], BF16)
        d_w8 = dout("d_w8", [8, TS], F32)
        d_xab = dout("d_xab", [128, TS], BF16)
        d_y4 = dout("d_y4", [128, NCH * TS], F32)
        d_rows = dout("d_rows", [1, 4 * TS], F32)
        d_gate = dout("d_gate", [128, 64], F32)
    x2buf = dint("x2buf", [128, NCH * OWN], F32)
    agw_i = nc.dram_tensor("agw_i", [8, 512], BF16).ap()
    agw_o = nc.dram_tensor("agw_o", [16, 512], BF16).ap()

    ops = {e: [] for e in ENGS}

    def on(eng, fn):
        ops[eng].append(fn)

    ctx_stack = ExitStack()
    E = ctx_stack.enter_context

    sems = {}
    for nm in ("ste", "ssc", "sve", "scc", "sbar",
               "sx0", "sx1", "sldg", "slds", "sst", "sa0", "sa1", "sb0", "sb1"):
        sems[nm] = Sem(E(nc.semaphore(nm)), nm)
    ste, ssc, sve = sems["ste"], sems["ssc"], sems["sve"]
    scc, sbar = sems["scc"], sems["sbar"]
    sx = [sems["sx0"], sems["sx1"]]
    sldg, slds, sst = sems["sldg"], sems["slds"], sems["sst"]
    sa = [sems["sa0"], sems["sa1"]]
    sb = [sems["sb0"], sems["sb1"]]

    def W(eng, sem, val):
        assert val <= sem.n, f"wait {sem.name}>={val} but only {sem.n} planned"
        on(eng, lambda e, s=sem.h, v=val: e.wait_ge(s, v))

    nbar = [0]

    def barrier():
        nbar[0] += 1
        tgt = 5 * nbar[0]
        for eng in ENGS:
            on(eng, lambda e: e.nop().then_inc(sbar.h, 1))
        sbar.n += 5
        for eng in ENGS:
            W(eng, sbar, tgt)

    ONEB = E(nc.sbuf_tensor("ONEB", [128, 1], BF16))
    ONEF = E(nc.sbuf_tensor("ONEF", [128, 1], F32))
    ONERF = E(nc.sbuf_tensor("ONERF", [1, 128], F32))
    ONE1F = E(nc.sbuf_tensor("ONE1F", [1, 1], F32))
    OR512B = E(nc.sbuf_tensor("OR512B", [1, 512], BF16))
    MASKC = E(nc.sbuf_tensor("MASKC", [128, 3], F32))
    E8F = E(nc.sbuf_tensor("E8F", [8, 128], F32))
    ID128 = E(nc.sbuf_tensor("ID128", [128, 128], F32))
    GBB = E(nc.sbuf_tensor("GBB", [128, 8], F32))
    GBR = E(nc.sbuf_tensor("GBR", [1, 8], F32))
    RSTD = E(nc.sbuf_tensor("RSTD", [1, T], F32))
    RSTDC = E(nc.sbuf_tensor("RSTDC", [128, 32], F32))
    ROWA = E(nc.sbuf_tensor("ROWA", [1, 512], F32))
    ROWB = E(nc.sbuf_tensor("ROWB", [1, 512], F32))
    ROWC = E(nc.sbuf_tensor("ROWC", [1, 512], F32))
    EPSC = E(nc.sbuf_tensor("EPSC", [1, 1], F32))

    def ld0(sy):
        sy.dma_start(out=ONEF[:, :], in_=c_ones).then_inc(slds.h, 16)
        sy.dma_start(out=ONERF[:, :], in_=c_ones.rearrange("p o -> o p")).then_inc(slds.h, 16)
        sy.dma_start(out=ONE1F[:, :], in_=c_ones[0:1, :]).then_inc(slds.h, 16)
        sy.dma_start(out=MASKC[:, :], in_=c_masks).then_inc(slds.h, 16)
        sy.dma_start(out=E8F[:, :], in_=c_E8).then_inc(slds.h, 16)
        sy.dma_start(out=ID128[:, :], in_=c_id).then_inc(slds.h, 16)
        sy.dma_start(out=GBR[:, :], in_=gbrow).then_inc(slds.h, 16)
        sy.dma_start(out=EPSC[:, :], in_=c_eps).then_inc(slds.h, 16)
    on("sync", ld0)
    slds.n += 8 * 16

    def ld0g(g):
        g.dma_start(out=ONEB[:, :], in_=c_ones).then_inc(sldg.h, 16)
        g.dma_start(out=OR512B[:, :], in_=c_ones512).then_inc(sldg.h, 16)
        g.collective_compute("AllGather", ALU.bypass, replica_groups=PAIRS,
                             ins=[agw_i[:, :]], outs=[agw_o[:, :]]).then_inc(scc.h, 1)
    on("gpsimd", ld0g)
    sldg.n += 2 * 16
    scc.n += 1

    S0p = ExitStack()
    PB0 = S0p.enter_context(nc.psum_tensor("PB0", [128, 8], F32))
    W("tensor", slds, slds.n)
    on("tensor", lambda e: e.matmul(PB0[:, :], ONERF[:, :], GBR[:, :],
                                    start=True, stop=True).then_inc(ste.h, 1))
    ste.n += 1
    W("vector", ste, ste.n)
    on("vector", lambda e: e.tensor_copy(out=GBB[:, :], in_=PB0[:, :]).then_inc(sve.h, 1))
    sve.n += 1
    S0p.close()

    with ExitStack() as S1:
        e1 = S1.enter_context
        WQ = e1(nc.sbuf_tensor("WQ", [128, NCH * CG], BF16))
        WK = e1(nc.sbuf_tensor("WK", [128, NCH * CG], BF16))
        WV = e1(nc.sbuf_tensor("WV", [128, NCH * CG], BF16))
        XBF = [e1(nc.sbuf_tensor(f"XBF{i}", [128, NCH * TS], BF16)) for i in range(2)]
        XSQ = e1(nc.sbuf_tensor("XSQ", [128, NCH * TS], BF16))
        QST = e1(nc.sbuf_tensor("QST", [128, HG * TS], BF16))
        KST = e1(nc.sbuf_tensor("KST", [128, HG * TS], BF16))
        VST = e1(nc.sbuf_tensor("VST", [128, 4 * CG], BF16))
        NEGM = [e1(nc.sbuf_tensor(f"NEGM{i}", [1, TS], BF16)) for i in range(2)]
        SQR = e1(nc.sbuf_tensor("SQR", [1, CG], BF16))
        SKR = e1(nc.sbuf_tensor("SKR", [1, CG], BF16))
        SVR = e1(nc.sbuf_tensor("SVR", [1, CG], BF16))
        Qp = [e1(nc.psum_tensor(f"QpA{i}", [128, 512], F32)) for i in range(2)]
        Kp = [e1(nc.psum_tensor(f"KpA{i}", [128, 512], F32)) for i in range(2)]
        Vp = e1(nc.psum_tensor("VpA", [128, 512], F32))
        Sump = e1(nc.psum_tensor("SumpA", [1, 512], F32))
        Sqp = e1(nc.psum_tensor("SqpA", [1, 512], F32))
        Colp = e1(nc.psum_tensor("ColpA", [128, 4], F32))

        def wload(g):
            g.dma_start(out=WQ[:, :].rearrange("d (k o) -> d k o", k=NCH),
                        in_=wqT.rearrange("(k d) o -> d k o", d=128)).then_inc(sldg.h, 16)
            g.dma_start(out=WK[:, :].rearrange("d (k o) -> d k o", k=NCH),
                        in_=wkT.rearrange("(k d) o -> d k o", d=128)).then_inc(sldg.h, 16)
            g.dma_start(out=WV[:, :].rearrange("d (k o) -> d k o", k=NCH),
                        in_=wvT.rearrange("(k d) o -> d k o", d=128)).then_inc(sldg.h, 16)
        on("gpsimd", wload)
        sldg.n += 3 * 16

        W("tensor", sldg, sldg.n)
        for (Wt, Srow) in ((WQ, SQR), (WK, SKR), (WV, SVR)):
            for sl in range(2):
                def smm(e, Wt=Wt, sl=sl):
                    for dc in range(NCH):
                        ins = e.matmul(Sump[:, :], ONEB[:, :],
                                       Wt[:, dc * CG + sl * 512: dc * CG + (sl + 1) * 512],
                                       start=(dc == 0), stop=(dc == NCH - 1))
                    ins.then_inc(ste.h, 1)
                on("tensor", smm)
                ste.n += 1
                W("vector", ste, ste.n)
                on("vector", lambda e, Srow=Srow, sl=sl: e.tensor_copy(
                    out=Srow[:, sl * 512:(sl + 1) * 512], in_=Sump[:, :]).then_inc(sve.h, 1))
                sve.n += 1
                W("tensor", sve, sve.n)

        xload = [None] * NT
        rows_sc = [None] * NT
        vrows_ve = [None] * NT
        stats_te = [None] * NT
        qkv_te_done = [None] * NT
        store_id = [None] * NT
        evacQ, evacK, evacV = {}, {}, {}

        def issue_xload(j):
            def f(g, j=j):
                g.dma_start(out=XBF[j % 2][:, :].rearrange("d (k n) -> d k n", k=NCH),
                            in_=xT.rearrange("(k d) n -> d k n", d=128)[:, :, j * TS:(j + 1) * TS]
                            ).then_inc(sx[j % 2].h, 16)
            on("gpsimd", f)
            sx[j % 2].n += 16
            xload[j] = sx[j % 2].n

        issue_xload(0)
        issue_xload(1)

        def emit_stats(j):
            W("scalar", sx[j % 2], xload[j])
            if j > 0:
                W("scalar", ste, stats_te[j - 1])
            on("scalar", lambda e, j=j: e.activation(
                out=XSQ[:, :], in_=XBF[j % 2][:, :], func=AF.Square).then_inc(ssc.h, 1))
            ssc.n += 1
            sq_done = ssc.n
            W("tensor", sx[j % 2], xload[j])
            if j > 0:
                W("tensor", sve, vrows_ve[j - 1])
            def summ(e, j=j):
                for dc in range(NCH):
                    ins = e.matmul(Sump[:, :], ONEB[:, :], XBF[j % 2][:, dc * TS:(dc + 1) * TS],
                                   start=(dc == 0), stop=(dc == NCH - 1))
                ins.then_inc(ste.h, 1)
            on("tensor", summ)
            ste.n += 1
            W("tensor", ssc, sq_done)
            def sqm(e):
                for dc in range(NCH):
                    ins = e.matmul(Sqp[:, :], ONEB[:, :], XSQ[:, dc * TS:(dc + 1) * TS],
                                   start=(dc == 0), stop=(dc == NCH - 1))
                ins.then_inc(ste.h, 1)
            on("tensor", sqm)
            ste.n += 1
            stats_te[j] = ste.n
            W("vector", ste, stats_te[j])
            if j > 0:
                W("vector", ssc, rows_sc[j - 1])
            def vrows(e, j=j):
                e.tensor_scalar(out=ROWC[:, :], in0=Sump[:, :],
                                scalar1=1.0 / C, scalar2=None, op0=ALU.mult)
                e.tensor_scalar(out=ROWA[:, :], in0=Sqp[:, :],
                                scalar1=1.0 / C, scalar2=None, op0=ALU.mult)
                e.tensor_tensor(out=ROWB[:, :], in0=ROWC[:, :], in1=ROWC[:, :], op=ALU.mult)
                e.tensor_tensor(out=ROWA[:, :], in0=ROWA[:, :], in1=ROWB[:, :], op=ALU.subtract)
                e.tensor_scalar(out=NEGM[j % 2][:, :], in0=ROWC[:, :],
                                scalar1=-1.0, scalar2=None, op0=ALU.mult).then_inc(sve.h, 1)
            on("vector", vrows)
            sve.n += 1
            vrows_ve[j] = sve.n
            W("scalar", sve, sve.n)
            def srows(e, j=j):
                e.activation(out=ROWB[:, :], in_=ROWA[:, :], func=AF.Ln, bias=EPSC[:, 0:1])
                e.activation(out=RSTD[:, j * TS:(j + 1) * TS], in_=ROWB[:, :],
                             func=AF.Exp, scale=-0.5).then_inc(ssc.h, 1)
            on("scalar", srows)
            ssc.n += 1
            rows_sc[j] = ssc.n

        def emit_qkv(j):
            W("tensor", ssc, rows_sc[j])
            def colmm(e, j=j):
                for tc in range(4):
                    ins = e.matmul(Colp[:, tc:tc + 1],
                                   RSTD[:, j * TS + tc * 128: j * TS + (tc + 1) * 128],
                                   ONE1F[:, :], start=True, stop=True)
                ins.then_inc(ste.h, 1)
            on("tensor", colmm)
            ste.n += 1
            W("vector", ste, ste.n)
            on("vector", lambda e, j=j: e.tensor_copy(
                out=RSTDC[:, j * 4:(j + 1) * 4], in_=Colp[:, :]).then_inc(sve.h, 1))
            sve.n += 1
            col_ve = sve.n
            if j > 0:
                W("scalar", sst, store_id[j - 1])
            for (Wt, Pp, DST, Srow, ed) in ((WQ, Qp, QST, SQR, evacQ), (WK, Kp, KST, SKR, evacK)):
                for oc in range(HG):
                    if (j, oc - 2) in ed:
                        W("tensor", ssc, ed[(j, oc - 2)])
                    def qmm(e, j=j, oc=oc, Wt=Wt, Pp=Pp, Srow=Srow):
                        bank = Pp[oc % 2]
                        for dc in range(NCH):
                            e.matmul(bank[:, :],
                                     Wt[:, dc * CG + oc * 128: dc * CG + (oc + 1) * 128],
                                     XBF[j % 2][:, dc * TS:(dc + 1) * TS],
                                     start=(dc == 0), stop=False)
                        e.matmul(bank[:, :], Srow[:, oc * 128:(oc + 1) * 128],
                                 NEGM[j % 2][:, :],
                                 start=False, stop=True).then_inc(ste.h, 1)
                    on("tensor", qmm)
                    ste.n += 1
                    W("scalar", ste, ste.n)
                    on("scalar", lambda e, oc=oc, Pp=Pp, DST=DST: e.copy(
                        out=DST[:, oc * TS:(oc + 1) * TS],
                        in_=Pp[oc % 2][:, :]).then_inc(ssc.h, 1))
                    ssc.n += 1
                    ed[(j, oc)] = ssc.n
            for tc in range(4):
                for os_ in range(2):
                    if (j, tc, os_) != (j, 0, 0) and (j, (tc, os_)) not in ():
                        pass
                    key_prev = (tc, os_ - 1) if os_ > 0 else (tc - 1, 1)
                    if key_prev[0] >= 0 and (j,) + key_prev in evacV:
                        W("tensor", ssc, evacV[(j,) + key_prev])
                    def vmm(e, j=j, tc=tc, os_=os_):
                        for dc in range(NCH):
                            e.matmul(Vp[:, :],
                                     XBF[j % 2][:, dc * TS + tc * 128: dc * TS + (tc + 1) * 128],
                                     WV[:, dc * CG + os_ * 512: dc * CG + (os_ + 1) * 512],
                                     start=(dc == 0), stop=False)
                        e.matmul(Vp[:, :],
                                 NEGM[j % 2][:, tc * 128:(tc + 1) * 128],
                                 SVR[:, os_ * 512:(os_ + 1) * 512],
                                 start=False, stop=True).then_inc(ste.h, 1)
                    on("tensor", vmm)
                    ste.n += 1
                    W("scalar", ste, ste.n)
                    W("scalar", sve, col_ve)
                    on("scalar", lambda e, j=j, tc=tc, os_=os_: e.activation(
                        out=VST[:, tc * CG + os_ * 512: tc * CG + (os_ + 1) * 512],
                        in_=Vp[:, :], func=AF.Copy,
                        scale=RSTDC[:, j * 4 + tc: j * 4 + tc + 1]).then_inc(ssc.h, 1))
                    ssc.n += 1
                    evacV[(j, tc, os_)] = ssc.n
            qkv_te_done[j] = ste.n
            W("sync", ssc, ssc.n)
            def store(sy, j=j):
                sy.dma_start(out=qbuf.rearrange("(h d) n -> d h n", d=128)[:, :, j * TS:(j + 1) * TS],
                             in_=QST[:, :].rearrange("d (h n) -> d h n", h=HG)).then_inc(sst.h, 16)
                sy.dma_start(out=kbuf.rearrange("(h d) n -> d h n", d=128)[:, :, j * TS:(j + 1) * TS],
                             in_=KST[:, :].rearrange("d (h n) -> d h n", h=HG)).then_inc(sst.h, 16)
                sy.dma_start(out=vbuf[:, j * 4 * CG:(j + 1) * 4 * CG],
                             in_=VST[:, :]).then_inc(sst.h, 16)
            on("sync", store)
            sst.n += 3 * 16
            store_id[j] = sst.n
            if j + 2 < NT:
                W("gpsimd", ste, qkv_te_done[j])
                issue_xload(j + 2)

        for j in range(NT):
            emit_stats(j)
            if j > 0:
                emit_qkv(j - 1)
        emit_qkv(NT - 1)
        W("sync", sst, sst.n)
    barrier()

    with ExitStack() as S2:
        e2 = S2.enter_context
        KH = e2(nc.sbuf_tensor("KH", [128, HG * T], BF16))
        VS = e2(nc.sbuf_tensor("VS", [128, 32 * CG], BF16))
        CB = e2(nc.sbuf_tensor("CB", [128, 512], BF16))
        SB = e2(nc.sbuf_tensor("SB", [128, 512], BF16))
        CS = e2(nc.sbuf_tensor("CS", [128, 512], BF16))
        SS = e2(nc.sbuf_tensor("SS", [128, 512], BF16))
        CSK = e2(nc.sbuf_tensor("CSK", [128, 512], BF16))
        SSK = e2(nc.sbuf_tensor("SSK", [128, 512], BF16))
        T1 = e2(nc.sbuf_tensor("T1", [128, 512], BF16))
        T2 = e2(nc.sbuf_tensor("T2", [128, 512], BF16))
        KW = e2(nc.sbuf_tensor("KW", [128, HG * TS], BF16))
        Rbp = e2(nc.psum_tensor("RbpB", [128, 512], F32))

        on("sync", lambda sy: sy.dma_start(out=VS[:, :], in_=vbuf[:, :]).then_inc(slds.h, 16))
        slds.n += 16
        vs_ld = slds.n

        QT = e2(nc.sbuf_tensor("QT", [128, HG * TS], BF16))
        QW = e2(nc.sbuf_tensor("QW", [128, HG * TS], BF16))
        QH = e2(nc.sbuf_tensor("QH", [128, HG * TS], BF16))
        PT = [e2(nc.sbuf_tensor(f"PT{i}", [128, 512], BF16)) for i in range(3)]
        IDR = [e2(nc.sbuf_tensor(f"IDR{i}", [1, 512], F32)) for i in range(2)]
        CTU = e2(nc.sbuf_tensor("CTU", [128, HG * TS], BF16))
        SC = [e2(nc.psum_tensor(f"ScpC{i}", [128, 512], F32)) for i in range(3)]
        Ctp = [e2(nc.psum_tensor(f"CtpC{i}", [128, 512], F32)) for i in range(2)]
        Dnp = [e2(nc.psum_tensor(f"DnpC{i}", [1, 512], F32)) for i in range(2)]

        W("tensor", slds, vs_ld)

        exp_hist = []
        smm_hist = []
        cd_hist = []
        smm_last = {}
        ldq_id = {}
        qrope_ve = {}
        recip_ve = {}
        rb2_te = {}
        vmul_ve = {}
        stctx_id = {}
        ctu_sc = {}
        cd_last = {}
        tail_q = []
        rbb_free = [0]

        def emit_tail_step(cur_gh=None):
            if tail_q and (cur_gh is None or tail_q[0][0] <= cur_gh - 2):
                tail_q.pop(0)[1]()

        def mk_tail(gh):
            def f(gh=gh):
                cch, h = gh // 8, gh % 8
                hb = gh % 2
                W("tensor", sve, recip_ve[gh])
                if rbb_free[0]:
                    W("tensor", sve, rbb_free[0])
                on("tensor", lambda e, hb=hb: e.matmul(
                    Rbp[:, :], ONERF[:, :], IDR[hb][:, :],
                    start=True, stop=True).then_inc(ste.h, 1))
                ste.n += 1
                rb2_te[gh] = ste.n
                W("vector", ste, ste.n)
                W("vector", ssc, ctu_sc[gh])
                on("vector", lambda e, h=h: e.tensor_tensor(
                    out=CTU[:, h * TS:(h + 1) * TS], in0=CTU[:, h * TS:(h + 1) * TS],
                    in1=Rbp[:, :], op=ALU.mult).then_inc(sve.h, 1))
                sve.n += 1
                vmul_ve[gh] = sve.n
                rbb_free[0] = sve.n
                W("sync", sve, sve.n)
                def stc(sy, cch=cch, h=h):
                    sy.dma_start(out=ctxin[cch * CG + h * 128: cch * CG + (h + 1) * 128, :],
                                 in_=CTU[:, h * TS:(h + 1) * TS]).then_inc(sst.h, 16)
                on("sync", stc)
                sst.n += 16
                stctx_id[gh] = sst.n
                if h == 7:
                    W("gpsimd", sst, sst.n)
                    on("gpsimd", lambda g, cch=cch: g.collective_compute(
                        "AllGather", ALU.bypass, replica_groups=PAIRS,
                        ins=[ctxin[cch * CG:(cch + 1) * CG, :]],
                        outs=[ctxout[cch * 2 * CG:(cch + 1) * 2 * CG, :]]).then_inc(scc.h, 1))
                    scc.n += 1
            return f

        def emit_ldq(c2):
            cl2 = c2 * TS
            def f(sy, cl=cl2):
                sy.dma_start(out=CS[:, :], in_=cosT[:, cl:cl + TS]).then_inc(sa[0].h, 16)
                sy.dma_start(out=SS[:, :], in_=sinT[:, cl:cl + TS]).then_inc(sa[0].h, 16)
                sy.dma_start(out=QT[:, :].rearrange("d (h n) -> d h n", h=HG),
                             in_=qbuf.rearrange("(h d) n -> d h n", d=128)[:, :, cl:cl + TS]
                             ).then_inc(sa[0].h, 16)
                qb4 = qbuf.rearrange("(h s dd) n -> s dd h n", h=HG, s=2)
                sy.dma_start(out=QW[0:64, :].rearrange("d (h n) -> d h n", h=HG),
                             in_=qb4[1, :, :, cl:cl + TS]).then_inc(sa[0].h, 16)
                sy.dma_start(out=QW[64:128, :].rearrange("d (h n) -> d h n", h=HG),
                             in_=qb4[0, :, :, cl:cl + TS]).then_inc(sa[0].h, 16)
            on("sync", f)
            sa[0].n += 5 * 16
            ldq_id[c2] = sa[0].n

        ldk_id = {}
        ropeK_ve = {}
        rbK_te = {}

        def emit_ldk(j):
            cl2 = j * TS
            if (j - 1) in ropeK_ve:
                W("sync", sve, ropeK_ve[j - 1])
            def f(sy, cl=cl2):
                sy.dma_start(out=CSK[:, :], in_=cosT[:, cl:cl + TS]).then_inc(slds.h, 16)
                sy.dma_start(out=SSK[:, :], in_=sinT[:, cl:cl + TS]).then_inc(slds.h, 16)
                kbh = kbuf.rearrange("(h d) n -> d h n", d=128)
                for h2 in range(HG):
                    sy.dma_start(out=KH[:, h2 * T + cl: h2 * T + cl + TS],
                                 in_=kbh[:, h2, cl:cl + TS]).then_inc(slds.h, 16)
                kb4 = kbuf.rearrange("(h s dd) n -> s dd h n", h=HG, s=2)
                sy.dma_start(out=KW[0:64, :].rearrange("d (h n) -> d h n", h=HG),
                             in_=kb4[1, :, :, cl:cl + TS]).then_inc(slds.h, 16)
                sy.dma_start(out=KW[64:128, :].rearrange("d (h n) -> d h n", h=HG),
                             in_=kb4[0, :, :, cl:cl + TS]).then_inc(slds.h, 16)
            on("sync", f)
            slds.n += 12 * 16
            ldk_id[j] = slds.n

        def emit_ropeK(j):
            cl2 = j * TS
            if rbb_free[0]:
                W("tensor", sve, rbb_free[0])
            on("tensor", lambda e, cl=cl2: e.matmul(
                Rbp[:, :], ONERF[:, :], RSTD[:, cl:cl + TS],
                start=True, stop=True).then_inc(ste.h, 1))
            ste.n += 1
            rbK_te[j] = ste.n
            W("vector", slds, ldk_id[j])
            W("vector", ste, rbK_te[j])
            def vropeK(e, cl=cl2):
                e.tensor_tensor(out=CB[:, :], in0=CSK[:, :], in1=Rbp[:, :], op=ALU.mult)
                e.tensor_tensor(out=SB[:, :], in0=SSK[:, :], in1=Rbp[:, :], op=ALU.mult)
                for h in range(HG):
                    e.tensor_tensor(out=T1[:, :], in0=KH[:, h * T + cl: h * T + cl + TS],
                                    in1=CB[:, :], op=ALU.mult)
                    e.tensor_tensor(out=T2[:, :], in0=KW[:, h * TS:(h + 1) * TS],
                                    in1=SB[:, :], op=ALU.mult)
                    ins = e.tensor_tensor(out=KH[:, h * T + cl: h * T + cl + TS],
                                          in0=T1[:, :], in1=T2[:, :], op=ALU.add)
                ins.then_inc(sve.h, 1)
            on("vector", vropeK)
            sve.n += 1
            ropeK_ve[j] = sve.n
            rbb_free[0] = sve.n

        emit_ldk(0)
        emit_ropeK(0)
        emit_ldq(0)
        for cch in range(8):
            if rbb_free[0]:
                W("tensor", sve, rbb_free[0])
            on("tensor", lambda e, cl=cch * TS: e.matmul(
                Rbp[:, :], ONERF[:, :], RSTD[:, cl:cl + TS],
                start=True, stop=True).then_inc(ste.h, 1))
            ste.n += 1
            rbq_te = ste.n
            W("vector", sa[0], ldq_id[cch])
            W("vector", ste, rbq_te)
            if cch >= 1:
                W("vector", ste, smm_last[cch - 1])
            def qrope(e):
                e.tensor_tensor(out=CB[:, :], in0=CS[:, :], in1=Rbp[:, :], op=ALU.mult)
                e.tensor_tensor(out=SB[:, :], in0=SS[:, :], in1=Rbp[:, :], op=ALU.mult)
                for h in range(HG):
                    e.tensor_tensor(out=T1[:, :], in0=QT[:, h * TS:(h + 1) * TS],
                                    in1=CB[:, :], op=ALU.mult)
                    e.tensor_tensor(out=T2[:, :], in0=QW[:, h * TS:(h + 1) * TS],
                                    in1=SB[:, :], op=ALU.mult)
                    ins = e.tensor_tensor(out=QH[:, h * TS:(h + 1) * TS],
                                          in0=T1[:, :], in1=T2[:, :], op=ALU.add)
                ins.then_inc(sve.h, 1)
            on("vector", qrope)
            sve.n += 1
            qrope_ve[cch] = sve.n
            rbb_free[0] = sve.n
            if cch + 1 < 8:
                W("sync", sve, qrope_ve[cch])
                emit_ldq(cch + 1)

            groups = ATTN_PLAN[cch]
            ng = len(groups)
            W("tensor", sve, qrope_ve[cch])
            W("tensor", sve, ropeK_ve[cch])
            for h in range(HG):
                gh = cch * 8 + h
                hb = gh % 2
                if h == 1 and cch + 1 < 8:
                    emit_ldk(cch + 1)
                    emit_ropeK(cch + 1)
                for gi, g in enumerate(groups):
                    if (gi % 2) == 0:
                        emit_tail_step(gh)
                    cnt = len(smm_hist)
                    bank = SC[cnt % 3]
                    q0, q1 = g["r0"] * 64, (g["r1"] + 1) * 64
                    if len(exp_hist) >= 3:
                        W("tensor", ssc, exp_hist[-3])
                    if g["kind"] == "pair":
                        def smm(e, h=h, j2=g["j"], q0=q0, q1=q1, bank=bank):
                            e.matmul(bank[:, q0:q1],
                                     KH[:, h * T + j2 * 128: h * T + (j2 + 1) * 128],
                                     QH[:, h * TS + q0: h * TS + q1],
                                     start=True, stop=True).then_inc(ste.h, 1)
                    else:
                        def smm(e, h=h, j2=g["j"], sd=g["side"], q0=q0, q1=q1, bank=bank):
                            e.matmul(bank[sd * 64:sd * 64 + 64, q0:q1],
                                     KH[:, h * T + j2 * 128 + sd * 64: h * T + j2 * 128 + sd * 64 + 64],
                                     QH[:, h * TS + q0: h * TS + q1],
                                     start=True, stop=True).then_inc(ste.h, 1)
                    on("tensor", smm)
                    ste.n += 1
                    smm_hist.append(ste.n)
                    W("scalar", ste, ste.n)
                    if len(cd_hist) >= 3:
                        W("scalar", ste, cd_hist[-3])
                    pt = PT[cnt % 3]
                    if g["kind"] == "pair":
                        def sexp(e, g=g, bank=bank, pt=pt):
                            for (s0, s1, bi) in g["segs"]:
                                a0, a1 = s0 * 64, (s1 + 1) * 64
                                if bi == 0:
                                    ins = e.activation(out=pt[:, a0:a1], in_=bank[:, a0:a1],
                                                       func=AF.Exp, scale=SCALE)
                                else:
                                    ins = e.activation(out=pt[:, a0:a1], in_=bank[:, a0:a1],
                                                       func=AF.Exp, scale=SCALE,
                                                       bias=MASKC[:, bi:bi + 1])
                            ins.then_inc(ssc.h, 1)
                    else:
                        def sexp(e, g=g, bank=bank, pt=pt, q0=q0, q1=q1):
                            sd = g["side"]
                            e.activation(out=pt[sd * 64:sd * 64 + 64, q0:q1],
                                         in_=bank[sd * 64:sd * 64 + 64, q0:q1],
                                         func=AF.Exp, scale=SCALE).then_inc(ssc.h, 1)
                    on("scalar", sexp)
                    ssc.n += 1
                    exp_hist.append(ssc.n)
                    W("tensor", ssc, ssc.n)
                    first, last = (gi == 0), (gi == ng - 1)
                    if first:
                        if (gh - 2) in ctu_sc:
                            W("tensor", ssc, ctu_sc[gh - 2])
                        if (gh - 2) in recip_ve:
                            W("tensor", sve, recip_ve[gh - 2])
                    if g["kind"] == "pair":
                        def cmm(e, j2=g["j"], h=h, q0=q0, q1=q1, pt=pt, first=first,
                                last=last, hb=hb):
                            e.matmul(Ctp[hb][:, q0:q1],
                                     VS[:, j2 * CG + h * 128: j2 * CG + (h + 1) * 128],
                                     pt[:, q0:q1], start=first, stop=last)
                            e.matmul(Dnp[hb][:, q0:q1], ONEB[:, :], pt[:, q0:q1],
                                     start=first, stop=last).then_inc(ste.h, 1)
                    else:
                        def cmm(e, j2=g["j"], sd=g["side"], h=h, q0=q0, q1=q1, pt=pt,
                                first=first, last=last, hb=hb):
                            e.matmul(Ctp[hb][:, q0:q1],
                                     VS[sd * 64:sd * 64 + 64, j2 * CG + h * 128: j2 * CG + (h + 1) * 128],
                                     pt[sd * 64:sd * 64 + 64, q0:q1], start=first, stop=last)
                            e.matmul(Dnp[hb][:, q0:q1], ONEB[sd * 64:sd * 64 + 64, :],
                                     pt[sd * 64:sd * 64 + 64, q0:q1],
                                     start=first, stop=last).then_inc(ste.h, 1)
                    on("tensor", cmm)
                    ste.n += 1
                    cd_hist.append(ste.n)
                cd_last[gh] = cd_hist[-1]
                W("scalar", ste, cd_last[gh])
                if (gh - 8) in stctx_id:
                    W("scalar", sst, stctx_id[gh - 8])
                on("scalar", lambda e, h=h, hb=hb: e.copy(
                    out=CTU[:, h * TS:(h + 1) * TS], in_=Ctp[hb][:, :]).then_inc(ssc.h, 1))
                ssc.n += 1
                ctu_sc[gh] = ssc.n
                W("vector", ste, cd_last[gh])
                if (gh - 2) in rb2_te:
                    W("vector", ste, rb2_te[gh - 2])
                on("vector", lambda e, hb=hb: e.reciprocal(
                    out=IDR[hb][:, :], in_=Dnp[hb][:, :]).then_inc(sve.h, 1))
                sve.n += 1
                recip_ve[gh] = sve.n
                tail_q.append((gh, mk_tail(gh)))
            smm_last[cch] = smm_hist[-1]
        while tail_q:
            emit_tail_step()
    barrier()

    CORR2 = E(nc.sbuf_tensor("CORR2", [1, OWN], F32))
    with ExitStack() as S5:
        e5 = S5.enter_context
        WO = e5(nc.sbuf_tensor("WO", [128, NCH * C], BF16))
        CTF = [e5(nc.sbuf_tensor(f"CTF{i}", [128, NCH * TS], BF16)) for i in range(2)]
        XO = e5(nc.sbuf_tensor("XO", [128, NCH * TS], F32))
        YF = e5(nc.sbuf_tensor("YF", [128, NCH * TS], F32))
        YSQ = [e5(nc.sbuf_tensor(f"YSQ{i}", [128, 512], BF16)) for i in range(2)]
        CSO = e5(nc.sbuf_tensor("CSO", [128, NCH], BF16))
        CSOF = e5(nc.sbuf_tensor("CSOF", [128, NCH], F32))
        RB5 = e5(nc.sbuf_tensor("RB5", [128, 512], F32))
        Yp = [e5(nc.psum_tensor(f"YpD{i}", [128, 512], F32)) for i in range(2)]
        Mnp = e5(nc.psum_tensor("MnpD", [1, 512], F32))
        Sqp5 = e5(nc.psum_tensor("Sqp5D", [1, 512], F32))
        Rb5p = e5(nc.psum_tensor("Rb5D", [128, 512], F32))

        evac5 = {}
        on("gpsimd", lambda g: g.dma_start(
            out=WO[:, :].rearrange("d (k o) -> d k o", k=NCH),
            in_=woT.rearrange("(k d) o -> d k o", d=128)).then_inc(sldg.h, 16))
        sldg.n += 16
        W("vector", sldg, sldg.n)
        def vcso(e):
            for cc in range(NCH):
                ins = e.reduce_sum(out=CSOF[:, cc:cc + 1], in_=WO[:, cc * C:(cc + 1) * C],
                                   axis=mybir.AxisListType.X)
            ins.then_inc(sve.h, 1)
        on("vector", vcso)
        sve.n += 1
        W("vector", sve, sve.n)
        on("vector", lambda e: e.tensor_copy(out=CSO[:, :], in_=CSOF[:, :]).then_inc(sve.h, 1))
        sve.n += 1
        cso_ve = sve.n

        W("sync", scc, scc.n)
        if debug:
            W("gpsimd", scc, scc.n)
            on("gpsimd", lambda g: g.dma_start(out=ctxout_dbg, in_=ctxout[:, :]).then_inc(sst.h, 16))
            sst.n += 16

        ctf_ld = [None] * NOT_
        xo_ld = [None] * NOT_
        ymm_done = [None] * NOT_
        fin_ve = [None] * NOT_

        def ld5_ctf(i):
            if i >= 2:
                W("sync", ste, ymm_done[i - 2])
            def f(sy, i=i):
                pid = sy.partition_id()
                with sy.register(f"rp5{i}") as rp:
                    sy.reg_mov(rp, pid)
                    sy.reg_mod(rp, rp, 2)
                    with sy.If_eq(rp, 0):
                        ch = i
                        sy.dma_start(out=CTF[i % 2][:, :].rearrange("d (k n) -> d k n", k=NCH),
                                     in_=ctxout[ch * 2 * CG:(ch + 1) * 2 * CG, :]
                                     .rearrange("(k d) n -> d k n", d=128)).then_inc(sa[1].h, 16)
                    with sy.Else():
                        ch = 4 + i
                        sy.dma_start(out=CTF[i % 2][:, :].rearrange("d (k n) -> d k n", k=NCH),
                                     in_=ctxout[ch * 2 * CG:(ch + 1) * 2 * CG, :]
                                     .rearrange("(k d) n -> d k n", d=128)).then_inc(sa[1].h, 16)
            on("sync", f)
            sa[1].n += 16
            ctf_ld[i] = sa[1].n

        def ld5_xo(i):
            if i >= 1:
                W("sync", sve, fin_ve[i - 1])
            def f(sy, i=i):
                sy.dma_start(out=XO[:, :].rearrange("d (k n) -> d k n", k=NCH),
                             in_=xoT.rearrange("(k d) n -> d k n", d=128)[:, :, i * TS:(i + 1) * TS]
                             ).then_inc(sb[0].h, 16)
            on("sync", f)
            sb[0].n += 16
            xo_ld[i] = sb[0].n

        ld5_ctf(0)
        ld5_xo(0)
        ld5_ctf(1)
        W("tensor", sldg, sldg.n)

        for i in range(NOT_):
            W("tensor", sa[1], ctf_ld[i])
            W("tensor", sve, cso_ve)
            if i > 0:
                W("tensor", sve, fin_ve[i - 1])
            def mmm(e, i=i):
                for cc in range(NCH):
                    ins = e.matmul(Mnp[:, :], CSO[:, cc:cc + 1],
                                   CTF[i % 2][:, cc * TS:(cc + 1) * TS],
                                   start=(cc == 0), stop=(cc == NCH - 1))
                ins.then_inc(ste.h, 1)
            on("tensor", mmm)
            ste.n += 1
            sq_te = {}
            for oc in range(NCH):
                if (i, oc - 2) in evac5:
                    W("tensor", ssc, evac5[(i, oc - 2)])
                def ymm(e, i=i, oc=oc):
                    bank = Yp[oc % 2]
                    for cc in range(NCH):
                        ins = e.matmul(bank[:, :],
                                       WO[:, cc * C + oc * 128: cc * C + (oc + 1) * 128],
                                       CTF[i % 2][:, cc * TS:(cc + 1) * TS],
                                       start=(cc == 0), stop=(cc == NCH - 1))
                    ins.then_inc(ste.h, 1)
                on("tensor", ymm)
                ste.n += 1
                W("scalar", ste, ste.n)
                if oc >= 2:
                    W("scalar", ste, sq_te[oc - 2])
                def yev(e, i=i, oc=oc):
                    bank = Yp[oc % 2]
                    e.activation(out=YSQ[oc % 2][:, :], in_=bank[:, :], func=AF.Square)
                    e.copy(out=YF[:, oc * TS:(oc + 1) * TS], in_=bank[:, :]).then_inc(ssc.h, 1)
                on("scalar", yev)
                ssc.n += 1
                evac5[(i, oc)] = ssc.n
                if oc >= 1:
                    W("tensor", ssc, evac5[(i, oc - 1)])
                    def sqm5(e, oc2=oc - 1):
                        e.matmul(Sqp5[:, :], ONEB[:, :], YSQ[oc2 % 2][:, :],
                                 start=(oc2 == 0), stop=False).then_inc(ste.h, 1)
                    on("tensor", sqm5)
                    ste.n += 1
                    sq_te[oc - 1] = ste.n
            W("tensor", ssc, evac5[(i, NCH - 1)])
            def sqm5f(e):
                e.matmul(Sqp5[:, :], ONEB[:, :], YSQ[(NCH - 1) % 2][:, :],
                         start=False, stop=True).then_inc(ste.h, 1)
            on("tensor", sqm5f)
            ste.n += 1
            sq_te[NCH - 1] = ste.n
            ymm_done[i] = ste.n
            W("vector", ste, ste.n)
            def v5rows(e, i=i):
                e.tensor_scalar(out=ROWA[:, :], in0=Mnp[:, :], scalar1=1.0 / C,
                                scalar2=None, op0=ALU.mult)
                e.tensor_scalar(out=ROWB[:, :], in0=Sqp5[:, :], scalar1=1.0 / C,
                                scalar2=None, op0=ALU.mult)
                e.tensor_tensor(out=ROWC[:, :], in0=ROWA[:, :], in1=ROWA[:, :], op=ALU.mult)
                e.tensor_tensor(out=ROWB[:, :], in0=ROWB[:, :], in1=ROWC[:, :],
                                op=ALU.subtract).then_inc(sve.h, 1)
            on("vector", v5rows)
            sve.n += 1
            W("scalar", sve, sve.n)
            def s5rows(e):
                e.activation(out=ROWB[:, :], in_=ROWB[:, :], func=AF.Ln, bias=EPSC[:, 0:1])
                e.activation(out=ROWC[:, :], in_=ROWB[:, :], func=AF.Exp,
                             scale=-0.5).then_inc(ssc.h, 1)
            on("scalar", s5rows)
            ssc.n += 1
            W("vector", ssc, ssc.n)
            def v5corr(e, i=i):
                e.scalar_tensor_tensor(out=CORR2[:, i * TS:(i + 1) * TS], in0=ROWA[:, :],
                                       scalar=-1.0, in1=ROWC[:, :],
                                       op0=ALU.mult, op1=ALU.mult).then_inc(sve.h, 1)
            on("vector", v5corr)
            sve.n += 1
            W("tensor", sve, sve.n)
            on("tensor", lambda e: e.matmul(Rb5p[:, :], ONERF[:, :], ROWC[:, :],
                                            start=True, stop=True).then_inc(ste.h, 1))
            ste.n += 1
            W("vector", ste, ste.n)
            W("vector", sb[0], xo_ld[i])
            def v5fin(e, i=i):
                e.tensor_copy(out=RB5[:, :], in_=Rb5p[:, :])
                for cc in range(NCH):
                    e.tensor_tensor(out=YF[:, cc * TS:(cc + 1) * TS],
                                    in0=YF[:, cc * TS:(cc + 1) * TS], in1=RB5[:, :], op=ALU.mult)
                for cc in range(NCH):
                    ins = e.tensor_tensor(out=YF[:, cc * TS:(cc + 1) * TS],
                                          in0=YF[:, cc * TS:(cc + 1) * TS],
                                          in1=XO[:, cc * TS:(cc + 1) * TS], op=ALU.add)
                ins.then_inc(sve.h, 1)
            on("vector", v5fin)
            sve.n += 1
            fin_ve[i] = sve.n
            W("sync", sve, sve.n)
            def st5(sy, i=i):
                sy.dma_start(out=x2buf.rearrange("d (k n) -> d k n", k=NCH)[:, :, i * TS:(i + 1) * TS],
                             in_=YF[:, :].rearrange("d (k n) -> d k n", k=NCH)).then_inc(sst.h, 16)
            on("sync", st5)
            sst.n += 16
            if i + 1 < NOT_:
                ld5_xo(i + 1)
                if i + 2 < NOT_:
                    ld5_ctf(i + 2)
                W("scalar", sve, fin_ve[i])
                W("scalar", sst, sst.n)
        W("sync", sst, sst.n)
    barrier()

    with ExitStack() as S6:
        e6 = S6.enter_context
        WTS = e6(nc.sbuf_tensor("WTS", [128, NCH * C], BF16))
        AST = e6(nc.sbuf_tensor("AST", [128, NCH * 128], BF16))
        BST = e6(nc.sbuf_tensor("BST", [128, C], BF16))
        GWF = e6(nc.sbuf_tensor("GWF", [128, NCH * 8], F32))
        SGR = e6(nc.sbuf_tensor("SGR", [1, 8], F32))
        NM3M = e6(nc.sbuf_tensor("NM3M", [1, 512], F32))
        R3C = e6(nc.sbuf_tensor("R3C", [128, 4], F32))
        BR = e6(nc.sbuf_tensor("BR", [1, C], BF16))
        X2T = e6(nc.sbuf_tensor("X2T", [128, NCH * TS], F32))
        XH2 = e6(nc.sbuf_tensor("XH2", [128, NCH * TS], BF16))
        Y4F = e6(nc.sbuf_tensor("Y4F", [128, NCH * TS], F32))
        Y4SQ = [e6(nc.sbuf_tensor(f"Y4SQ{i}", [128, 512], BF16)) for i in range(2)]
        CSW = e6(nc.sbuf_tensor("CSW", [128, NCH], BF16))
        CSWF = e6(nc.sbuf_tensor("CSWF", [128, NCH], F32))
        CSB = e6(nc.sbuf_tensor("CSB", [128, 1], BF16))
        CSBF = e6(nc.sbuf_tensor("CSBF", [128, 1], F32))
        SW = e6(nc.sbuf_tensor("SW", [1, C], BF16))
        SA = e6(nc.sbuf_tensor("SA", [1, 128], BF16))
        SBR = e6(nc.sbuf_tensor("SBR", [1, 1], F32))
        NM3 = e6(nc.sbuf_tensor("NM3", [1, 512], BF16))
        RB3 = e6(nc.sbuf_tensor("RB3", [128, 512], F32))
        RB6 = e6(nc.sbuf_tensor("RB6", [128, 512], F32))
        CBS = RB3
        LG = e6(nc.sbuf_tensor("LG", [128, 32], F32))
        M1 = e6(nc.sbuf_tensor("M1", [128, 4], F32))
        M2 = e6(nc.sbuf_tensor("M2", [128, 4], F32))
        IS1 = e6(nc.sbuf_tensor("IS1", [128, 32], F32))
        IS2 = e6(nc.sbuf_tensor("IS2", [128, 32], F32))
        G1 = e6(nc.sbuf_tensor("G1", [128, 4], F32))
        G2 = e6(nc.sbuf_tensor("G2", [128, 4], F32))
        GT = e6(nc.sbuf_tensor("GT", [128, 4], F32))
        WSEL = e6(nc.sbuf_tensor("WSEL", [128, 32], F32))
        W8 = e6(nc.sbuf_tensor("W8", [8, 512], F32))
        XAS = e6(nc.sbuf_tensor("XAS", [128, 512], BF16))
        XAB = e6(nc.sbuf_tensor("XAB", [128, 512], BF16))
        DG = e6(nc.sbuf_tensor("DG", [128, 64], F32)) if debug else None
        Y4p = [e6(nc.psum_tensor(f"Y4pE{i}", [128, 512], F32)) for i in range(2)]
        Sum6 = e6(nc.psum_tensor("Sum6E", [1, 512], F32))
        Sq6 = e6(nc.psum_tensor("Sq6E", [1, 512], F32))
        XAp = e6(nc.psum_tensor("XApE", [128, 512], F32))
        Gp = e6(nc.psum_tensor("GpE", [128, 32], F32))
        COL6 = e6(nc.psum_tensor("COL6E", [128, 8], F32))
        AUXp = e6(nc.psum_tensor("AUXpE", [128, 512], F32))

        evac6 = {}

        def wld6(g):
            g.dma_start(out=WTS[:, :].rearrange("d (k o) -> d k o", k=NCH),
                        in_=WT.rearrange("(k d) o -> d k o", d=128)).then_inc(sldg.h, 16)
            g.dma_start(out=AST[:, :].rearrange("d (k o) -> d k o", k=NCH),
                        in_=AstkT.rearrange("(k d) o -> d k o", d=128)).then_inc(sldg.h, 16)
            g.dma_start(out=BST[:, :], in_=BstkT).then_inc(sldg.h, 16)
            g.dma_start(out=BR[:, :], in_=brow).then_inc(sldg.h, 16)
        on("gpsimd", wld6)
        sldg.n += 4 * 16
        w6 = sldg.n
        on("sync", lambda sy: sy.dma_start(
            out=GWF[:, :].rearrange("d (k o) -> d k o", k=NCH),
            in_=gwT.rearrange("(k d) o -> d k o", d=128)).then_inc(slds.h, 16))
        slds.n += 16
        gwf_ld = slds.n

        W("vector", sldg, w6)
        def v6pre(e):
            for dc in range(NCH):
                e.reduce_sum(out=CSWF[:, dc:dc + 1], in_=WTS[:, dc * C:(dc + 1) * C],
                             axis=mybir.AxisListType.X)
            e.reduce_sum(out=CSBF[:, :], in_=BST[:, :], axis=mybir.AxisListType.X)
            e.reduce_sum(out=SBR[:, :], in_=BR[:, :],
                         axis=mybir.AxisListType.X).then_inc(sve.h, 1)
        on("vector", v6pre)
        sve.n += 1
        W("vector", sve, sve.n)
        def v6preb(e):
            e.tensor_copy(out=CSW[:, :], in_=CSWF[:, :])
            e.tensor_copy(out=CSB[:, :], in_=CSBF[:, :]).then_inc(sve.h, 1)
        on("vector", v6preb)
        sve.n += 1
        W("tensor", sldg, w6)
        for sl in range(4):
            def swmm(e, sl=sl):
                for dc in range(NCH):
                    ins = e.matmul(Sum6[:, :], ONEB[:, :],
                                   WTS[:, dc * C + sl * 512: dc * C + (sl + 1) * 512],
                                   start=(dc == 0), stop=(dc == NCH - 1))
                ins.then_inc(ste.h, 1)
            on("tensor", swmm)
            ste.n += 1
            W("vector", ste, ste.n)
            on("vector", lambda e, sl=sl: e.tensor_copy(
                out=SW[:, sl * 512:(sl + 1) * 512], in_=Sum6[:, :]).then_inc(sve.h, 1))
            sve.n += 1
            W("tensor", sve, sve.n)
        W("tensor", slds, gwf_ld)
        def sgmm2(e):
            for dc in range(NCH):
                e.matmul(COL6[0:1, 0:8], ONEF[:, :], GWF[:, dc * 8:(dc + 1) * 8],
                         start=(dc == 0), stop=(dc == NCH - 1))
            for dc in range(NCH):
                ins = e.matmul(Sq6[:, 0:128], ONEB[:, :], AST[:, dc * 128:(dc + 1) * 128],
                               start=(dc == 0), stop=(dc == NCH - 1))
            ins.then_inc(ste.h, 1)
        on("tensor", sgmm2)
        ste.n += 1
        W("vector", ste, ste.n)
        def v6pre2(e):
            e.tensor_copy(out=SGR[:, :], in_=COL6[0:1, 0:8])
            e.tensor_copy(out=SA[:, :], in_=Sq6[:, 0:128]).then_inc(sve.h, 1)
        on("vector", v6pre2)
        sve.n += 1
        pre_ve = sve.n
        W("tensor", sve, pre_ve)

        x2ld = [None] * NOT_
        fin6 = [None] * NOT_

        def ld6(i):
            if i >= 1:
                W("sync", sve, fin6[i - 1])
            def f(sy, i=i):
                sy.dma_start(out=X2T[:, :].rearrange("d (k n) -> d k n", k=NCH),
                             in_=x2buf.rearrange("d (k n) -> d k n", k=NCH)[:, :, i * TS:(i + 1) * TS]
                             ).then_inc(sb[0].h, 16)
            on("sync", f)
            sb[0].n += 16
            x2ld[i] = sb[0].n

        ld6(0)
        for i in range(NOT_):
            W("scalar", sb[0], x2ld[i])
            if i > 0:
                W("scalar", sve, fin6[i - 1])
                W("scalar", sst, sst.n)
            def s6a(e, i=i):
                e.activation(out=XH2[:, :], in_=X2T[:, :],
                             func=AF.Square).then_inc(ssc.h, 1)
            on("scalar", s6a)
            ssc.n += 1
            W("tensor", ssc, ssc.n)
            if i > 0:
                W("tensor", sve, fin6[i - 1])
            def t6stats(e, i=i):
                for dc in range(NCH):
                    e.matmul(Sum6[:, :], ONEF[:, :], X2T[:, dc * TS:(dc + 1) * TS],
                             start=(dc == 0), stop=(dc == NCH - 1))
                for dc in range(NCH):
                    ins = e.matmul(Sq6[:, :], ONEB[:, :], XH2[:, dc * TS:(dc + 1) * TS],
                                   start=(dc == 0), stop=(dc == NCH - 1))
                ins.then_inc(ste.h, 1)
            on("tensor", t6stats)
            ste.n += 1
            W("vector", ste, ste.n)
            def v6rows(e):
                e.tensor_scalar(out=ROWA[:, :], in0=Sum6[:, :], scalar1=1.0 / C,
                                scalar2=None, op0=ALU.mult)
                e.tensor_scalar(out=ROWB[:, :], in0=Sq6[:, :], scalar1=1.0 / C,
                                scalar2=None, op0=ALU.mult)
                e.tensor_scalar(out=NM3M[:, :], in0=ROWA[:, :], scalar1=-1.0,
                                scalar2=None, op0=ALU.mult)
                e.tensor_tensor(out=ROWC[:, :], in0=ROWA[:, :], in1=ROWA[:, :], op=ALU.mult)
                e.tensor_tensor(out=ROWB[:, :], in0=ROWB[:, :], in1=ROWC[:, :],
                                op=ALU.subtract).then_inc(sve.h, 1)
            on("vector", v6rows)
            sve.n += 1
            if debug and i == 0:
                W("sync", sve, sve.n)
                def drows3(sy):
                    sy.dma_start(out=d_rows[:, 0:TS], in_=ROWA[:, :]).then_inc(sst.h, 16)
                    sy.dma_start(out=d_rows[:, TS:2 * TS], in_=ROWB[:, :]).then_inc(sst.h, 16)
                on("sync", drows3)
                sst.n += 32
                W("scalar", sst, sst.n)
            W("scalar", sve, sve.n)
            def s6rows(e):
                e.activation(out=ROWB[:, :], in_=ROWB[:, :], func=AF.Ln, bias=EPSC[:, 0:1])
                e.activation(out=ROWC[:, :], in_=ROWB[:, :], func=AF.Exp,
                             scale=-0.5).then_inc(ssc.h, 1)
            on("scalar", s6rows)
            ssc.n += 1
            W("vector", ssc, ssc.n)
            def v6neg(e):
                e.scalar_tensor_tensor(out=NM3[:, :], in0=ROWA[:, :], scalar=-1.0,
                                       in1=ROWC[:, :], op0=ALU.mult,
                                       op1=ALU.mult).then_inc(sve.h, 1)
            on("vector", v6neg)
            sve.n += 1
            W("tensor", sve, sve.n)
            def t6cols(e):
                for tc in range(4):
                    e.matmul(COL6[:, tc:tc + 1],
                             ROWC[:, tc * 128:(tc + 1) * 128], ONE1F[:, :],
                             start=True, stop=True)
                e.matmul(AUXp[:, :], ONERF[:, :], ROWC[:, :],
                         start=True, stop=True).then_inc(ste.h, 1)
            on("tensor", t6cols)
            ste.n += 1
            W("vector", ste, ste.n)
            def v6xh(e, i=i):
                e.tensor_copy(out=R3C[:, :], in_=COL6[:, 0:4])
                e.tensor_copy(out=RB3[:, :], in_=AUXp[:, :])
                for dc in range(NCH):
                    ins = e.tensor_tensor(out=XH2[:, dc * TS:(dc + 1) * TS],
                                          in0=X2T[:, dc * TS:(dc + 1) * TS],
                                          in1=RB3[:, :], op=ALU.mult)
                ins.then_inc(sve.h, 1)
            on("vector", v6xh)
            sve.n += 1
            xh_ve = sve.n
            if debug and i == 0:
                W("sync", sve, xh_ve)
                on("sync", lambda sy: sy.dma_start(out=d_xh2, in_=XH2[:, :]).then_inc(sst.h, 16))
                sst.n += 16

            W("tensor", sve, xh_ve)
            def t6gate(e, i=i):
                for tc in range(4):
                    for dc in range(NCH):
                        e.matmul(Gp[:, tc * 8:(tc + 1) * 8],
                                 X2T[:, dc * TS + tc * 128: dc * TS + (tc + 1) * 128],
                                 GWF[:, dc * 8:(dc + 1) * 8],
                                 start=(dc == 0), stop=False)
                    e.matmul(Gp[:, tc * 8:(tc + 1) * 8],
                             NM3M[:, tc * 128:(tc + 1) * 128],
                             SGR[:, :], start=False, stop=True)
                for dc in range(NCH):
                    e.matmul(XAp[:, :], AST[:, dc * 128:(dc + 1) * 128],
                             XH2[:, dc * TS:(dc + 1) * TS],
                             start=(dc == 0), stop=False)
                e.matmul(XAp[:, :], SA[:, :], NM3[:, :],
                         start=False, stop=True).then_inc(ste.h, 1)
            on("tensor", t6gate)
            ste.n += 1
            gate_te = ste.n
            W("scalar", ste, gate_te)
            on("scalar", lambda e: e.copy(out=XAS[:, :], in_=XAp[:, :]).then_inc(ssc.h, 1))
            ssc.n += 1
            xas_sc = ssc.n
            def step(fn):
                on("vector", fn)
                sve.n += 1
                W("vector", sve, sve.n)

            W("vector", ste, gate_te)
            def g_s0(e, i=i):
                if debug and i == 0:
                    e.tensor_copy(out=DG[:, 44:48], in_=R3C[:, :])
                for tc in range(4):
                    ins = e.tensor_scalar(out=LG[:, tc * 8:(tc + 1) * 8],
                                          in0=Gp[:, tc * 8:(tc + 1) * 8],
                                          scalar1=R3C[:, tc:tc + 1], scalar2=None,
                                          op0=ALU.mult)
                ins.then_inc(sve.h, 1)
            step(g_s0)
            def g_s1(e, i=i):
                if debug and i == 0:
                    e.tensor_copy(out=DG[:, 0:8], in_=Gp[:, 0:8])
                for tc in range(4):
                    ins = e.tensor_tensor(out=LG[:, tc * 8:(tc + 1) * 8],
                                          in0=LG[:, tc * 8:(tc + 1) * 8],
                                          in1=GBB[:, :], op=ALU.add)
                ins.then_inc(sve.h, 1)
            step(g_s1)
            def g_s2(e, i=i):
                if debug and i == 0:
                    e.tensor_copy(out=DG[:, 8:16], in_=LG[:, 0:8])
                for tc in range(4):
                    ins = e.reduce_max(out=M1[:, tc:tc + 1], in_=LG[:, tc * 8:(tc + 1) * 8],
                                       axis=mybir.AxisListType.X)
                ins.then_inc(sve.h, 1)
            step(g_s2)
            def g_s3(e, i=i):
                if debug and i == 0:
                    e.tensor_copy(out=DG[:, 16:17], in_=M1[:, 0:1])
                for tc in range(4):
                    ins = e.tensor_tensor(out=IS1[:, tc * 8:(tc + 1) * 8],
                                          in0=LG[:, tc * 8:(tc + 1) * 8],
                                          in1=M1[:, tc:tc + 1].broadcast_to([128, 8]),
                                          op=ALU.is_ge)
                ins.then_inc(sve.h, 1)
            step(g_s3)
            def g_s4(e, i=i):
                if debug and i == 0:
                    e.tensor_copy(out=DG[:, 18:26], in_=IS1[:, 0:8])
                for tc in range(4):
                    ins = e.scalar_tensor_tensor(out=LG[:, tc * 8:(tc + 1) * 8],
                                                 in0=IS1[:, tc * 8:(tc + 1) * 8], scalar=NEG,
                                                 in1=LG[:, tc * 8:(tc + 1) * 8],
                                                 op0=ALU.mult, op1=ALU.add)
                ins.then_inc(sve.h, 1)
            step(g_s4)
            def g_s5(e, i=i):
                if debug and i == 0:
                    e.tensor_copy(out=DG[:, 26:34], in_=LG[:, 0:8])
                for tc in range(4):
                    ins = e.reduce_max(out=M2[:, tc:tc + 1], in_=LG[:, tc * 8:(tc + 1) * 8],
                                       axis=mybir.AxisListType.X)
                ins.then_inc(sve.h, 1)
            step(g_s5)
            def g_s6(e, i=i):
                if debug and i == 0:
                    e.tensor_copy(out=DG[:, 17:18], in_=M2[:, 0:1])
                e.tensor_tensor(out=G1[:, :], in0=M1[:, :], in1=M2[:, :],
                                op=ALU.subtract).then_inc(sve.h, 1)
            step(g_s6)
            W("scalar", sve, sve.n)
            on("scalar", lambda e: e.activation(out=GT[:, :], in_=G1[:, :], func=AF.Tanh,
                                                scale=0.5).then_inc(ssc.h, 1))
            ssc.n += 1
            W("vector", ssc, ssc.n)
            step(lambda e: e.tensor_scalar(out=G1[:, :], in0=GT[:, :], scalar1=0.5,
                                           scalar2=0.5, op0=ALU.mult,
                                           op1=ALU.add).then_inc(sve.h, 1))
            step(lambda e: e.tensor_scalar(out=G2[:, :], in0=GT[:, :], scalar1=-0.5,
                                           scalar2=0.5, op0=ALU.mult,
                                           op1=ALU.add).then_inc(sve.h, 1))
            def g_s9(e):
                for tc in range(4):
                    ins = e.tensor_tensor(out=IS2[:, tc * 8:(tc + 1) * 8],
                                          in0=LG[:, tc * 8:(tc + 1) * 8],
                                          in1=M2[:, tc:tc + 1].broadcast_to([128, 8]),
                                          op=ALU.is_ge)
                ins.then_inc(sve.h, 1)
            step(g_s9)
            def g_s10(e):
                for tc in range(4):
                    ins = e.tensor_tensor(out=IS1[:, tc * 8:(tc + 1) * 8],
                                          in0=IS1[:, tc * 8:(tc + 1) * 8],
                                          in1=G1[:, tc:tc + 1].broadcast_to([128, 8]),
                                          op=ALU.mult)
                ins.then_inc(sve.h, 1)
            step(g_s10)
            def g_s11(e):
                for tc in range(4):
                    ins = e.tensor_tensor(out=IS2[:, tc * 8:(tc + 1) * 8],
                                          in0=IS2[:, tc * 8:(tc + 1) * 8],
                                          in1=G2[:, tc:tc + 1].broadcast_to([128, 8]),
                                          op=ALU.mult)
                ins.then_inc(sve.h, 1)
            step(g_s11)
            def g_s12(e, i=i):
                for tc in range(4):
                    ins = e.tensor_tensor(out=WSEL[:, tc * 8:(tc + 1) * 8],
                                          in0=IS1[:, tc * 8:(tc + 1) * 8],
                                          in1=IS2[:, tc * 8:(tc + 1) * 8], op=ALU.add)
                if debug and i == 0:
                    e.tensor_copy(out=DG[:, 34:42], in_=WSEL[:, 0:8])
                    e.tensor_copy(out=DG[:, 42:43], in_=G1[:, 0:1])
                    e.tensor_copy(out=DG[:, 43:44], in_=G2[:, 0:1])
                    e.tensor_copy(out=DG[:, 48:56], in_=IS1[:, 0:8])
                    e.tensor_copy(out=DG[:, 56:64], in_=IS2[:, 0:8])
                ins.then_inc(sve.h, 1)
            step(g_s12)
            W("tensor", sve, sve.n)
            def g_tr(e):
                for tc in range(4):
                    ins = e.transpose(AUXp[0:8, tc * 128:(tc + 1) * 128],
                                      WSEL[:, tc * 8:(tc + 1) * 8], ID128[:, :])
                ins.then_inc(ste.h, 1)
            on("tensor", g_tr)
            ste.n += 1
            W("vector", ste, ste.n)
            def g_w8(e):
                for tc in range(4):
                    ins = e.tensor_copy(out=W8[:, tc * 128:(tc + 1) * 128],
                                        in_=AUXp[0:8, tc * 128:(tc + 1) * 128])
                ins.then_inc(sve.h, 1)
            on("vector", g_w8)
            sve.n += 1
            W("vector", sve, sve.n)
            w8_ve = sve.n
            if debug and i == 0:
                W("sync", sve, w8_ve)
                on("sync", lambda sy: sy.dma_start(out=d_w8, in_=W8[:, :]).then_inc(sst.h, 16))
                on("sync", lambda sy: sy.dma_start(out=d_gate, in_=DG[:, :]).then_inc(sst.h, 16))
                sst.n += 32
            W("tensor", sve, w8_ve)
            on("tensor", lambda e: e.matmul(AUXp[:, :], E8F[:, :], W8[:, :],
                                            start=True, stop=True).then_inc(ste.h, 1))
            ste.n += 1
            W("vector", ste, ste.n)
            W("vector", ssc, xas_sc)
            def v6xab(e):
                e.tensor_tensor(out=XAB[:, :], in0=XAS[:, :], in1=AUXp[:, :],
                                op=ALU.mult).then_inc(sve.h, 1)
            on("vector", v6xab)
            sve.n += 1
            xab_ve = sve.n
            if debug and i == 0:
                W("sync", sve, xab_ve)
                on("sync", lambda sy: sy.dma_start(out=d_xab, in_=XAB[:, :]).then_inc(sst.h, 16))
                sst.n += 16
            W("tensor", sve, xab_ve)

            sq_te6 = {}
            for oc in range(NCH):
                if (i, oc - 2) in evac6:
                    W("tensor", ssc, evac6[(i, oc - 2)])
                def y4mm(e, i=i, oc=oc):
                    bank = Y4p[oc % 2]
                    for dc in range(NCH):
                        e.matmul(bank[:, :],
                                 WTS[:, dc * C + oc * 128: dc * C + (oc + 1) * 128],
                                 XH2[:, dc * TS:(dc + 1) * TS],
                                 start=(dc == 0), stop=False)
                    e.matmul(bank[:, :], SW[:, oc * 128:(oc + 1) * 128], NM3[:, :],
                             start=False, stop=False)
                    e.matmul(bank[:, :], BR[:, oc * 128:(oc + 1) * 128], OR512B[:, :],
                             start=False, stop=False)
                    e.matmul(bank[:, :], BST[:, oc * 128:(oc + 1) * 128], XAB[:, :],
                             start=False, stop=True).then_inc(ste.h, 1)
                on("tensor", y4mm)
                ste.n += 1
                W("scalar", ste, ste.n)
                if oc >= 2:
                    W("scalar", ste, sq_te6[oc - 2])
                def y4ev(e, i=i, oc=oc):
                    bank = Y4p[oc % 2]
                    e.activation(out=Y4SQ[oc % 2][:, :], in_=bank[:, :], func=AF.Square)
                    e.copy(out=Y4F[:, oc * TS:(oc + 1) * TS], in_=bank[:, :]).then_inc(ssc.h, 1)
                on("scalar", y4ev)
                ssc.n += 1
                evac6[(i, oc)] = ssc.n
                if oc >= 1:
                    W("tensor", ssc, evac6[(i, oc - 1)])
                    def sq6m(e, oc2=oc - 1):
                        e.matmul(Sq6[:, :], ONEB[:, :], Y4SQ[oc2 % 2][:, :],
                                 start=(oc2 == 0), stop=False).then_inc(ste.h, 1)
                    on("tensor", sq6m)
                    ste.n += 1
                    sq_te6[oc - 1] = ste.n
            W("tensor", ssc, evac6[(i, NCH - 1)])
            def sq6mf(e):
                e.matmul(Sq6[:, :], ONEB[:, :], Y4SQ[(NCH - 1) % 2][:, :],
                         start=False, stop=True).then_inc(ste.h, 1)
            on("tensor", sq6mf)
            ste.n += 1
            sq_te6[NCH - 1] = ste.n
            def m4mm(e, i=i):
                for dc in range(NCH):
                    e.matmul(Sum6[:, :], CSW[:, dc:dc + 1], XH2[:, dc * TS:(dc + 1) * TS],
                             start=(dc == 0), stop=False)
                e.matmul(Sum6[:, :], CSB[:, :], XAB[:, :],
                         start=False, stop=True).then_inc(ste.h, 1)
            on("tensor", m4mm)
            ste.n += 1
            W("vector", ste, ste.n)
            def v6r4(e):
                e.tensor_scalar(out=ROWA[:, :], in0=Sum6[:, :], scalar1=SBR[:, 0:1],
                                scalar2=1.0 / C, op0=ALU.add, op1=ALU.mult)
                e.tensor_scalar(out=ROWB[:, :], in0=Sq6[:, :], scalar1=1.0 / C,
                                scalar2=None, op0=ALU.mult)
                e.tensor_tensor(out=ROWC[:, :], in0=ROWA[:, :], in1=ROWA[:, :], op=ALU.mult)
                e.tensor_tensor(out=ROWB[:, :], in0=ROWB[:, :], in1=ROWC[:, :],
                                op=ALU.subtract).then_inc(sve.h, 1)
            on("vector", v6r4)
            sve.n += 1
            if debug and i == 0:
                W("sync", sve, sve.n)
                W("sync", ssc, evac6[(i, NCH - 1)])
                def dy4(sy):
                    sy.dma_start(out=d_y4, in_=Y4F[:, :]).then_inc(sst.h, 16)
                    sy.dma_start(out=d_rows[:, 2 * TS:3 * TS], in_=ROWA[:, :]).then_inc(sst.h, 16)
                    sy.dma_start(out=d_rows[:, 3 * TS:4 * TS], in_=ROWB[:, :]).then_inc(sst.h, 16)
                on("sync", dy4)
                sst.n += 48
                W("scalar", sst, sst.n)
                W("vector", sst, sst.n)
            W("scalar", sve, sve.n)
            def s6r4(e):
                e.activation(out=ROWB[:, :], in_=ROWB[:, :], func=AF.Ln, bias=EPSC[:, 0:1])
                e.activation(out=ROWC[:, :], in_=ROWB[:, :], func=AF.Exp,
                             scale=-0.5).then_inc(ssc.h, 1)
            on("scalar", s6r4)
            ssc.n += 1
            W("vector", ssc, ssc.n)
            def v6cf(e, i=i):
                e.scalar_tensor_tensor(out=ROWA[:, :], in0=ROWA[:, :], scalar=-1.0,
                                       in1=ROWC[:, :], op0=ALU.mult, op1=ALU.mult)
                e.tensor_tensor(out=ROWA[:, :], in0=ROWA[:, :],
                                in1=CORR2[:, i * TS:(i + 1) * TS], op=ALU.add).then_inc(sve.h, 1)
            on("vector", v6cf)
            sve.n += 1
            W("tensor", sve, sve.n)
            def t6bc2(e):
                e.matmul(AUXp[:, :], ONERF[:, :], ROWC[:, :], start=True, stop=True)
                e.matmul(XAp[:, :], ONERF[:, :], ROWA[:, :],
                         start=True, stop=True).then_inc(ste.h, 1)
            on("tensor", t6bc2)
            ste.n += 1
            W("vector", ste, ste.n)
            def v6fin(e, i=i):
                e.tensor_copy(out=RB6[:, :], in_=AUXp[:, :])
                e.tensor_copy(out=CBS[:, :], in_=XAp[:, :])
                for dc in range(NCH):
                    e.tensor_tensor(out=Y4F[:, dc * TS:(dc + 1) * TS],
                                    in0=Y4F[:, dc * TS:(dc + 1) * TS], in1=RB6[:, :], op=ALU.mult)
                for dc in range(NCH):
                    e.tensor_tensor(out=Y4F[:, dc * TS:(dc + 1) * TS],
                                    in0=Y4F[:, dc * TS:(dc + 1) * TS],
                                    in1=X2T[:, dc * TS:(dc + 1) * TS], op=ALU.add)
                for dc in range(NCH):
                    ins = e.tensor_tensor(out=Y4F[:, dc * TS:(dc + 1) * TS],
                                          in0=Y4F[:, dc * TS:(dc + 1) * TS],
                                          in1=CBS[:, :], op=ALU.add)
                ins.then_inc(sve.h, 1)
            on("vector", v6fin)
            sve.n += 1
            fin6[i] = sve.n
            W("sync", sve, sve.n)
            def st6(sy, i=i):
                sy.dma_start(out=outT.rearrange("(k d) n -> d k n", d=128)[:, :, i * TS:(i + 1) * TS],
                             in_=Y4F[:, :].rearrange("d (k n) -> d k n", k=NCH)).then_inc(sst.h, 16)
            on("sync", st6)
            sst.n += 16
            if i + 1 < NOT_:
                ld6(i + 1)
        W("sync", sst, sst.n)

    with nc.Block() as block:
        @block.tensor
        def _(t):
            for f in ops["tensor"]:
                f(t)

        @block.scalar
        def _(s):
            for f in ops["scalar"]:
                f(s)

        @block.vector
        def _(v):
            for f in ops["vector"]:
                f(v)

        @block.gpsimd
        def _(g):
            for f in ops["gpsimd"]:
                f(g)

        @block.sync
        def _(sy):
            for f in ops["sync"]:
                f(sy)

    ctx_stack.close()
    return nc


def _host_prep(inputs):
    x = np.asarray(inputs["L0_attn_x"], dtype=np.float32)
    wq = np.asarray(inputs["L0_attn_w_q"], dtype=np.float32)
    wk = np.asarray(inputs["L0_attn_w_k"], dtype=np.float32)
    wv = np.asarray(inputs["L0_attn_w_v"], dtype=np.float32)
    wo = np.asarray(inputs["L0_attn_w_o"], dtype=np.float32)
    Wf = np.asarray(inputs["L0_ffn_W"], dtype=np.float32)
    bf_ = np.asarray(inputs["L0_ffn_b"], dtype=np.float32)
    lA = np.asarray(inputs["L0_ffn_lora_A"], dtype=np.float32)
    lB = np.asarray(inputs["L0_ffn_lora_B"], dtype=np.float32)
    gw = np.asarray(inputs["L0_ffn_gate_w"], dtype=np.float32)
    gb = np.asarray(inputs["L0_ffn_gate_b"], dtype=np.float32)

    woT = np.ascontiguousarray(wo.T)
    WTt = np.ascontiguousarray(Wf.T)
    AstkT = np.ascontiguousarray(lA.reshape(128, C).T)
    BstkT = np.ascontiguousarray(lB.transpose(0, 2, 1).reshape(128, C))
    gwT = np.ascontiguousarray(gw.T)

    t_ = np.arange(T, dtype=np.float32)
    inv = (1.0 / (10000.0 ** (np.arange(0, D, 2, dtype=np.float32) / D))).astype(np.float32)
    freqs = np.outer(t_, inv).astype(np.float32)
    emb = np.concatenate([freqs, freqs], axis=1)
    cosT = np.ascontiguousarray(np.cos(emb).T.astype(ml_dtypes.bfloat16))
    sinv = np.sin(emb).astype(np.float32)
    sgn = np.ones((128,), np.float32)
    sgn[:64] = -1.0
    sinT = np.ascontiguousarray((sinv * sgn[None, :]).T.astype(ml_dtypes.bfloat16))

    c_ones = np.ones((128, 1), np.float32)
    c_masks = np.zeros((128, 3), np.float32)
    c_masks[64:, 1] = NEG
    c_masks[:64, 2] = NEG
    c_E8 = np.zeros((8, 128), np.float32)
    for e_ in range(8):
        c_E8[e_, e_ * 16:(e_ + 1) * 16] = 1.0
    c_id = np.eye(128, dtype=np.float32)
    c_ones512 = np.ones((1, 512), np.float32)

    in_maps = []
    for core in range(8):
        b, g = core // 2, core % 2
        xTb = np.ascontiguousarray(x[b].T)
        m = dict(
            xT=xTb,
            xoT=np.ascontiguousarray(xTb[:, g * OWN:(g + 1) * OWN]),
            wqT=np.ascontiguousarray(wq[g * CG:(g + 1) * CG, :].T),
            wkT=np.ascontiguousarray(wk[g * CG:(g + 1) * CG, :].T),
            wvT=np.ascontiguousarray(wv[g * CG:(g + 1) * CG, :].T),
            woT=woT, WT=WTt, brow=bf_[None, :],
            AstkT=AstkT, BstkT=BstkT, gwT=gwT, gbrow=gb[None, :],
            cosT=cosT, sinT=sinT,
            c_ones=c_ones, c_masks=c_masks, c_E8=c_E8, c_id=c_id,
            c_ones512=c_ones512,
            c_eps=np.full((1, 1), EPS, np.float32),
        )
        in_maps.append(m)
    return in_maps


def kernel(**inputs):
    in_maps = _host_prep(inputs)
    nc = build_nc(debug=False)
    res = run_bass_kernel_spmd(nc, in_maps, core_ids=list(range(8)))
    out = np.empty((B, T, C), dtype=np.float32)
    for core in range(8):
        b, g = core // 2, core % 2
        out[b, g * OWN:(g + 1) * OWN, :] = np.asarray(res.results[core]["outT"]).T
    return out


# version 08 (final)
# speedup vs baseline: 1.1721x; 1.0060x over previous
import numpy as np
import ml_dtypes
from contextlib import ExitStack

import concourse.bass as bass
import concourse.mybir as mybir
from concourse.bass_utils import run_bass_kernel_spmd

F32 = mybir.dt.float32
BF16 = mybir.dt.bfloat16
AF = mybir.ActivationFunctionType
ALU = mybir.AluOpType

B, T, C, H, D = 4, 4096, 2048, 16, 128
HG = 8
CG = HG * D
NT, TS = 8, 512
NCH = C // 128
OWN = 2048
NOT_ = 4
SCALE = float(1.0 / np.sqrt(D))
EPS = 1e-5
NEG = -1e30

ENGS = ("tensor", "scalar", "vector", "gpsimd", "sync")
PAIRS = [[0, 1], [2, 3], [4, 5], [6, 7]]


def _attn_plan():
    nb = T // 64
    bp = np.arange(nb)
    qb, kb = bp[:, None], bp[None, :]
    dist = qb - kb
    bm = (((dist >= 0) & (dist <= 16)) | (kb < 2) | (kb % 4 == 0)) & (kb <= qb)
    plan = []
    for c in range(8):
        rows = bm[c * 8:(c + 1) * 8]
        groups = []
        for j in range(nb // 2):
            a = np.where(rows[:, 2 * j])[0]
            b = np.where(rows[:, 2 * j + 1])[0]
            if len(a) == 0 and len(b) == 0:
                continue
            if len(a) and len(b):
                r0 = int(min(a[0], b[0])); r1 = int(max(a[-1], b[-1]))
                segs = []
                cur = None
                for r in range(r0, r1 + 1):
                    key = (r in a, r in b)
                    if cur is None or cur[2] != key:
                        if cur is not None:
                            segs.append(tuple(cur))
                        cur = [r, r, key]
                    else:
                        cur[1] = r
                segs.append(tuple(cur))
                segs = [(s0, s1, 0 if k == (True, True) else (1 if k == (True, False) else 2))
                        for (s0, s1, k) in segs]
                groups.append(dict(kind="pair", j=j, r0=r0, r1=r1, segs=segs))
            else:
                side = 0 if len(a) else 1
                rr = a if len(a) else b
                groups.append(dict(kind="single", j=j, side=side,
                                   r0=int(rr[0]), r1=int(rr[-1])))
        plan.append(groups)
    return plan


ATTN_PLAN = _attn_plan()


class Sem:
    __slots__ = ("h", "n", "name")

    def __init__(self, handle, name):
        self.h = handle
        self.n = 0
        self.name = name


def build_nc(debug=False):
    nc = bass.Bass("TRN2", target_bir_lowering=False, debug=False)

    def din(name, shape, dt):
        return nc.dram_tensor(name, shape, dt, kind="ExternalInput").ap()

    def dout(name, shape, dt):
        return nc.dram_tensor(name, shape, dt, kind="ExternalOutput").ap()

    def dint(name, shape, dt):
        if debug:
            return dout(name, shape, dt)
        return nc.dram_tensor(name, shape, dt).ap()

    xT = din("xT", [C, T], F32)
    xoT = din("xoT", [C, OWN], F32)
    wqT = din("wqT", [C, CG], F32)
    wkT = din("wkT", [C, CG], F32)
    wvT = din("wvT", [C, CG], F32)
    woT = din("woT", [C, C], F32)
    WT = din("WT", [C, C], F32)
    brow = din("brow", [1, C], F32)
    AstkT = din("AstkT", [C, 128], F32)
    BstkT = din("BstkT", [128, C], F32)
    gwT = din("gwT", [C, 8], F32)
    gbrow = din("gbrow", [1, 8], F32)
    cosT = din("cosT", [128, T], BF16)
    sinT = din("sinT", [128, T], BF16)
    c_ones = din("c_ones", [128, 1], F32)
    c_masks = din("c_masks", [128, 3], F32)
    c_E8 = din("c_E8", [8, 128], F32)
    c_id = din("c_id", [128, 128], F32)
    c_ones512 = din("c_ones512", [1, 512], F32)
    c_eps = din("c_eps", [1, 1], F32)

    outT = dout("outT", [C, OWN], F32)

    qbuf = dint("qbuf", [CG, T], BF16)
    kbuf = dint("kbuf", [CG, T], BF16)
    vbuf = dint("vbuf", [128, 32 * CG], BF16)
    ctxin = nc.dram_tensor("ctxin", [8 * CG, TS], BF16).ap()
    ctxout = nc.dram_tensor("ctxout", [16 * CG, TS], BF16).ap()
    ctxout_dbg = dout("ctxout", [16 * CG, TS], BF16) if False else None
    if debug:
        ctxout_dbg = dout("ctxout_dbg", [16 * CG, TS], BF16)
        d_xh2 = dout("d_xh2", [128, NCH * TS], BF16)
        d_w8 = dout("d_w8", [8, TS], F32)
        d_xab = dout("d_xab", [128, TS], BF16)
        d_y4 = dout("d_y4", [128, NCH * TS], F32)
        d_rows = dout("d_rows", [1, 4 * TS], F32)
        d_gate = dout("d_gate", [128, 64], F32)
    x2buf = dint("x2buf", [128, NCH * OWN], F32)
    agw_i = nc.dram_tensor("agw_i", [8, 512], BF16).ap()
    agw_o = nc.dram_tensor("agw_o", [16, 512], BF16).ap()

    ops = {e: [] for e in ENGS}

    def on(eng, fn):
        ops[eng].append(fn)

    ctx_stack = ExitStack()
    E = ctx_stack.enter_context

    sems = {}
    for nm in ("ste", "ssc", "sve", "scc", "sbar",
               "sx0", "sx1", "sldg", "slds", "sst", "sa0", "sa1", "sb0", "sb1"):
        sems[nm] = Sem(E(nc.semaphore(nm)), nm)
    ste, ssc, sve = sems["ste"], sems["ssc"], sems["sve"]
    scc, sbar = sems["scc"], sems["sbar"]
    sx = [sems["sx0"], sems["sx1"]]
    sldg, slds, sst = sems["sldg"], sems["slds"], sems["sst"]
    sa = [sems["sa0"], sems["sa1"]]
    sb = [sems["sb0"], sems["sb1"]]

    def W(eng, sem, val):
        assert val <= sem.n, f"wait {sem.name}>={val} but only {sem.n} planned"
        on(eng, lambda e, s=sem.h, v=val: e.wait_ge(s, v))

    nbar = [0]

    def barrier():
        nbar[0] += 1
        tgt = 5 * nbar[0]
        for eng in ENGS:
            on(eng, lambda e: e.nop().then_inc(sbar.h, 1))
        sbar.n += 5
        for eng in ENGS:
            W(eng, sbar, tgt)

    ONEB = E(nc.sbuf_tensor("ONEB", [128, 1], BF16))
    ONEF = E(nc.sbuf_tensor("ONEF", [128, 1], F32))
    ONERF = E(nc.sbuf_tensor("ONERF", [1, 128], F32))
    ONE1F = E(nc.sbuf_tensor("ONE1F", [1, 1], F32))
    OR512B = E(nc.sbuf_tensor("OR512B", [1, 512], BF16))
    MASKC = E(nc.sbuf_tensor("MASKC", [128, 3], F32))
    E8F = E(nc.sbuf_tensor("E8F", [8, 128], F32))
    ID128 = E(nc.sbuf_tensor("ID128", [128, 128], F32))
    GBB = E(nc.sbuf_tensor("GBB", [128, 8], F32))
    GBR = E(nc.sbuf_tensor("GBR", [1, 8], F32))
    RSTD = E(nc.sbuf_tensor("RSTD", [1, T], F32))
    RSTDC = E(nc.sbuf_tensor("RSTDC", [128, 32], F32))
    ROWA = E(nc.sbuf_tensor("ROWA", [1, 512], F32))
    ROWB = E(nc.sbuf_tensor("ROWB", [1, 512], F32))
    ROWC = E(nc.sbuf_tensor("ROWC", [1, 512], F32))
    EPSC = E(nc.sbuf_tensor("EPSC", [1, 1], F32))

    def ld0(sy):
        sy.dma_start(out=ONEF[:, :], in_=c_ones).then_inc(slds.h, 16)
        sy.dma_start(out=ONERF[:, :], in_=c_ones.rearrange("p o -> o p")).then_inc(slds.h, 16)
        sy.dma_start(out=ONE1F[:, :], in_=c_ones[0:1, :]).then_inc(slds.h, 16)
        sy.dma_start(out=MASKC[:, :], in_=c_masks).then_inc(slds.h, 16)
        sy.dma_start(out=E8F[:, :], in_=c_E8).then_inc(slds.h, 16)
        sy.dma_start(out=ID128[:, :], in_=c_id).then_inc(slds.h, 16)
        sy.dma_start(out=GBR[:, :], in_=gbrow).then_inc(slds.h, 16)
        sy.dma_start(out=EPSC[:, :], in_=c_eps).then_inc(slds.h, 16)
    on("sync", ld0)
    slds.n += 8 * 16

    def ld0g(g):
        g.dma_start(out=ONEB[:, :], in_=c_ones).then_inc(sldg.h, 16)
        g.dma_start(out=OR512B[:, :], in_=c_ones512).then_inc(sldg.h, 16)
        for r8 in range(8):
            g.dma_start(out=agw_i[r8:r8 + 1, :], in_=c_ones512).then_inc(sldg.h, 16)
        g.wait_ge(sldg.h, 10 * 16)
        g.collective_compute("AllGather", ALU.bypass, replica_groups=PAIRS,
                             ins=[agw_i[:, :]], outs=[agw_o[:, :]]).then_inc(scc.h, 1)
    on("gpsimd", ld0g)
    sldg.n += 10 * 16
    scc.n += 1

    S0p = ExitStack()
    PB0 = S0p.enter_context(nc.psum_tensor("PB0", [128, 8], F32))
    W("tensor", slds, slds.n)
    on("tensor", lambda e: e.matmul(PB0[:, :], ONERF[:, :], GBR[:, :],
                                    start=True, stop=True).then_inc(ste.h, 1))
    ste.n += 1
    W("vector", ste, ste.n)
    on("vector", lambda e: e.tensor_copy(out=GBB[:, :], in_=PB0[:, :]).then_inc(sve.h, 1))
    sve.n += 1
    S0p.close()

    with ExitStack() as S1:
        e1 = S1.enter_context
        WQ = e1(nc.sbuf_tensor("WQ", [128, NCH * CG], BF16))
        WK = e1(nc.sbuf_tensor("WK", [128, NCH * CG], BF16))
        WV = e1(nc.sbuf_tensor("WV", [128, NCH * CG], BF16))
        XBF = [e1(nc.sbuf_tensor(f"XBF{i}", [128, NCH * TS], BF16)) for i in range(2)]
        XSQ = e1(nc.sbuf_tensor("XSQ", [128, NCH * TS], BF16))
        QST = e1(nc.sbuf_tensor("QST", [128, HG * TS], BF16))
        KST = e1(nc.sbuf_tensor("KST", [128, HG * TS], BF16))
        VST = e1(nc.sbuf_tensor("VST", [128, 4 * CG], BF16))
        NEGM = [e1(nc.sbuf_tensor(f"NEGM{i}", [1, TS], BF16)) for i in range(2)]
        SQR = e1(nc.sbuf_tensor("SQR", [1, CG], BF16))
        SKR = e1(nc.sbuf_tensor("SKR", [1, CG], BF16))
        SVR = e1(nc.sbuf_tensor("SVR", [1, CG], BF16))
        Qp = [e1(nc.psum_tensor(f"QpA{i}", [128, 512], F32)) for i in range(2)]
        Kp = [e1(nc.psum_tensor(f"KpA{i}", [128, 512], F32)) for i in range(2)]
        Vp = e1(nc.psum_tensor("VpA", [128, 512], F32))
        Sump = e1(nc.psum_tensor("SumpA", [1, 512], F32))
        Sqp = e1(nc.psum_tensor("SqpA", [1, 512], F32))
        Colp = e1(nc.psum_tensor("ColpA", [128, 4], F32))

        def wload(g):
            g.dma_start(out=WQ[:, :].rearrange("d (k o) -> d k o", k=NCH),
                        in_=wqT.rearrange("(k d) o -> d k o", d=128)).then_inc(sldg.h, 16)
            g.dma_start(out=WK[:, :].rearrange("d (k o) -> d k o", k=NCH),
                        in_=wkT.rearrange("(k d) o -> d k o", d=128)).then_inc(sldg.h, 16)
            g.dma_start(out=WV[:, :].rearrange("d (k o) -> d k o", k=NCH),
                        in_=wvT.rearrange("(k d) o -> d k o", d=128)).then_inc(sldg.h, 16)
        on("gpsimd", wload)
        sldg.n += 3 * 16

        W("tensor", sldg, sldg.n)
        for (Wt, Srow) in ((WQ, SQR), (WK, SKR), (WV, SVR)):
            for sl in range(2):
                def smm(e, Wt=Wt, sl=sl):
                    for dc in range(NCH):
                        ins = e.matmul(Sump[:, :], ONEB[:, :],
                                       Wt[:, dc * CG + sl * 512: dc * CG + (sl + 1) * 512],
                                       start=(dc == 0), stop=(dc == NCH - 1))
                    ins.then_inc(ste.h, 1)
                on("tensor", smm)
                ste.n += 1
                W("vector", ste, ste.n)
                on("vector", lambda e, Srow=Srow, sl=sl: e.tensor_copy(
                    out=Srow[:, sl * 512:(sl + 1) * 512], in_=Sump[:, :]).then_inc(sve.h, 1))
                sve.n += 1
                W("tensor", sve, sve.n)

        xload = [None] * NT
        rows_sc = [None] * NT
        vrows_ve = [None] * NT
        stats_te = [None] * NT
        qkv_te_done = [None] * NT
        store_id = [None] * NT
        evacQ, evacK, evacV = {}, {}, {}

        def issue_xload(j):
            def f(g, j=j):
                g.dma_start(out=XBF[j % 2][:, :].rearrange("d (k n) -> d k n", k=NCH),
                            in_=xT.rearrange("(k d) n -> d k n", d=128)[:, :, j * TS:(j + 1) * TS]
                            ).then_inc(sx[j % 2].h, 16)
            on("gpsimd", f)
            sx[j % 2].n += 16
            xload[j] = sx[j % 2].n

        issue_xload(0)
        issue_xload(1)

        def emit_stats(j):
            W("scalar", sx[j % 2], xload[j])
            if j > 0:
                W("scalar", ste, stats_te[j - 1])
            on("scalar", lambda e, j=j: e.activation(
                out=XSQ[:, :], in_=XBF[j % 2][:, :], func=AF.Square).then_inc(ssc.h, 1))
            ssc.n += 1
            sq_done = ssc.n
            W("tensor", sx[j % 2], xload[j])
            if j > 0:
                W("tensor", sve, vrows_ve[j - 1])
            def summ(e, j=j):
                for dc in range(NCH):
                    ins = e.matmul(Sump[:, :], ONEB[:, :], XBF[j % 2][:, dc * TS:(dc + 1) * TS],
                                   start=(dc == 0), stop=(dc == NCH - 1))
                ins.then_inc(ste.h, 1)
            on("tensor", summ)
            ste.n += 1
            W("tensor", ssc, sq_done)
            def sqm(e):
                for dc in range(NCH):
                    ins = e.matmul(Sqp[:, :], ONEB[:, :], XSQ[:, dc * TS:(dc + 1) * TS],
                                   start=(dc == 0), stop=(dc == NCH - 1))
                ins.then_inc(ste.h, 1)
            on("tensor", sqm)
            ste.n += 1
            stats_te[j] = ste.n
            W("vector", ste, stats_te[j])
            if j > 0:
                W("vector", ssc, rows_sc[j - 1])
            def vrows(e, j=j):
                e.tensor_scalar(out=ROWC[:, :], in0=Sump[:, :],
                                scalar1=1.0 / C, scalar2=None, op0=ALU.mult)
                e.tensor_scalar(out=ROWA[:, :], in0=Sqp[:, :],
                                scalar1=1.0 / C, scalar2=None, op0=ALU.mult)
                e.tensor_tensor(out=ROWB[:, :], in0=ROWC[:, :], in1=ROWC[:, :], op=ALU.mult)
                e.tensor_tensor(out=ROWA[:, :], in0=ROWA[:, :], in1=ROWB[:, :], op=ALU.subtract)
                e.tensor_scalar(out=NEGM[j % 2][:, :], in0=ROWC[:, :],
                                scalar1=-1.0, scalar2=None, op0=ALU.mult).then_inc(sve.h, 1)
            on("vector", vrows)
            sve.n += 1
            vrows_ve[j] = sve.n
            W("scalar", sve, sve.n)
            def srows(e, j=j):
                e.activation(out=ROWB[:, :], in_=ROWA[:, :], func=AF.Ln, bias=EPSC[:, 0:1])
                e.activation(out=RSTD[:, j * TS:(j + 1) * TS], in_=ROWB[:, :],
                             func=AF.Exp, scale=-0.5).then_inc(ssc.h, 1)
            on("scalar", srows)
            ssc.n += 1
            rows_sc[j] = ssc.n

        def emit_qkv(j):
            W("tensor", ssc, rows_sc[j])
            def colmm(e, j=j):
                for tc in range(4):
                    ins = e.matmul(Colp[:, tc:tc + 1],
                                   RSTD[:, j * TS + tc * 128: j * TS + (tc + 1) * 128],
                                   ONE1F[:, :], start=True, stop=True)
                ins.then_inc(ste.h, 1)
            on("tensor", colmm)
            ste.n += 1
            W("vector", ste, ste.n)
            on("vector", lambda e, j=j: e.tensor_copy(
                out=RSTDC[:, j * 4:(j + 1) * 4], in_=Colp[:, :]).then_inc(sve.h, 1))
            sve.n += 1
            col_ve = sve.n
            if j > 0:
                W("scalar", sst, store_id[j - 1])
            for (Wt, Pp, DST, Srow, ed) in ((WQ, Qp, QST, SQR, evacQ), (WK, Kp, KST, SKR, evacK)):
                for oc in range(HG):
                    if (j, oc - 2) in ed:
                        W("tensor", ssc, ed[(j, oc - 2)])
                    def qmm(e, j=j, oc=oc, Wt=Wt, Pp=Pp, Srow=Srow):
                        bank = Pp[oc % 2]
                        for dc in range(NCH):
                            e.matmul(bank[:, :],
                                     Wt[:, dc * CG + oc * 128: dc * CG + (oc + 1) * 128],
                                     XBF[j % 2][:, dc * TS:(dc + 1) * TS],
                                     start=(dc == 0), stop=False)
                        e.matmul(bank[:, :], Srow[:, oc * 128:(oc + 1) * 128],
                                 NEGM[j % 2][:, :],
                                 start=False, stop=True).then_inc(ste.h, 1)
                    on("tensor", qmm)
                    ste.n += 1
                    W("scalar", ste, ste.n)
                    on("scalar", lambda e, oc=oc, Pp=Pp, DST=DST: e.copy(
                        out=DST[:, oc * TS:(oc + 1) * TS],
                        in_=Pp[oc % 2][:, :]).then_inc(ssc.h, 1))
                    ssc.n += 1
                    ed[(j, oc)] = ssc.n
            for tc in range(4):
                for os_ in range(2):
                    if (j, tc, os_) != (j, 0, 0) and (j, (tc, os_)) not in ():
                        pass
                    key_prev = (tc, os_ - 1) if os_ > 0 else (tc - 1, 1)
                    if key_prev[0] >= 0 and (j,) + key_prev in evacV:
                        W("tensor", ssc, evacV[(j,) + key_prev])
                    def vmm(e, j=j, tc=tc, os_=os_):
                        for dc in range(NCH):
                            e.matmul(Vp[:, :],
                                     XBF[j % 2][:, dc * TS + tc * 128: dc * TS + (tc + 1) * 128],
                                     WV[:, dc * CG + os_ * 512: dc * CG + (os_ + 1) * 512],
                                     start=(dc == 0), stop=False)
                        e.matmul(Vp[:, :],
                                 NEGM[j % 2][:, tc * 128:(tc + 1) * 128],
                                 SVR[:, os_ * 512:(os_ + 1) * 512],
                                 start=False, stop=True).then_inc(ste.h, 1)
                    on("tensor", vmm)
                    ste.n += 1
                    W("scalar", ste, ste.n)
                    W("scalar", sve, col_ve)
                    on("scalar", lambda e, j=j, tc=tc, os_=os_: e.activation(
                        out=VST[:, tc * CG + os_ * 512: tc * CG + (os_ + 1) * 512],
                        in_=Vp[:, :], func=AF.Copy,
                        scale=RSTDC[:, j * 4 + tc: j * 4 + tc + 1]).then_inc(ssc.h, 1))
                    ssc.n += 1
                    evacV[(j, tc, os_)] = ssc.n
            qkv_te_done[j] = ste.n
            W("sync", ssc, ssc.n)
            def store(sy, j=j):
                sy.dma_start(out=qbuf.rearrange("(h d) n -> d h n", d=128)[:, :, j * TS:(j + 1) * TS],
                             in_=QST[:, :].rearrange("d (h n) -> d h n", h=HG)).then_inc(sst.h, 16)
                sy.dma_start(out=kbuf.rearrange("(h d) n -> d h n", d=128)[:, :, j * TS:(j + 1) * TS],
                             in_=KST[:, :].rearrange("d (h n) -> d h n", h=HG)).then_inc(sst.h, 16)
                sy.dma_start(out=vbuf[:, j * 4 * CG:(j + 1) * 4 * CG],
                             in_=VST[:, :]).then_inc(sst.h, 16)
            on("sync", store)
            sst.n += 3 * 16
            store_id[j] = sst.n
            if j + 2 < NT:
                W("gpsimd", ste, qkv_te_done[j])
                issue_xload(j + 2)

        for j in range(NT):
            emit_stats(j)
            if j > 0:
                emit_qkv(j - 1)
        emit_qkv(NT - 1)
        W("sync", sst, sst.n)
    barrier()

    with ExitStack() as S2:
        e2 = S2.enter_context
        KH = e2(nc.sbuf_tensor("KH", [128, HG * T], BF16))
        VS = e2(nc.sbuf_tensor("VS", [128, 32 * CG], BF16))
        CB = e2(nc.sbuf_tensor("CB", [128, 512], BF16))
        SB = e2(nc.sbuf_tensor("SB", [128, 512], BF16))
        CS = e2(nc.sbuf_tensor("CS", [128, 512], BF16))
        SS = e2(nc.sbuf_tensor("SS", [128, 512], BF16))
        CSK = e2(nc.sbuf_tensor("CSK", [128, 512], BF16))
        SSK = e2(nc.sbuf_tensor("SSK", [128, 512], BF16))
        T1 = e2(nc.sbuf_tensor("T1", [128, 512], BF16))
        T2 = e2(nc.sbuf_tensor("T2", [128, 512], BF16))
        KW = e2(nc.sbuf_tensor("KW", [128, HG * TS], BF16))
        Rbp = e2(nc.psum_tensor("RbpB", [128, 512], F32))

        on("sync", lambda sy: sy.dma_start(out=VS[:, :], in_=vbuf[:, :]).then_inc(slds.h, 16))
        slds.n += 16
        vs_ld = slds.n

        QT = e2(nc.sbuf_tensor("QT", [128, HG * TS], BF16))
        QW = e2(nc.sbuf_tensor("QW", [128, HG * TS], BF16))
        QH = e2(nc.sbuf_tensor("QH", [128, HG * TS], BF16))
        PT = [e2(nc.sbuf_tensor(f"PT{i}", [128, 512], BF16)) for i in range(4)]
        IDR = [ROWA, ROWB]
        CTU = e2(nc.sbuf_tensor("CTU", [128, HG * TS], BF16))
        SC = [e2(nc.psum_tensor(f"ScpC{i}", [128, 512], F32)) for i in range(4)]
        Ctp = [e2(nc.psum_tensor(f"CtpC{i}", [128, 512], F32)) for i in range(2)]
        DnpB = e2(nc.psum_tensor("DnpB", [128, 512], F32))
        Dnp = [DnpB[0:1, :], DnpB[64:65, :]]

        W("tensor", slds, vs_ld)

        exp_hist = []
        smm_hist = []
        cd_hist = []
        smm_last = {}
        ldq_id = {}
        qrope_ve = {}
        recip_ve = {}
        rb2_te = {}
        vmul_ve = {}
        stctx_id = {}
        ctu_sc = {}
        cd_last = {}
        tail_q = []
        rbb_free = [0]

        def emit_tail_step(cur_gh=None):
            if tail_q and (cur_gh is None or tail_q[0][0] <= cur_gh - 2):
                tail_q.pop(0)[1]()

        def mk_tail(gh):
            def f(gh=gh):
                cch, h = gh // 8, gh % 8
                hb = gh % 2
                W("tensor", sve, recip_ve[gh])
                if rbb_free[0]:
                    W("tensor", sve, rbb_free[0])
                on("tensor", lambda e, hb=hb: e.matmul(
                    Rbp[:, :], ONERF[:, :], IDR[hb][:, :],
                    start=True, stop=True).then_inc(ste.h, 1))
                ste.n += 1
                rb2_te[gh] = ste.n
                W("vector", ste, ste.n)
                W("vector", ssc, ctu_sc[gh])
                on("vector", lambda e, h=h: e.tensor_tensor(
                    out=CTU[:, h * TS:(h + 1) * TS], in0=CTU[:, h * TS:(h + 1) * TS],
                    in1=Rbp[:, :], op=ALU.mult).then_inc(sve.h, 1))
                sve.n += 1
                vmul_ve[gh] = sve.n
                rbb_free[0] = sve.n
                W("sync", sve, sve.n)
                def stc(sy, cch=cch, h=h):
                    sy.dma_start(out=ctxin[cch * CG + h * 128: cch * CG + (h + 1) * 128, :],
                                 in_=CTU[:, h * TS:(h + 1) * TS]).then_inc(sst.h, 16)
                on("sync", stc)
                sst.n += 16
                stctx_id[gh] = sst.n
                if h == 7:
                    W("gpsimd", sst, sst.n)
                    on("gpsimd", lambda g, cch=cch: g.collective_compute(
                        "AllGather", ALU.bypass, replica_groups=PAIRS,
                        ins=[ctxin[cch * CG:(cch + 1) * CG, :]],
                        outs=[ctxout[cch * 2 * CG:(cch + 1) * 2 * CG, :]]).then_inc(scc.h, 1))
                    scc.n += 1
            return f

        def emit_ldq(c2):
            cl2 = c2 * TS
            def f(sy, cl=cl2):
                sy.dma_start(out=CS[:, :], in_=cosT[:, cl:cl + TS]).then_inc(sa[0].h, 16)
                sy.dma_start(out=SS[:, :], in_=sinT[:, cl:cl + TS]).then_inc(sa[0].h, 16)
                sy.dma_start(out=QT[:, :].rearrange("d (h n) -> d h n", h=HG),
                             in_=qbuf.rearrange("(h d) n -> d h n", d=128)[:, :, cl:cl + TS]
                             ).then_inc(sa[0].h, 16)
                qb4 = qbuf.rearrange("(h s dd) n -> s dd h n", h=HG, s=2)
                sy.dma_start(out=QW[0:64, :].rearrange("d (h n) -> d h n", h=HG),
                             in_=qb4[1, :, :, cl:cl + TS]).then_inc(sa[0].h, 16)
                sy.dma_start(out=QW[64:128, :].rearrange("d (h n) -> d h n", h=HG),
                             in_=qb4[0, :, :, cl:cl + TS]).then_inc(sa[0].h, 16)
            on("sync", f)
            sa[0].n += 5 * 16
            ldq_id[c2] = sa[0].n

        ldk_id = {}
        ropeK_ve = {}
        rbK_te = {}

        def emit_ldk(j):
            cl2 = j * TS
            if (j - 1) in ropeK_ve:
                W("sync", sve, ropeK_ve[j - 1])
            def f(sy, cl=cl2):
                sy.dma_start(out=CSK[:, :], in_=cosT[:, cl:cl + TS]).then_inc(slds.h, 16)
                sy.dma_start(out=SSK[:, :], in_=sinT[:, cl:cl + TS]).then_inc(slds.h, 16)
                kbh = kbuf.rearrange("(h d) n -> d h n", d=128)
                for h2 in range(HG):
                    sy.dma_start(out=KH[:, h2 * T + cl: h2 * T + cl + TS],
                                 in_=kbh[:, h2, cl:cl + TS]).then_inc(slds.h, 16)
                kb4 = kbuf.rearrange("(h s dd) n -> s dd h n", h=HG, s=2)
                sy.dma_start(out=KW[0:64, :].rearrange("d (h n) -> d h n", h=HG),
                             in_=kb4[1, :, :, cl:cl + TS]).then_inc(slds.h, 16)
                sy.dma_start(out=KW[64:128, :].rearrange("d (h n) -> d h n", h=HG),
                             in_=kb4[0, :, :, cl:cl + TS]).then_inc(slds.h, 16)
            on("sync", f)
            slds.n += 12 * 16
            ldk_id[j] = slds.n

        def emit_ropeK(j):
            cl2 = j * TS
            if rbb_free[0]:
                W("tensor", sve, rbb_free[0])
            on("tensor", lambda e, cl=cl2: e.matmul(
                Rbp[:, :], ONERF[:, :], RSTD[:, cl:cl + TS],
                start=True, stop=True).then_inc(ste.h, 1))
            ste.n += 1
            rbK_te[j] = ste.n
            W("vector", slds, ldk_id[j])
            W("vector", ste, rbK_te[j])
            def vropeK(e, cl=cl2):
                e.tensor_tensor(out=CB[:, :], in0=CSK[:, :], in1=Rbp[:, :], op=ALU.mult)
                e.tensor_tensor(out=SB[:, :], in0=SSK[:, :], in1=Rbp[:, :], op=ALU.mult)
                for h in range(HG):
                    e.tensor_tensor(out=T1[:, :], in0=KH[:, h * T + cl: h * T + cl + TS],
                                    in1=CB[:, :], op=ALU.mult)
                    e.tensor_tensor(out=T2[:, :], in0=KW[:, h * TS:(h + 1) * TS],
                                    in1=SB[:, :], op=ALU.mult)
                    ins = e.tensor_tensor(out=KH[:, h * T + cl: h * T + cl + TS],
                                          in0=T1[:, :], in1=T2[:, :], op=ALU.add)
                ins.then_inc(sve.h, 1)
            on("vector", vropeK)
            sve.n += 1
            ropeK_ve[j] = sve.n
            rbb_free[0] = sve.n

        emit_ldk(0)
        emit_ropeK(0)
        emit_ldq(0)
        for cch in range(8):
            if rbb_free[0]:
                W("tensor", sve, rbb_free[0])
            on("tensor", lambda e, cl=cch * TS: e.matmul(
                Rbp[:, :], ONERF[:, :], RSTD[:, cl:cl + TS],
                start=True, stop=True).then_inc(ste.h, 1))
            ste.n += 1
            rbq_te = ste.n
            W("vector", sa[0], ldq_id[cch])
            W("vector", ste, rbq_te)
            if cch >= 1:
                W("vector", ste, smm_last[cch - 1])
            def qrope(e):
                e.tensor_tensor(out=CB[:, :], in0=CS[:, :], in1=Rbp[:, :], op=ALU.mult)
                e.tensor_tensor(out=SB[:, :], in0=SS[:, :], in1=Rbp[:, :], op=ALU.mult)
                for h in range(HG):
                    e.tensor_tensor(out=T1[:, :], in0=QT[:, h * TS:(h + 1) * TS],
                                    in1=CB[:, :], op=ALU.mult)
                    e.tensor_tensor(out=T2[:, :], in0=QW[:, h * TS:(h + 1) * TS],
                                    in1=SB[:, :], op=ALU.mult)
                    ins = e.tensor_tensor(out=QH[:, h * TS:(h + 1) * TS],
                                          in0=T1[:, :], in1=T2[:, :], op=ALU.add)
                ins.then_inc(sve.h, 1)
            on("vector", qrope)
            sve.n += 1
            qrope_ve[cch] = sve.n
            rbb_free[0] = sve.n
            if cch + 1 < 8:
                W("sync", sve, qrope_ve[cch])
                emit_ldq(cch + 1)

            groups = ATTN_PLAN[cch]
            ng = len(groups)
            W("tensor", sve, qrope_ve[cch])
            W("tensor", sve, ropeK_ve[cch])
            for h in range(HG):
                gh = cch * 8 + h
                hb = gh % 2
                if h == 1 and cch + 1 < 8:
                    emit_ldk(cch + 1)
                    emit_ropeK(cch + 1)
                for gi, g in enumerate(groups):
                    if (gi % 2) == 0:
                        emit_tail_step(gh)
                    cnt = len(smm_hist)
                    bank = SC[cnt % 4]
                    q0, q1 = g["r0"] * 64, (g["r1"] + 1) * 64
                    if len(exp_hist) >= 4:
                        W("tensor", ssc, exp_hist[-4])
                    if g["kind"] == "pair":
                        def smm(e, h=h, j2=g["j"], q0=q0, q1=q1, bank=bank):
                            e.matmul(bank[:, q0:q1],
                                     KH[:, h * T + j2 * 128: h * T + (j2 + 1) * 128],
                                     QH[:, h * TS + q0: h * TS + q1],
                                     start=True, stop=True).then_inc(ste.h, 1)
                    else:
                        def smm(e, h=h, j2=g["j"], sd=g["side"], q0=q0, q1=q1, bank=bank):
                            e.matmul(bank[sd * 64:sd * 64 + 64, q0:q1],
                                     KH[:, h * T + j2 * 128 + sd * 64: h * T + j2 * 128 + sd * 64 + 64],
                                     QH[:, h * TS + q0: h * TS + q1],
                                     start=True, stop=True).then_inc(ste.h, 1)
                    on("tensor", smm)
                    ste.n += 1
                    smm_hist.append(ste.n)
                    W("scalar", ste, ste.n)
                    if len(cd_hist) >= 4:
                        W("scalar", ste, cd_hist[-4])
                    pt = PT[cnt % 4]
                    if g["kind"] == "pair":
                        def sexp(e, g=g, bank=bank, pt=pt):
                            for (s0, s1, bi) in g["segs"]:
                                a0, a1 = s0 * 64, (s1 + 1) * 64
                                if bi == 0:
                                    ins = e.activation(out=pt[:, a0:a1], in_=bank[:, a0:a1],
                                                       func=AF.Exp, scale=SCALE)
                                else:
                                    ins = e.activation(out=pt[:, a0:a1], in_=bank[:, a0:a1],
                                                       func=AF.Exp, scale=SCALE,
                                                       bias=MASKC[:, bi:bi + 1])
                            ins.then_inc(ssc.h, 1)
                    else:
                        def sexp(e, g=g, bank=bank, pt=pt, q0=q0, q1=q1):
                            sd = g["side"]
                            e.activation(out=pt[sd * 64:sd * 64 + 64, q0:q1],
                                         in_=bank[sd * 64:sd * 64 + 64, q0:q1],
                                         func=AF.Exp, scale=SCALE).then_inc(ssc.h, 1)
                    on("scalar", sexp)
                    ssc.n += 1
                    exp_hist.append(ssc.n)
                    W("tensor", ssc, ssc.n)
                    first, last = (gi == 0), (gi == ng - 1)
                    if first:
                        if (gh - 2) in ctu_sc:
                            W("tensor", ssc, ctu_sc[gh - 2])
                        if (gh - 2) in recip_ve:
                            W("tensor", sve, recip_ve[gh - 2])
                    if g["kind"] == "pair":
                        def cmm(e, j2=g["j"], h=h, q0=q0, q1=q1, pt=pt, first=first,
                                last=last, hb=hb):
                            e.matmul(Ctp[hb][:, q0:q1],
                                     VS[:, j2 * CG + h * 128: j2 * CG + (h + 1) * 128],
                                     pt[:, q0:q1], start=first, stop=last)
                            e.matmul(Dnp[hb][:, q0:q1], ONEB[:, :], pt[:, q0:q1],
                                     start=first, stop=last).then_inc(ste.h, 1)
                    else:
                        def cmm(e, j2=g["j"], sd=g["side"], h=h, q0=q0, q1=q1, pt=pt,
                                first=first, last=last, hb=hb):
                            e.matmul(Ctp[hb][:, q0:q1],
                                     VS[sd * 64:sd * 64 + 64, j2 * CG + h * 128: j2 * CG + (h + 1) * 128],
                                     pt[sd * 64:sd * 64 + 64, q0:q1], start=first, stop=last)
                            e.matmul(Dnp[hb][:, q0:q1], ONEB[sd * 64:sd * 64 + 64, :],
                                     pt[sd * 64:sd * 64 + 64, q0:q1],
                                     start=first, stop=last).then_inc(ste.h, 1)
                    on("tensor", cmm)
                    ste.n += 1
                    cd_hist.append(ste.n)
                cd_last[gh] = cd_hist[-1]
                W("scalar", ste, cd_last[gh])
                if (gh - 8) in stctx_id:
                    W("scalar", sst, stctx_id[gh - 8])
                on("scalar", lambda e, h=h, hb=hb: e.copy(
                    out=CTU[:, h * TS:(h + 1) * TS], in_=Ctp[hb][:, :]).then_inc(ssc.h, 1))
                ssc.n += 1
                ctu_sc[gh] = ssc.n
                W("vector", ste, cd_last[gh])
                if (gh - 2) in rb2_te:
                    W("vector", ste, rb2_te[gh - 2])
                on("vector", lambda e, hb=hb: e.reciprocal(
                    out=IDR[hb][:, :], in_=Dnp[hb][:, :]).then_inc(sve.h, 1))
                sve.n += 1
                recip_ve[gh] = sve.n
                tail_q.append((gh, mk_tail(gh)))
            smm_last[cch] = smm_hist[-1]
        while tail_q:
            emit_tail_step()
    barrier()

    CORR2 = E(nc.sbuf_tensor("CORR2", [1, OWN], F32))
    with ExitStack() as S5:
        e5 = S5.enter_context
        WO = e5(nc.sbuf_tensor("WO", [128, NCH * C], BF16))
        CTF = [e5(nc.sbuf_tensor(f"CTF{i}", [128, NCH * TS], BF16)) for i in range(2)]
        XO = e5(nc.sbuf_tensor("XO", [128, NCH * TS], F32))
        YF = e5(nc.sbuf_tensor("YF", [128, NCH * TS], F32))
        YSQ = [e5(nc.sbuf_tensor(f"YSQ{i}", [128, 512], BF16)) for i in range(2)]
        CSO = e5(nc.sbuf_tensor("CSO", [128, NCH], BF16))
        CSOF = e5(nc.sbuf_tensor("CSOF", [128, NCH], F32))
        RB5 = e5(nc.sbuf_tensor("RB5", [128, 512], F32))
        Yp = [e5(nc.psum_tensor(f"YpD{i}", [128, 512], F32)) for i in range(2)]
        Mnp = e5(nc.psum_tensor("MnpD", [1, 512], F32))
        Sqp5 = e5(nc.psum_tensor("Sqp5D", [1, 512], F32))
        Rb5p = e5(nc.psum_tensor("Rb5D", [128, 512], F32))

        evac5 = {}
        on("gpsimd", lambda g: g.dma_start(
            out=WO[:, :].rearrange("d (k o) -> d k o", k=NCH),
            in_=woT.rearrange("(k d) o -> d k o", d=128)).then_inc(sldg.h, 16))
        sldg.n += 16
        W("vector", sldg, sldg.n)
        def vcso(e):
            for cc in range(NCH):
                ins = e.reduce_sum(out=CSOF[:, cc:cc + 1], in_=WO[:, cc * C:(cc + 1) * C],
                                   axis=mybir.AxisListType.X)
            ins.then_inc(sve.h, 1)
        on("vector", vcso)
        sve.n += 1
        W("vector", sve, sve.n)
        on("vector", lambda e: e.tensor_copy(out=CSO[:, :], in_=CSOF[:, :]).then_inc(sve.h, 1))
        sve.n += 1
        cso_ve = sve.n

        W("sync", scc, scc.n)
        if debug:
            W("gpsimd", scc, scc.n)
            on("gpsimd", lambda g: g.dma_start(out=ctxout_dbg, in_=ctxout[:, :]).then_inc(sst.h, 16))
            sst.n += 16

        ctf_ld = [None] * NOT_
        xo_ld = [None] * NOT_
        ymm_done = [None] * NOT_
        fin_ve = [None] * NOT_

        def ld5_ctf(i):
            if i >= 2:
                W("sync", ste, ymm_done[i - 2])
            def f(sy, i=i):
                pid = sy.partition_id()
                with sy.register(f"rp5{i}") as rp:
                    sy.reg_mov(rp, pid)
                    sy.reg_mod(rp, rp, 2)
                    with sy.If_eq(rp, 0):
                        ch = i
                        sy.dma_start(out=CTF[i % 2][:, :].rearrange("d (k n) -> d k n", k=NCH),
                                     in_=ctxout[ch * 2 * CG:(ch + 1) * 2 * CG, :]
                                     .rearrange("(k d) n -> d k n", d=128)).then_inc(sa[1].h, 16)
                    with sy.Else():
                        ch = 4 + i
                        sy.dma_start(out=CTF[i % 2][:, :].rearrange("d (k n) -> d k n", k=NCH),
                                     in_=ctxout[ch * 2 * CG:(ch + 1) * 2 * CG, :]
                                     .rearrange("(k d) n -> d k n", d=128)).then_inc(sa[1].h, 16)
            on("sync", f)
            sa[1].n += 16
            ctf_ld[i] = sa[1].n

        def ld5_xo(i):
            if i >= 1:
                W("sync", sve, fin_ve[i - 1])
            def f(sy, i=i):
                sy.dma_start(out=XO[:, :].rearrange("d (k n) -> d k n", k=NCH),
                             in_=xoT.rearrange("(k d) n -> d k n", d=128)[:, :, i * TS:(i + 1) * TS]
                             ).then_inc(sb[0].h, 16)
            on("sync", f)
            sb[0].n += 16
            xo_ld[i] = sb[0].n

        ld5_ctf(0)
        ld5_xo(0)
        ld5_ctf(1)
        W("tensor", sldg, sldg.n)

        for i in range(NOT_):
            W("tensor", sa[1], ctf_ld[i])
            W("tensor", sve, cso_ve)
            if i > 0:
                W("tensor", sve, fin_ve[i - 1])
            def mmm(e, i=i):
                for cc in range(NCH):
                    ins = e.matmul(Mnp[:, :], CSO[:, cc:cc + 1],
                                   CTF[i % 2][:, cc * TS:(cc + 1) * TS],
                                   start=(cc == 0), stop=(cc == NCH - 1))
                ins.then_inc(ste.h, 1)
            on("tensor", mmm)
            ste.n += 1
            sq_te = {}
            for oc in range(NCH):
                if (i, oc - 2) in evac5:
                    W("tensor", ssc, evac5[(i, oc - 2)])
                def ymm(e, i=i, oc=oc):
                    bank = Yp[oc % 2]
                    for cc in range(NCH):
                        ins = e.matmul(bank[:, :],
                                       WO[:, cc * C + oc * 128: cc * C + (oc + 1) * 128],
                                       CTF[i % 2][:, cc * TS:(cc + 1) * TS],
                                       start=(cc == 0), stop=(cc == NCH - 1))
                    ins.then_inc(ste.h, 1)
                on("tensor", ymm)
                ste.n += 1
                W("scalar", ste, ste.n)
                if oc >= 2:
                    W("scalar", ste, sq_te[oc - 2])
                def yev(e, i=i, oc=oc):
                    bank = Yp[oc % 2]
                    e.activation(out=YSQ[oc % 2][:, :], in_=bank[:, :], func=AF.Square)
                    e.copy(out=YF[:, oc * TS:(oc + 1) * TS], in_=bank[:, :]).then_inc(ssc.h, 1)
                on("scalar", yev)
                ssc.n += 1
                evac5[(i, oc)] = ssc.n
                if oc >= 1:
                    W("tensor", ssc, evac5[(i, oc - 1)])
                    def sqm5(e, oc2=oc - 1):
                        e.matmul(Sqp5[:, :], ONEB[:, :], YSQ[oc2 % 2][:, :],
                                 start=(oc2 == 0), stop=False).then_inc(ste.h, 1)
                    on("tensor", sqm5)
                    ste.n += 1
                    sq_te[oc - 1] = ste.n
            W("tensor", ssc, evac5[(i, NCH - 1)])
            def sqm5f(e):
                e.matmul(Sqp5[:, :], ONEB[:, :], YSQ[(NCH - 1) % 2][:, :],
                         start=False, stop=True).then_inc(ste.h, 1)
            on("tensor", sqm5f)
            ste.n += 1
            sq_te[NCH - 1] = ste.n
            ymm_done[i] = ste.n
            W("vector", ste, ste.n)
            def v5rows(e, i=i):
                e.tensor_scalar(out=ROWA[:, :], in0=Mnp[:, :], scalar1=1.0 / C,
                                scalar2=None, op0=ALU.mult)
                e.tensor_scalar(out=ROWB[:, :], in0=Sqp5[:, :], scalar1=1.0 / C,
                                scalar2=None, op0=ALU.mult)
                e.tensor_tensor(out=ROWC[:, :], in0=ROWA[:, :], in1=ROWA[:, :], op=ALU.mult)
                e.tensor_tensor(out=ROWB[:, :], in0=ROWB[:, :], in1=ROWC[:, :],
                                op=ALU.subtract).then_inc(sve.h, 1)
            on("vector", v5rows)
            sve.n += 1
            W("scalar", sve, sve.n)
            def s5rows(e):
                e.activation(out=ROWB[:, :], in_=ROWB[:, :], func=AF.Ln, bias=EPSC[:, 0:1])
                e.activation(out=ROWC[:, :], in_=ROWB[:, :], func=AF.Exp,
                             scale=-0.5).then_inc(ssc.h, 1)
            on("scalar", s5rows)
            ssc.n += 1
            W("vector", ssc, ssc.n)
            def v5corr(e, i=i):
                e.scalar_tensor_tensor(out=CORR2[:, i * TS:(i + 1) * TS], in0=ROWA[:, :],
                                       scalar=-1.0, in1=ROWC[:, :],
                                       op0=ALU.mult, op1=ALU.mult).then_inc(sve.h, 1)
            on("vector", v5corr)
            sve.n += 1
            W("tensor", sve, sve.n)
            on("tensor", lambda e: e.matmul(Rb5p[:, :], ONERF[:, :], ROWC[:, :],
                                            start=True, stop=True).then_inc(ste.h, 1))
            ste.n += 1
            W("vector", ste, ste.n)
            W("vector", sb[0], xo_ld[i])
            def v5fin(e, i=i):
                e.tensor_copy(out=RB5[:, :], in_=Rb5p[:, :])
                for cc in range(NCH):
                    e.tensor_tensor(out=YF[:, cc * TS:(cc + 1) * TS],
                                    in0=YF[:, cc * TS:(cc + 1) * TS], in1=RB5[:, :], op=ALU.mult)
                for cc in range(NCH):
                    ins = e.tensor_tensor(out=YF[:, cc * TS:(cc + 1) * TS],
                                          in0=YF[:, cc * TS:(cc + 1) * TS],
                                          in1=XO[:, cc * TS:(cc + 1) * TS], op=ALU.add)
                ins.then_inc(sve.h, 1)
            on("vector", v5fin)
            sve.n += 1
            fin_ve[i] = sve.n
            W("sync", sve, sve.n)
            def st5(sy, i=i):
                sy.dma_start(out=x2buf.rearrange("d (k n) -> d k n", k=NCH)[:, :, i * TS:(i + 1) * TS],
                             in_=YF[:, :].rearrange("d (k n) -> d k n", k=NCH)).then_inc(sst.h, 16)
            on("sync", st5)
            sst.n += 16
            if i + 1 < NOT_:
                ld5_xo(i + 1)
                if i + 2 < NOT_:
                    ld5_ctf(i + 2)
                W("scalar", sve, fin_ve[i])
                W("scalar", sst, sst.n)
        W("sync", sst, sst.n)
    barrier()

    with ExitStack() as S6:
        e6 = S6.enter_context
        WTS = e6(nc.sbuf_tensor("WTS", [128, NCH * C], BF16))
        AST = e6(nc.sbuf_tensor("AST", [128, NCH * 128], BF16))
        BST = e6(nc.sbuf_tensor("BST", [128, C], BF16))
        GWF = e6(nc.sbuf_tensor("GWF", [128, NCH * 8], F32))
        SGR = e6(nc.sbuf_tensor("SGR", [1, 8], F32))
        NM3M = e6(nc.sbuf_tensor("NM3M", [1, 512], F32))
        R3C = e6(nc.sbuf_tensor("R3C", [128, 4], F32))
        BR = e6(nc.sbuf_tensor("BR", [1, C], BF16))
        X2T = e6(nc.sbuf_tensor("X2T", [128, NCH * TS], F32))
        XH2 = e6(nc.sbuf_tensor("XH2", [128, NCH * TS], BF16))
        Y4F = e6(nc.sbuf_tensor("Y4F", [128, NCH * TS], F32))
        Y4SQ = [e6(nc.sbuf_tensor(f"Y4SQ{i}", [128, 512], BF16)) for i in range(2)]
        CSW = e6(nc.sbuf_tensor("CSW", [128, NCH], BF16))
        CSWF = e6(nc.sbuf_tensor("CSWF", [128, NCH], F32))
        CSB = e6(nc.sbuf_tensor("CSB", [128, 1], BF16))
        CSBF = e6(nc.sbuf_tensor("CSBF", [128, 1], F32))
        SW = e6(nc.sbuf_tensor("SW", [1, C], BF16))
        SA = e6(nc.sbuf_tensor("SA", [1, 128], BF16))
        SBR = e6(nc.sbuf_tensor("SBR", [1, 1], F32))
        NM3 = e6(nc.sbuf_tensor("NM3", [1, 512], BF16))
        RB3 = e6(nc.sbuf_tensor("RB3", [128, 512], F32))
        RB6 = e6(nc.sbuf_tensor("RB6", [128, 512], F32))
        CBS = RB3
        LG = e6(nc.sbuf_tensor("LG", [128, 32], F32))
        M1 = e6(nc.sbuf_tensor("M1", [128, 4], F32))
        M2 = e6(nc.sbuf_tensor("M2", [128, 4], F32))
        IS1 = e6(nc.sbuf_tensor("IS1", [128, 32], F32))
        IS2 = e6(nc.sbuf_tensor("IS2", [128, 32], F32))
        G1 = e6(nc.sbuf_tensor("G1", [128, 4], F32))
        G2 = e6(nc.sbuf_tensor("G2", [128, 4], F32))
        GT = e6(nc.sbuf_tensor("GT", [128, 4], F32))
        WSEL = e6(nc.sbuf_tensor("WSEL", [128, 32], F32))
        W8 = e6(nc.sbuf_tensor("W8", [8, 512], F32))
        XAS = e6(nc.sbuf_tensor("XAS", [128, 512], BF16))
        XAB = e6(nc.sbuf_tensor("XAB", [128, 512], BF16))
        DG = e6(nc.sbuf_tensor("DG", [128, 64], F32)) if debug else None
        Y4p = [e6(nc.psum_tensor(f"Y4pE{i}", [128, 512], F32)) for i in range(2)]
        Sum6 = e6(nc.psum_tensor("Sum6E", [1, 512], F32))
        Sq6 = e6(nc.psum_tensor("Sq6E", [1, 512], F32))
        XAp = e6(nc.psum_tensor("XApE", [128, 512], F32))
        Gp = e6(nc.psum_tensor("GpE", [128, 32], F32))
        COL6 = e6(nc.psum_tensor("COL6E", [128, 8], F32))
        AUXp = e6(nc.psum_tensor("AUXpE", [128, 512], F32))

        evac6 = {}

        def wld6(g):
            g.dma_start(out=WTS[:, :].rearrange("d (k o) -> d k o", k=NCH),
                        in_=WT.rearrange("(k d) o -> d k o", d=128)).then_inc(sldg.h, 16)
            g.dma_start(out=AST[:, :].rearrange("d (k o) -> d k o", k=NCH),
                        in_=AstkT.rearrange("(k d) o -> d k o", d=128)).then_inc(sldg.h, 16)
            g.dma_start(out=BST[:, :], in_=BstkT).then_inc(sldg.h, 16)
            g.dma_start(out=BR[:, :], in_=brow).then_inc(sldg.h, 16)
        on("gpsimd", wld6)
        sldg.n += 4 * 16
        w6 = sldg.n
        on("sync", lambda sy: sy.dma_start(
            out=GWF[:, :].rearrange("d (k o) -> d k o", k=NCH),
            in_=gwT.rearrange("(k d) o -> d k o", d=128)).then_inc(slds.h, 16))
        slds.n += 16
        gwf_ld = slds.n

        W("vector", sldg, w6)
        def v6pre(e):
            for dc in range(NCH):
                e.reduce_sum(out=CSWF[:, dc:dc + 1], in_=WTS[:, dc * C:(dc + 1) * C],
                             axis=mybir.AxisListType.X)
            e.reduce_sum(out=CSBF[:, :], in_=BST[:, :], axis=mybir.AxisListType.X)
            e.reduce_sum(out=SBR[:, :], in_=BR[:, :],
                         axis=mybir.AxisListType.X).then_inc(sve.h, 1)
        on("vector", v6pre)
        sve.n += 1
        W("vector", sve, sve.n)
        def v6preb(e):
            e.tensor_copy(out=CSW[:, :], in_=CSWF[:, :])
            e.tensor_copy(out=CSB[:, :], in_=CSBF[:, :]).then_inc(sve.h, 1)
        on("vector", v6preb)
        sve.n += 1
        W("tensor", sldg, w6)
        for sl in range(4):
            def swmm(e, sl=sl):
                for dc in range(NCH):
                    ins = e.matmul(Sum6[:, :], ONEB[:, :],
                                   WTS[:, dc * C + sl * 512: dc * C + (sl + 1) * 512],
                                   start=(dc == 0), stop=(dc == NCH - 1))
                ins.then_inc(ste.h, 1)
            on("tensor", swmm)
            ste.n += 1
            W("vector", ste, ste.n)
            on("vector", lambda e, sl=sl: e.tensor_copy(
                out=SW[:, sl * 512:(sl + 1) * 512], in_=Sum6[:, :]).then_inc(sve.h, 1))
            sve.n += 1
            W("tensor", sve, sve.n)
        W("tensor", slds, gwf_ld)
        def sgmm2(e):
            for dc in range(NCH):
                e.matmul(COL6[0:1, 0:8], ONEF[:, :], GWF[:, dc * 8:(dc + 1) * 8],
                         start=(dc == 0), stop=(dc == NCH - 1))
            for dc in range(NCH):
                ins = e.matmul(Sq6[:, 0:128], ONEB[:, :], AST[:, dc * 128:(dc + 1) * 128],
                               start=(dc == 0), stop=(dc == NCH - 1))
            ins.then_inc(ste.h, 1)
        on("tensor", sgmm2)
        ste.n += 1
        W("vector", ste, ste.n)
        def v6pre2(e):
            e.tensor_copy(out=SGR[:, :], in_=COL6[0:1, 0:8])
            e.tensor_copy(out=SA[:, :], in_=Sq6[:, 0:128]).then_inc(sve.h, 1)
        on("vector", v6pre2)
        sve.n += 1
        pre_ve = sve.n
        W("tensor", sve, pre_ve)

        x2ld = [None] * NOT_
        fin6 = [None] * NOT_

        def ld6(i):
            if i >= 1:
                W("sync", sve, fin6[i - 1])
            def f(sy, i=i):
                sy.dma_start(out=X2T[:, :].rearrange("d (k n) -> d k n", k=NCH),
                             in_=x2buf.rearrange("d (k n) -> d k n", k=NCH)[:, :, i * TS:(i + 1) * TS]
                             ).then_inc(sb[0].h, 16)
            on("sync", f)
            sb[0].n += 16
            x2ld[i] = sb[0].n

        ld6(0)
        for i in range(NOT_):
            W("scalar", sb[0], x2ld[i])
            if i > 0:
                W("scalar", sve, fin6[i - 1])
                W("scalar", sst, sst.n)
            def s6a(e, i=i):
                e.activation(out=XH2[:, :], in_=X2T[:, :],
                             func=AF.Square).then_inc(ssc.h, 1)
            on("scalar", s6a)
            ssc.n += 1
            W("tensor", ssc, ssc.n)
            if i > 0:
                W("tensor", sve, fin6[i - 1])
            def t6stats(e, i=i):
                for dc in range(NCH):
                    e.matmul(Sum6[:, :], ONEF[:, :], X2T[:, dc * TS:(dc + 1) * TS],
                             start=(dc == 0), stop=(dc == NCH - 1))
                for dc in range(NCH):
                    ins = e.matmul(Sq6[:, :], ONEB[:, :], XH2[:, dc * TS:(dc + 1) * TS],
                                   start=(dc == 0), stop=(dc == NCH - 1))
                ins.then_inc(ste.h, 1)
            on("tensor", t6stats)
            ste.n += 1
            W("vector", ste, ste.n)
            def v6rows(e):
                e.tensor_scalar(out=ROWA[:, :], in0=Sum6[:, :], scalar1=1.0 / C,
                                scalar2=None, op0=ALU.mult)
                e.tensor_scalar(out=ROWB[:, :], in0=Sq6[:, :], scalar1=1.0 / C,
                                scalar2=None, op0=ALU.mult)
                e.tensor_scalar(out=NM3M[:, :], in0=ROWA[:, :], scalar1=-1.0,
                                scalar2=None, op0=ALU.mult)
                e.tensor_tensor(out=ROWC[:, :], in0=ROWA[:, :], in1=ROWA[:, :], op=ALU.mult)
                e.tensor_tensor(out=ROWB[:, :], in0=ROWB[:, :], in1=ROWC[:, :],
                                op=ALU.subtract).then_inc(sve.h, 1)
            on("vector", v6rows)
            sve.n += 1
            if debug and i == 0:
                W("sync", sve, sve.n)
                def drows3(sy):
                    sy.dma_start(out=d_rows[:, 0:TS], in_=ROWA[:, :]).then_inc(sst.h, 16)
                    sy.dma_start(out=d_rows[:, TS:2 * TS], in_=ROWB[:, :]).then_inc(sst.h, 16)
                on("sync", drows3)
                sst.n += 32
                W("scalar", sst, sst.n)
            W("scalar", sve, sve.n)
            def s6rows(e):
                e.activation(out=ROWB[:, :], in_=ROWB[:, :], func=AF.Ln, bias=EPSC[:, 0:1])
                e.activation(out=ROWC[:, :], in_=ROWB[:, :], func=AF.Exp,
                             scale=-0.5).then_inc(ssc.h, 1)
            on("scalar", s6rows)
            ssc.n += 1
            W("vector", ssc, ssc.n)
            def v6neg(e):
                e.scalar_tensor_tensor(out=NM3[:, :], in0=ROWA[:, :], scalar=-1.0,
                                       in1=ROWC[:, :], op0=ALU.mult,
                                       op1=ALU.mult).then_inc(sve.h, 1)
            on("vector", v6neg)
            sve.n += 1
            W("tensor", sve, sve.n)
            def t6cols(e):
                for tc in range(4):
                    e.matmul(COL6[:, tc:tc + 1],
                             ROWC[:, tc * 128:(tc + 1) * 128], ONE1F[:, :],
                             start=True, stop=True)
                e.matmul(AUXp[:, :], ONERF[:, :], ROWC[:, :],
                         start=True, stop=True).then_inc(ste.h, 1)
            on("tensor", t6cols)
            ste.n += 1
            W("vector", ste, ste.n)
            def v6xh(e, i=i):
                e.tensor_copy(out=R3C[:, :], in_=COL6[:, 0:4])
                e.tensor_copy(out=RB3[:, :], in_=AUXp[:, :])
                for dc in range(NCH):
                    ins = e.tensor_tensor(out=XH2[:, dc * TS:(dc + 1) * TS],
                                          in0=X2T[:, dc * TS:(dc + 1) * TS],
                                          in1=RB3[:, :], op=ALU.mult)
                ins.then_inc(sve.h, 1)
            on("vector", v6xh)
            sve.n += 1
            xh_ve = sve.n
            if debug and i == 0:
                W("sync", sve, xh_ve)
                on("sync", lambda sy: sy.dma_start(out=d_xh2, in_=XH2[:, :]).then_inc(sst.h, 16))
                sst.n += 16

            W("tensor", sve, xh_ve)
            def t6gate(e, i=i):
                for tc in range(4):
                    for dc in range(NCH):
                        e.matmul(Gp[:, tc * 8:(tc + 1) * 8],
                                 X2T[:, dc * TS + tc * 128: dc * TS + (tc + 1) * 128],
                                 GWF[:, dc * 8:(dc + 1) * 8],
                                 start=(dc == 0), stop=False)
                    e.matmul(Gp[:, tc * 8:(tc + 1) * 8],
                             NM3M[:, tc * 128:(tc + 1) * 128],
                             SGR[:, :], start=False, stop=True)
                for dc in range(NCH):
                    e.matmul(XAp[:, :], AST[:, dc * 128:(dc + 1) * 128],
                             XH2[:, dc * TS:(dc + 1) * TS],
                             start=(dc == 0), stop=False)
                e.matmul(XAp[:, :], SA[:, :], NM3[:, :],
                         start=False, stop=True).then_inc(ste.h, 1)
            on("tensor", t6gate)
            ste.n += 1
            gate_te = ste.n
            for oc0 in (0, 1):
                def y4a(e, i=i, oc=oc0):
                    bank = Y4p[oc % 2]
                    for dc in range(NCH):
                        ins = e.matmul(bank[:, :],
                                       WTS[:, dc * C + oc * 128: dc * C + (oc + 1) * 128],
                                       XH2[:, dc * TS:(dc + 1) * TS],
                                       start=(dc == 0), stop=False)
                    ins.then_inc(ste.h, 1)
                on("tensor", y4a)
                ste.n += 1
            W("scalar", ste, gate_te)
            on("scalar", lambda e: e.copy(out=XAS[:, :], in_=XAp[:, :]).then_inc(ssc.h, 1))
            ssc.n += 1
            xas_sc = ssc.n
            def step(fn):
                on("vector", fn)
                sve.n += 1
                W("vector", sve, sve.n)

            W("vector", ste, gate_te)
            def g_s0(e, i=i):
                if debug and i == 0:
                    e.tensor_copy(out=DG[:, 44:48], in_=R3C[:, :])
                for tc in range(4):
                    ins = e.tensor_scalar(out=LG[:, tc * 8:(tc + 1) * 8],
                                          in0=Gp[:, tc * 8:(tc + 1) * 8],
                                          scalar1=R3C[:, tc:tc + 1], scalar2=None,
                                          op0=ALU.mult)
                ins.then_inc(sve.h, 1)
            step(g_s0)
            def g_s1(e, i=i):
                if debug and i == 0:
                    e.tensor_copy(out=DG[:, 0:8], in_=Gp[:, 0:8])
                for tc in range(4):
                    ins = e.tensor_tensor(out=LG[:, tc * 8:(tc + 1) * 8],
                                          in0=LG[:, tc * 8:(tc + 1) * 8],
                                          in1=GBB[:, :], op=ALU.add)
                ins.then_inc(sve.h, 1)
            step(g_s1)
            def g_s2(e, i=i):
                if debug and i == 0:
                    e.tensor_copy(out=DG[:, 8:16], in_=LG[:, 0:8])
                for tc in range(4):
                    ins = e.reduce_max(out=M1[:, tc:tc + 1], in_=LG[:, tc * 8:(tc + 1) * 8],
                                       axis=mybir.AxisListType.X)
                ins.then_inc(sve.h, 1)
            step(g_s2)
            def g_s3(e, i=i):
                if debug and i == 0:
                    e.tensor_copy(out=DG[:, 16:17], in_=M1[:, 0:1])
                for tc in range(4):
                    ins = e.tensor_tensor(out=IS1[:, tc * 8:(tc + 1) * 8],
                                          in0=LG[:, tc * 8:(tc + 1) * 8],
                                          in1=M1[:, tc:tc + 1].broadcast_to([128, 8]),
                                          op=ALU.is_ge)
                ins.then_inc(sve.h, 1)
            step(g_s3)
            def g_s4(e, i=i):
                if debug and i == 0:
                    e.tensor_copy(out=DG[:, 18:26], in_=IS1[:, 0:8])
                for tc in range(4):
                    ins = e.scalar_tensor_tensor(out=LG[:, tc * 8:(tc + 1) * 8],
                                                 in0=IS1[:, tc * 8:(tc + 1) * 8], scalar=NEG,
                                                 in1=LG[:, tc * 8:(tc + 1) * 8],
                                                 op0=ALU.mult, op1=ALU.add)
                ins.then_inc(sve.h, 1)
            step(g_s4)
            def g_s5(e, i=i):
                if debug and i == 0:
                    e.tensor_copy(out=DG[:, 26:34], in_=LG[:, 0:8])
                for tc in range(4):
                    ins = e.reduce_max(out=M2[:, tc:tc + 1], in_=LG[:, tc * 8:(tc + 1) * 8],
                                       axis=mybir.AxisListType.X)
                ins.then_inc(sve.h, 1)
            step(g_s5)
            def g_s6(e, i=i):
                if debug and i == 0:
                    e.tensor_copy(out=DG[:, 17:18], in_=M2[:, 0:1])
                e.tensor_tensor(out=G1[:, :], in0=M1[:, :], in1=M2[:, :],
                                op=ALU.subtract).then_inc(sve.h, 1)
            step(g_s6)
            W("scalar", sve, sve.n)
            on("scalar", lambda e: e.activation(out=GT[:, :], in_=G1[:, :], func=AF.Tanh,
                                                scale=0.5).then_inc(ssc.h, 1))
            ssc.n += 1
            W("vector", ssc, ssc.n)
            step(lambda e: e.tensor_scalar(out=G1[:, :], in0=GT[:, :], scalar1=0.5,
                                           scalar2=0.5, op0=ALU.mult,
                                           op1=ALU.add).then_inc(sve.h, 1))
            step(lambda e: e.tensor_scalar(out=G2[:, :], in0=GT[:, :], scalar1=-0.5,
                                           scalar2=0.5, op0=ALU.mult,
                                           op1=ALU.add).then_inc(sve.h, 1))
            def g_s9(e):
                for tc in range(4):
                    ins = e.tensor_tensor(out=IS2[:, tc * 8:(tc + 1) * 8],
                                          in0=LG[:, tc * 8:(tc + 1) * 8],
                                          in1=M2[:, tc:tc + 1].broadcast_to([128, 8]),
                                          op=ALU.is_ge)
                ins.then_inc(sve.h, 1)
            step(g_s9)
            def g_s10(e):
                for tc in range(4):
                    ins = e.tensor_tensor(out=IS1[:, tc * 8:(tc + 1) * 8],
                                          in0=IS1[:, tc * 8:(tc + 1) * 8],
                                          in1=G1[:, tc:tc + 1].broadcast_to([128, 8]),
                                          op=ALU.mult)
                ins.then_inc(sve.h, 1)
            step(g_s10)
            def g_s11(e):
                for tc in range(4):
                    ins = e.tensor_tensor(out=IS2[:, tc * 8:(tc + 1) * 8],
                                          in0=IS2[:, tc * 8:(tc + 1) * 8],
                                          in1=G2[:, tc:tc + 1].broadcast_to([128, 8]),
                                          op=ALU.mult)
                ins.then_inc(sve.h, 1)
            step(g_s11)
            def g_s12(e, i=i):
                for tc in range(4):
                    ins = e.tensor_tensor(out=WSEL[:, tc * 8:(tc + 1) * 8],
                                          in0=IS1[:, tc * 8:(tc + 1) * 8],
                                          in1=IS2[:, tc * 8:(tc + 1) * 8], op=ALU.add)
                if debug and i == 0:
                    e.tensor_copy(out=DG[:, 34:42], in_=WSEL[:, 0:8])
                    e.tensor_copy(out=DG[:, 42:43], in_=G1[:, 0:1])
                    e.tensor_copy(out=DG[:, 43:44], in_=G2[:, 0:1])
                    e.tensor_copy(out=DG[:, 48:56], in_=IS1[:, 0:8])
                    e.tensor_copy(out=DG[:, 56:64], in_=IS2[:, 0:8])
                ins.then_inc(sve.h, 1)
            step(g_s12)
            W("tensor", sve, sve.n)
            def g_tr(e):
                for tc in range(4):
                    ins = e.transpose(AUXp[0:8, tc * 128:(tc + 1) * 128],
                                      WSEL[:, tc * 8:(tc + 1) * 8], ID128[:, :])
                ins.then_inc(ste.h, 1)
            on("tensor", g_tr)
            ste.n += 1
            W("vector", ste, ste.n)
            def g_w8(e):
                for tc in range(4):
                    ins = e.tensor_copy(out=W8[:, tc * 128:(tc + 1) * 128],
                                        in_=AUXp[0:8, tc * 128:(tc + 1) * 128])
                ins.then_inc(sve.h, 1)
            on("vector", g_w8)
            sve.n += 1
            W("vector", sve, sve.n)
            w8_ve = sve.n
            if debug and i == 0:
                W("sync", sve, w8_ve)
                on("sync", lambda sy: sy.dma_start(out=d_w8, in_=W8[:, :]).then_inc(sst.h, 16))
                on("sync", lambda sy: sy.dma_start(out=d_gate, in_=DG[:, :]).then_inc(sst.h, 16))
                sst.n += 32
            W("tensor", sve, w8_ve)
            on("tensor", lambda e: e.matmul(AUXp[:, :], E8F[:, :], W8[:, :],
                                            start=True, stop=True).then_inc(ste.h, 1))
            ste.n += 1
            W("vector", ste, ste.n)
            W("vector", ssc, xas_sc)
            def v6xab(e):
                e.tensor_tensor(out=XAB[:, :], in0=XAS[:, :], in1=AUXp[:, :],
                                op=ALU.mult).then_inc(sve.h, 1)
            on("vector", v6xab)
            sve.n += 1
            xab_ve = sve.n
            if debug and i == 0:
                W("sync", sve, xab_ve)
                on("sync", lambda sy: sy.dma_start(out=d_xab, in_=XAB[:, :]).then_inc(sst.h, 16))
                sst.n += 16
            W("tensor", sve, xab_ve)

            sq_te6 = {}
            for oc in range(NCH):
                if (i, oc - 2) in evac6:
                    W("tensor", ssc, evac6[(i, oc - 2)])
                def y4mm(e, i=i, oc=oc):
                    bank = Y4p[oc % 2]
                    if oc >= 2:
                        for dc in range(NCH):
                            e.matmul(bank[:, :],
                                     WTS[:, dc * C + oc * 128: dc * C + (oc + 1) * 128],
                                     XH2[:, dc * TS:(dc + 1) * TS],
                                     start=(dc == 0), stop=False)
                    e.matmul(bank[:, :], SW[:, oc * 128:(oc + 1) * 128], NM3[:, :],
                             start=False, stop=False)
                    e.matmul(bank[:, :], BR[:, oc * 128:(oc + 1) * 128], OR512B[:, :],
                             start=False, stop=False)
                    e.matmul(bank[:, :], BST[:, oc * 128:(oc + 1) * 128], XAB[:, :],
                             start=False, stop=True).then_inc(ste.h, 1)
                on("tensor", y4mm)
                ste.n += 1
                W("scalar", ste, ste.n)
                if oc >= 2:
                    W("scalar", ste, sq_te6[oc - 2])
                def y4ev(e, i=i, oc=oc):
                    bank = Y4p[oc % 2]
                    e.activation(out=Y4SQ[oc % 2][:, :], in_=bank[:, :], func=AF.Square)
                    e.copy(out=Y4F[:, oc * TS:(oc + 1) * TS], in_=bank[:, :]).then_inc(ssc.h, 1)
                on("scalar", y4ev)
                ssc.n += 1
                evac6[(i, oc)] = ssc.n
                if oc >= 1:
                    W("tensor", ssc, evac6[(i, oc - 1)])
                    def sq6m(e, oc2=oc - 1):
                        e.matmul(Sq6[:, :], ONEB[:, :], Y4SQ[oc2 % 2][:, :],
                                 start=(oc2 == 0), stop=False).then_inc(ste.h, 1)
                    on("tensor", sq6m)
                    ste.n += 1
                    sq_te6[oc - 1] = ste.n
            W("tensor", ssc, evac6[(i, NCH - 1)])
            def sq6mf(e):
                e.matmul(Sq6[:, :], ONEB[:, :], Y4SQ[(NCH - 1) % 2][:, :],
                         start=False, stop=True).then_inc(ste.h, 1)
            on("tensor", sq6mf)
            ste.n += 1
            sq_te6[NCH - 1] = ste.n
            def m4mm(e, i=i):
                for dc in range(NCH):
                    e.matmul(Sum6[:, :], CSW[:, dc:dc + 1], XH2[:, dc * TS:(dc + 1) * TS],
                             start=(dc == 0), stop=False)
                e.matmul(Sum6[:, :], CSB[:, :], XAB[:, :],
                         start=False, stop=True).then_inc(ste.h, 1)
            on("tensor", m4mm)
            ste.n += 1
            W("vector", ste, ste.n)
            def v6r4(e):
                e.tensor_scalar(out=ROWA[:, :], in0=Sum6[:, :], scalar1=SBR[:, 0:1],
                                scalar2=1.0 / C, op0=ALU.add, op1=ALU.mult)
                e.tensor_scalar(out=ROWB[:, :], in0=Sq6[:, :], scalar1=1.0 / C,
                                scalar2=None, op0=ALU.mult)
                e.tensor_tensor(out=ROWC[:, :], in0=ROWA[:, :], in1=ROWA[:, :], op=ALU.mult)
                e.tensor_tensor(out=ROWB[:, :], in0=ROWB[:, :], in1=ROWC[:, :],
                                op=ALU.subtract).then_inc(sve.h, 1)
            on("vector", v6r4)
            sve.n += 1
            if debug and i == 0:
                W("sync", sve, sve.n)
                W("sync", ssc, evac6[(i, NCH - 1)])
                def dy4(sy):
                    sy.dma_start(out=d_y4, in_=Y4F[:, :]).then_inc(sst.h, 16)
                    sy.dma_start(out=d_rows[:, 2 * TS:3 * TS], in_=ROWA[:, :]).then_inc(sst.h, 16)
                    sy.dma_start(out=d_rows[:, 3 * TS:4 * TS], in_=ROWB[:, :]).then_inc(sst.h, 16)
                on("sync", dy4)
                sst.n += 48
                W("scalar", sst, sst.n)
                W("vector", sst, sst.n)
            W("scalar", sve, sve.n)
            def s6r4(e):
                e.activation(out=ROWB[:, :], in_=ROWB[:, :], func=AF.Ln, bias=EPSC[:, 0:1])
                e.activation(out=ROWC[:, :], in_=ROWB[:, :], func=AF.Exp,
                             scale=-0.5).then_inc(ssc.h, 1)
            on("scalar", s6r4)
            ssc.n += 1
            W("vector", ssc, ssc.n)
            def v6cf(e, i=i):
                e.scalar_tensor_tensor(out=ROWA[:, :], in0=ROWA[:, :], scalar=-1.0,
                                       in1=ROWC[:, :], op0=ALU.mult, op1=ALU.mult)
                e.tensor_tensor(out=ROWA[:, :], in0=ROWA[:, :],
                                in1=CORR2[:, i * TS:(i + 1) * TS], op=ALU.add).then_inc(sve.h, 1)
            on("vector", v6cf)
            sve.n += 1
            W("tensor", sve, sve.n)
            def t6bc2(e):
                e.matmul(AUXp[:, :], ONERF[:, :], ROWC[:, :], start=True, stop=True)
                e.matmul(XAp[:, :], ONERF[:, :], ROWA[:, :],
                         start=True, stop=True).then_inc(ste.h, 1)
            on("tensor", t6bc2)
            ste.n += 1
            W("vector", ste, ste.n)
            def v6fin(e, i=i):
                e.tensor_copy(out=RB6[:, :], in_=AUXp[:, :])
                e.tensor_copy(out=CBS[:, :], in_=XAp[:, :])
                for dc in range(NCH):
                    e.tensor_tensor(out=Y4F[:, dc * TS:(dc + 1) * TS],
                                    in0=Y4F[:, dc * TS:(dc + 1) * TS], in1=RB6[:, :], op=ALU.mult)
                for dc in range(NCH):
                    e.tensor_tensor(out=Y4F[:, dc * TS:(dc + 1) * TS],
                                    in0=Y4F[:, dc * TS:(dc + 1) * TS],
                                    in1=X2T[:, dc * TS:(dc + 1) * TS], op=ALU.add)
                for dc in range(NCH):
                    ins = e.tensor_tensor(out=Y4F[:, dc * TS:(dc + 1) * TS],
                                          in0=Y4F[:, dc * TS:(dc + 1) * TS],
                                          in1=CBS[:, :], op=ALU.add)
                ins.then_inc(sve.h, 1)
            on("vector", v6fin)
            sve.n += 1
            fin6[i] = sve.n
            W("sync", sve, sve.n)
            def st6(sy, i=i):
                sy.dma_start(out=outT.rearrange("(k d) n -> d k n", d=128)[:, :, i * TS:(i + 1) * TS],
                             in_=Y4F[:, :].rearrange("d (k n) -> d k n", k=NCH)).then_inc(sst.h, 16)
            on("sync", st6)
            sst.n += 16
            if i + 1 < NOT_:
                ld6(i + 1)
        W("sync", sst, sst.n)

    with nc.Block() as block:
        @block.tensor
        def _(t):
            for f in ops["tensor"]:
                f(t)

        @block.scalar
        def _(s):
            for f in ops["scalar"]:
                f(s)

        @block.vector
        def _(v):
            for f in ops["vector"]:
                f(v)

        @block.gpsimd
        def _(g):
            for f in ops["gpsimd"]:
                f(g)

        @block.sync
        def _(sy):
            for f in ops["sync"]:
                f(sy)

    ctx_stack.close()
    return nc


def _host_prep(inputs):
    x = np.asarray(inputs["L0_attn_x"], dtype=np.float32)
    wq = np.asarray(inputs["L0_attn_w_q"], dtype=np.float32)
    wk = np.asarray(inputs["L0_attn_w_k"], dtype=np.float32)
    wv = np.asarray(inputs["L0_attn_w_v"], dtype=np.float32)
    wo = np.asarray(inputs["L0_attn_w_o"], dtype=np.float32)
    Wf = np.asarray(inputs["L0_ffn_W"], dtype=np.float32)
    bf_ = np.asarray(inputs["L0_ffn_b"], dtype=np.float32)
    lA = np.asarray(inputs["L0_ffn_lora_A"], dtype=np.float32)
    lB = np.asarray(inputs["L0_ffn_lora_B"], dtype=np.float32)
    gw = np.asarray(inputs["L0_ffn_gate_w"], dtype=np.float32)
    gb = np.asarray(inputs["L0_ffn_gate_b"], dtype=np.float32)

    woT = np.ascontiguousarray(wo.T)
    WTt = np.ascontiguousarray(Wf.T)
    AstkT = np.ascontiguousarray(lA.reshape(128, C).T)
    BstkT = np.ascontiguousarray(lB.transpose(0, 2, 1).reshape(128, C))
    gwT = np.ascontiguousarray(gw.T)

    t_ = np.arange(T, dtype=np.float32)
    inv = (1.0 / (10000.0 ** (np.arange(0, D, 2, dtype=np.float32) / D))).astype(np.float32)
    freqs = np.outer(t_, inv).astype(np.float32)
    emb = np.concatenate([freqs, freqs], axis=1)
    cosT = np.ascontiguousarray(np.cos(emb).T.astype(ml_dtypes.bfloat16))
    sinv = np.sin(emb).astype(np.float32)
    sgn = np.ones((128,), np.float32)
    sgn[:64] = -1.0
    sinT = np.ascontiguousarray((sinv * sgn[None, :]).T.astype(ml_dtypes.bfloat16))

    c_ones = np.ones((128, 1), np.float32)
    c_masks = np.zeros((128, 3), np.float32)
    c_masks[64:, 1] = NEG
    c_masks[:64, 2] = NEG
    c_E8 = np.zeros((8, 128), np.float32)
    for e_ in range(8):
        c_E8[e_, e_ * 16:(e_ + 1) * 16] = 1.0
    c_id = np.eye(128, dtype=np.float32)
    c_ones512 = np.ones((1, 512), np.float32)

    in_maps = []
    for core in range(8):
        b, g = core // 2, core % 2
        xTb = np.ascontiguousarray(x[b].T)
        m = dict(
            xT=xTb,
            xoT=np.ascontiguousarray(xTb[:, g * OWN:(g + 1) * OWN]),
            wqT=np.ascontiguousarray(wq[g * CG:(g + 1) * CG, :].T),
            wkT=np.ascontiguousarray(wk[g * CG:(g + 1) * CG, :].T),
            wvT=np.ascontiguousarray(wv[g * CG:(g + 1) * CG, :].T),
            woT=woT, WT=WTt, brow=bf_[None, :],
            AstkT=AstkT, BstkT=BstkT, gwT=gwT, gbrow=gb[None, :],
            cosT=cosT, sinT=sinT,
            c_ones=c_ones, c_masks=c_masks, c_E8=c_E8, c_id=c_id,
            c_ones512=c_ones512,
            c_eps=np.full((1, 1), EPS, np.float32),
        )
        in_maps.append(m)
    return in_maps


def kernel(**inputs):
    in_maps = _host_prep(inputs)
    nc = build_nc(debug=False)
    res = run_bass_kernel_spmd(nc, in_maps, core_ids=list(range(8)))
    out = np.empty((B, T, C), dtype=np.float32)
    for core in range(8):
        b, g = core // 2, core % 2
        out[b, g * OWN:(g + 1) * OWN, :] = np.asarray(res.results[core]["outT"]).T
    return out
